# version 21
# baseline (speedup 1.0000x reference)
import numpy as np
import concourse.bass as bass
import concourse.mybir as mybir
from contextlib import ExitStack

F32 = mybir.dt.float32
BF16 = mybir.dt.bfloat16
I32 = mybir.dt.int32
AF = mybir.ActivationFunctionType
OP = mybir.AluOpType
AX = mybir.AxisListType

LN_EPS = 1e-5
ISQ128 = 0.08838834764831845
ISQ64 = 0.125
NEG_BIG = -1e30

_uid = [0]


def uname(s):
    _uid[0] += 1
    return f"{s}_u{_uid[0]}"




def ln_stats_comb(q, x_ap, st6, mv, rstd, eps, dep_in, war=None):
    q.op("v", "bn_stats", st6[:, 0:6], x_ap[:, 0:512], deps=[dep_in])
    q.op("v", "bn_stats", st6[:, 6:12], x_ap[:, 512:1024])
    ag = q.op("v", "bn_aggr", mv[:, 0:2], st6[:, 0:12], deps=[war])
    q.op("s", "activation", mv[:, 2:3], mv[:, 1:2], AF.Ln, bias=eps,
         deps=[ag])
    r1 = q.op("s", "activation", rstd[:, 0:1], mv[:, 2:3], AF.Exp, scale=-0.5)
    q.op("v", "tensor_mul", mv[:, 3:4], rstd[:, 0:1], rstd[:, 0:1], deps=[r1])
    q.op("v", "tensor_mul", mv[:, 3:4], mv[:, 3:4], mv[:, 1:2])
    q.op("s", "activation", mv[:, 3:4], mv[:, 3:4], AF.Ln, bias=eps,
         deps=[(q.ch["v"][0], q.ch["v"][1])])
    q.op("s", "activation", mv[:, 3:4], mv[:, 3:4], AF.Exp, scale=-0.5)
    rc = q.op("v", "tensor_mul", rstd[:, 1:2], rstd[:, 0:1], mv[:, 3:4],
              deps=[(q.ch["s"][0], q.ch["s"][1])])
    return r1, rc


class Par:

    def __init__(self, nc):
        self.nc = nc
        self.cm = []
        def sem(name):
            cm = nc.semaphore(name)
            self.cm.append(cm)
            return [cm.__enter__(), 0]
        self.ch = {k: sem("ch_" + k) for k in "tvsg"}
        self.dpool = {"y": [sem(f"dqy{i}") for i in range(10)],
                      "g": [sem(f"dqg{i}") for i in range(6)],
                      "s": [sem(f"dqs{i}") for i in range(6)],
                      "stg": [sem(f"dqstg{i}") for i in range(2)]}
        self.didx = {"y": 0, "g": 0, "s": 0, "stg": 0}
        self.eng = dict(t=nc.tensor, v=nc.vector, s=nc.scalar,
                        g=nc.gpsimd, y=nc.sync)

    def _waits(self, e, deps):
        best = {}
        for d in deps:
            if d is None:
                continue
            if isinstance(d, list):
                for dd in d:
                    if dd is not None:
                        k = id(dd[0])
                        if k not in best or dd[1] > best[k][1]:
                            best[k] = dd
                continue
            k = id(d[0])
            if k not in best or d[1] > best[k][1]:
                best[k] = d
        for d in best.values():
            e.wait_ge(d[0], d[1])

    def op(self, which, name, *a, deps=(), **k):
        e = self.eng[which]
        c = self.ch[which]
        if which != "t":
            e.wait_ge(c[0], c[1])
        self._waits(e, deps)
        ins = getattr(e, name)(*a, **k)
        ins.then_inc(c[0], 1)
        c[1] += 1
        return (c[0], c[1])

    def mm(self, out, lhsT, rhs, start, stop, deps=()):
        e = self.eng["t"]
        self._waits(e, deps)
        ins = e.matmul(out, lhsT, rhs, start=start, stop=stop)
        c = self.ch["t"]
        ins.then_inc(c[0], 1)
        c[1] += 1
        return (c[0], c[1])

    def mm_group(self, mms, deps=()):
        ev = None
        for i, (out, lhsT, rhs) in enumerate(mms):
            ev = self.mm(out, lhsT, rhs, start=(i == 0),
                         stop=(i == len(mms) - 1),
                         deps=deps if i == 0 else ())
        return ev

    def tr(self, out, in_, ident, deps=()):
        e = self.eng["t"]
        self._waits(e, deps)
        ins = e.transpose(out, in_, ident)
        c = self.ch["t"]
        ins.then_inc(c[0], 1)
        c[1] += 1
        return (c[0], c[1])

    def dma(self, which, out, in_, deps=(), pool=None, **k):
        e = self.eng[which]
        qn = pool or ("g" if which == "g" else
                      ("s" if which == "s" else "y"))
        pool = self.dpool[qn]
        d = pool[self.didx[qn] % len(pool)]
        self.didx[qn] += 1
        if d[1] > 0:
            e.wait_ge(d[0], d[1])
        self._waits(e, deps)
        ins = e.dma_start(out=out, in_=in_, **k)
        ins.then_inc(d[0], 16)
        d[1] += 16
        return (d[0], d[1])

    def dma_fence(self):
        e = self.eng["y"]
        for qn, pool in self.dpool.items():
            for d in pool:
                if d[1] > 0:
                    e.wait_ge(d[0], d[1])


def emit_constants(q, ident_bf, ident_f32, ones_bf, ones_f32, cpool):
    e1 = q.op("g", "memset", ones_f32[:, :], 1.0)
    q.op("g", "memset", ones_bf[:, :], 1.0)
    q.op("g", "memset", cpool[:, 0:1], LN_EPS)
    q.op("g", "affine_select", ident_f32[:, :], ones_f32[:, :],
         pattern=[[-1, 128]], compare_op=OP.is_equal, fill=0.0,
         base=0, channel_multiplier=1, deps=[e1])
    q.op("v", "tensor_copy", ident_bf[:, :], ident_f32[:, :],
         deps=[(q.ch["g"][0], q.ch["g"][1])])


def ln_tile(q, x_ap, st6, mv, rstd, z_f32, eps, dep_in, affine=None,
            out_bf=None, war=None):
    q.op("v", "bn_stats", st6[:, 0:6], x_ap[:, 0:512], deps=[dep_in])
    q.op("v", "bn_stats", st6[:, 6:12], x_ap[:, 512:1024])
    ag = q.op("v", "bn_aggr", mv[:, 0:2], st6[:, 0:12], deps=[war])
    q.op("s", "activation", mv[:, 1:2], mv[:, 1:2], AF.Ln, bias=eps,
         deps=[ag])
    r1 = q.op("s", "activation", rstd[:, 0:1], mv[:, 1:2], AF.Exp, scale=-0.5)
    sev = r1
    ev = q.op("v", "tensor_scalar", z_f32, x_ap, mv[:, 0:1], rstd[:, 0:1],
              OP.subtract, OP.mult, deps=[r1])
    if affine is not None:
        g_rep, b_rep = affine
        q.op("v", "bn_stats", st6[:, 0:6], z_f32[:, 0:512])
        q.op("v", "bn_stats", st6[:, 6:12], z_f32[:, 512:1024])
        ag = q.op("v", "bn_aggr", mv[:, 2:4], st6[:, 0:12])
        q.op("s", "activation", mv[:, 3:4], mv[:, 3:4], AF.Ln, bias=eps,
             deps=[ag])
        r2 = q.op("s", "activation", rstd[:, 1:2], mv[:, 3:4], AF.Exp,
                  scale=-0.5)
        sev = r2
        q.op("v", "tensor_scalar", z_f32, z_f32, mv[:, 2:3], rstd[:, 1:2],
             OP.subtract, OP.mult, deps=[r2])
        q.op("v", "tensor_mul", z_f32, z_f32, g_rep)
        ev = q.op("v", "tensor_add", out_bf if out_bf is not None else z_f32,
                  z_f32, b_rep)
    elif out_bf is not None:
        ev = q.op("v", "tensor_copy", out_bf, z_f32)
    return ev, sev


def build_l0a(nc, q, x, Wq, bq, Wk, bk, Wv, bv, gamma, beta, attn_out, consts):
    ident_bf, ident_f32, ones_bf, ones_f32, cpool = consts
    eps = cpool[:, 0:1]
    NB, T, NT, H = 2, 2048, 16, 8

    x3 = x.rearrange("(i p) d -> i p d", p=128)
    wq3 = Wq.rearrange("(c p) d -> c p d", p=128)
    wk3 = Wk.rearrange("(c p) d -> c p d", p=128)
    wv3 = Wv.rearrange("(c p) d -> c p d", p=128)
    attn_flat = attn_out.rearrange("a d -> (a d)")

    with ExitStack() as st0:
        E = st0.enter_context
        wq_bf = E(nc.sbuf_tensor(uname("wq_bf"), [128, 8, 1024], BF16))
        wk_bf = E(nc.sbuf_tensor(uname("wk_bf"), [128, 8, 1024], BF16))
        wvp_bf = E(nc.sbuf_tensor(uname("wvp_bf"), [128, 8, 1024], BF16))
        bq_sb = E(nc.sbuf_tensor(uname("bq_sb"), [128, 8], F32))
        bk_sb = E(nc.sbuf_tensor(uname("bk_sb"), [128, 8], F32))
        g_rep = E(nc.sbuf_tensor(uname("g_rep"), [128, 1024], F32))
        b_rep = E(nc.sbuf_tensor(uname("b_rep"), [128, 1024], F32))
        bvp_bf = E(nc.sbuf_tensor(uname("bvp_bf"), [1, 1024], BF16))

        with ExitStack() as st1:
            E1 = st1.enter_context
            E1(nc.Block())
            wqe = q.dma("g", wq_bf[:, :, :], wq3.rearrange("c p d -> p c d"))
            wke = q.dma("g", wk_bf[:, :, :], wk3.rearrange("c p d -> p c d"))
            bqke = []
            for h in range(H):
                bqke.append(q.dma("y", bq_sb[:, h:h + 1],
                            bq[h * 128:(h + 1) * 128].unsqueeze(1)))
                bqke.append(q.dma("y", bk_sb[:, h:h + 1],
                            bk[h * 128:(h + 1) * 128].unsqueeze(1)))
            grpe = q.dma("y", g_rep[:, :],
                         gamma.unsqueeze(0).broadcast_to((128, 1024)))
            brpe = q.dma("y", b_rep[:, :],
                         beta.unsqueeze(0).broadcast_to((128, 1024)))

        pool_done = [None]

        def emit_wv_pooling(wv_buf, pool_f32, free_ev):
            dmae = {}
            addse = {}
            for cq in range(64):
                c, qq = cq // 8, cq % 8
                dmae[cq] = q.dma("s", wv_buf[cq % 2][:, :],
                                 wv3[c][:, qq * 1024:(qq + 1) * 1024],
                                 deps=[addse.get(cq - 2),
                                       free_ev if cq < 2 else None])
                wvg = wv_buf[cq % 2][:, :].rearrange("p (g e) -> p e g", e=8)
                dst = pool_f32[:, qq * 128:(qq + 1) * 128]
                q.op("v", "tensor_add", dst, wvg[:, 0, :], wvg[:, 1, :],
                     deps=[dmae[cq]])
                for j in range(2, 8):
                    addse[cq] = q.op("v", "tensor_add", dst, dst,
                                     wvg[:, j, :])
                if qq == 7:
                    pool_done[0] = q.op("v", "tensor_copy", wvp_bf[:, c, :],
                                        pool_f32[:, :])
            for p in range(8):
                cq = 64 + p
                dmae[cq] = q.dma("s", wv_buf[cq % 2][0:1, :],
                                 bv[p * 1024:(p + 1) * 1024].unsqueeze(0),
                                 deps=[addse.get(cq - 2)])
                bvg = wv_buf[cq % 2][0:1, :].rearrange("p (g e) -> p e g", e=8)
                dst = pool_f32[0:1, p * 128:(p + 1) * 128]
                q.op("v", "tensor_add", dst, bvg[:, 0, :], bvg[:, 1, :],
                     deps=[dmae[cq]])
                for j in range(2, 8):
                    addse[cq] = q.op("v", "tensor_add", dst, dst,
                                     bvg[:, j, :])
            pool_done[0] = q.op("v", "tensor_copy", bvp_bf[:, :],
                                pool_f32[0:1, 0:1024])

        for b in range(NB):
            with ExitStack() as st2:
                E2 = st2.enter_context
                qT = E2(nc.sbuf_tensor(uname("qT"), [128, 8, T], BF16))
                kT = E2(nc.sbuf_tensor(uname("kT"), [128, 8, T], BF16))
                vp = E2(nc.sbuf_tensor(uname("vp"), [128, 16, 1024], BF16))
                stAB = ExitStack()
                xnT = stAB.enter_context(
                    nc.sbuf_tensor(uname("xnT"), [128, 8, T], BF16))

                with ExitStack() as st3:
                    E3 = st3.enter_context
                    xt = [E3(nc.sbuf_tensor(uname("xt"), [128, 1024], F32))
                          for _ in range(2)]
                    zt = E3(nc.sbuf_tensor(uname("zt"), [128, 1024], F32))
                    xnbf = [E3(nc.sbuf_tensor(uname("xnbf"), [128, 1024], BF16))
                            for _ in range(2)]
                    st6 = [E3(nc.sbuf_tensor(uname("st6"), [128, 12], F32))
                           for _ in range(2)]
                    mv = [E3(nc.sbuf_tensor(uname("mv"), [128, 4], F32))
                          for _ in range(2)]
                    rstd = [E3(nc.sbuf_tensor(uname("rstd"), [128, 2], F32))
                            for _ in range(2)]
                    ptr = [E3(nc.psum_tensor(uname("ptr"), [128, 1024], BF16))
                           for _ in range(2)]
                    E3(nc.Block())
                    lne = {}
                    tre = {}
                    eve = {}
                    rce = {}
                    for i in range(NT + 1):
                        if i < NT:
                            de = q.dma("y", xt[i % 2][:, :], x3[b * NT + i],
                                       deps=[lne.get(i - 2)])
                            _, rce[i] = ln_stats_comb(
                                q, xt[i % 2][:, :], st6[i % 2], mv[i % 2],
                                rstd[i % 2], eps, de)
                        if i > 0:
                            j = i - 1
                            q.op("v", "tensor_scalar", zt[:, :],
                                 xt[j % 2][:, :], mv[j % 2][:, 0:1],
                                 rstd[j % 2][:, 1:2], OP.subtract, OP.mult,
                                 deps=[rce[j]])
                            q.op("v", "tensor_mul", zt[:, :], zt[:, :],
                                 g_rep[:, :], deps=[grpe] if j == 0 else ())
                            lne[j] = q.op("v", "tensor_add", xnbf[j % 2][:, :],
                                          zt[:, :], b_rep[:, :],
                                          deps=[brpe] if j == 0 else ())
                            for c in range(8):
                                tre[(j, c)] = q.tr(
                                    ptr[j % 2][:, c * 128:(c + 1) * 128],
                                    xnbf[j % 2][:, c * 128:(c + 1) * 128],
                                    ident_bf[:, :],
                                    deps=([lne[j], eve.get(j - 2)]
                                          if c == 0 else ()))
                            eve[j] = q.op(
                                "v", "tensor_copy",
                                xnT[:, :, j * 128:(j + 1) * 128],
                                ptr[j % 2][:, 0:1024].rearrange(
                                    "p (c n) -> p c n", n=128),
                                deps=[tre[(j, 7)]])
                    if b == 0:
                        emit_wv_pooling(xt, zt, lne[NT - 1])
                    psq = [E3(nc.psum_tensor(uname("psq"), [128, 512], F32))
                           for _ in range(2)]
                    k = 0
                    evace = {}
                    for h in range(H):
                        for which, wbuf, bias, wev in ((0, wq_bf, bq_sb, wqe),
                                                       (1, wk_bf, bk_sb, wke)):
                            dst = qT if which == 0 else kT
                            for p in range(4):
                                dep = [evace.get(k - 2)]
                                if h == 0 and p == 0:
                                    dep.append(wev)
                                if h == 0 and which == 0:
                                    dep.append(eve[4 * p + 3])
                                ge = q.mm_group([
                                    (psq[k % 2][:, :],
                                     wbuf[:, c, h * 128:(h + 1) * 128],
                                     xnT[:, c, p * 512:(p + 1) * 512])
                                    for c in range(8)],
                                    deps=dep)
                                evace[k] = q.op(
                                    "s", "activation",
                                    dst[:, h, p * 512:(p + 1) * 512],
                                    psq[k % 2][:, :], AF.Identity,
                                    bias=bias[:, h:h + 1],
                                    deps=[ge] + (bqke if h == 0 and p == 0
                                                 else []))
                                k += 1
                    for ti in range(16):
                        for n2 in range(2):
                            mms = [
                                (psq[k % 2][:, :],
                                 xnT[:, c, ti * 128:(ti + 1) * 128],
                                 wvp_bf[:, c, n2 * 512:(n2 + 1) * 512])
                                for c in range(8)]
                            mms.append((psq[k % 2][:, :], ones_bf[:, :],
                                        bvp_bf[:, n2 * 512:(n2 + 1) * 512]))
                            ge = q.mm_group(
                                mms, deps=[evace.get(k - 2)] +
                                ([pool_done[0]] if ti == 0 and n2 == 0
                                 else []))
                            evace[k] = q.op(
                                "s", "activation",
                                vp[:, ti, n2 * 512:(n2 + 1) * 512],
                                psq[k % 2][:, :], AF.Identity, deps=[ge])
                            k += 1
                    q.dma_fence()
                stAB.close()

                with ExitStack() as st3:
                    E3 = st3.enter_context
                    P_sb = E3(nc.sbuf_tensor(uname("P_sb"), [128, 8, 2048], BF16))
                    PT_sb = [E3(nc.sbuf_tensor(uname("PT_sb"), [128, 1024], BF16))
                             for _ in range(2)]
                    aT_sb = [E3(nc.sbuf_tensor(uname("aT_sb"), [128, 512], F32))
                             for _ in range(2)]
                    att_sb = [E3(nc.sbuf_tensor(uname("att_sb"), [128, 512], F32))
                              for _ in range(2)]
                    mx = E3(nc.sbuf_tensor(uname("mx"), [128, 64], F32))
                    rs = E3(nc.sbuf_tensor(uname("rs"), [128, 32], F32))
                    rrs = E3(nc.sbuf_tensor(uname("rrs"), [128, 16], F32))
                    ps_scA = E3(nc.psum_tensor(uname("ps_scA"), [128, 1024], F32))
                    ps_scB = E3(nc.psum_tensor(uname("ps_scB"), [128, 1024], F32))
                    ptb = [E3(nc.psum_tensor(uname("ptb"), [128, 1024], BF16))
                           for _ in range(2)]
                    attP = E3(nc.psum_tensor(uname("attP"), [128, 512], F32))
                    ptf = E3(nc.psum_tensor(uname("ptf"), [128, 128], F32))
                    E3(nc.Block())
                    exp_g = {}
                    mx_war = {}
                    rrs_war = {}
                    rs_war = {}
                    pt_last = {}
                    ptevac = {}
                    pv_ev = {}
                    aT_ev = {}
                    tb_ev = {}
                    tbe_ev = {}
                    dma_ev = {}
                    recip_ev = {}

                    expA_g = {}
                    expB_g = {}
                    deferred = []

                    def flush_deferred():
                        while deferred:
                            qt, eB = deferred.pop(0)
                            q.op("v", "tensor_add", rs[:, 2 * qt:2 * qt + 1],
                                 rs[:, 2 * qt:2 * qt + 1],
                                 rs[:, 2 * qt + 1:2 * qt + 2],
                                 deps=[eB, rrs_war.get(qt)])
                            recip_ev[qt] = q.op("v", "reciprocal",
                                                rrs[:, qt:qt + 1],
                                                rs[:, 2 * qt:2 * qt + 1])
                            rs_war[qt] = recip_ev[qt]

                    def emit_scores(sg, j, h):
                        qt = (sg % 4) * 4 + j
                        qtg = sg * 4 + j
                        hh = sg // 4
                        c0 = 4 * qt
                        sA = None
                        sB = None
                        for p in range(2):
                            sB = q.mm(
                                ps_scB[:, p * 512:(p + 1) * 512],
                                qT[:, hh, qt * 128:(qt + 1) * 128],
                                kT[:, hh, 1024 + p * 512:1024 + (p + 1) * 512],
                                start=True, stop=True,
                                deps=([expB_g.get(qtg - 1)] if p == 0 else ()))
                        for p in range(2):
                            sA = q.mm(
                                ps_scA[:, p * 512:(p + 1) * 512],
                                qT[:, hh, qt * 128:(qt + 1) * 128],
                                kT[:, hh, p * 512:(p + 1) * 512],
                                start=True, stop=True,
                                deps=([expA_g.get(qtg - 1)] if p == 0 else ()))
                        q.op("v", "reduce_max", mx[:, c0 + 1:c0 + 2],
                             ps_scB[:, :], axis=AX.X,
                             deps=[sB, mx_war.get(qt)])
                        q.op("v", "reduce_max", mx[:, c0:c0 + 1],
                             ps_scA[:, :], axis=AX.X, deps=[sA])
                        q.op("v", "tensor_max", mx[:, c0 + 2:c0 + 3],
                             mx[:, c0:c0 + 1], mx[:, c0 + 1:c0 + 2])
                        nm = q.op("v", "tensor_scalar", mx[:, c0 + 3:c0 + 4],
                                  mx[:, c0 + 2:c0 + 3], -ISQ128, None,
                                  OP.mult)
                        eA = q.op("s", "activation",
                                  P_sb[:, 4 * (sg % 2) + j, 0:1024],
                                  ps_scA[:, :], AF.Exp, scale=ISQ128,
                                  bias=mx[:, c0 + 3:c0 + 4],
                                  accum_out=rs[:, 2 * qt:2 * qt + 1],
                                  deps=[nm, pt_last.get(sg - 2),
                                        rs_war.get(qt)] if j == 0 else
                                  [nm, rs_war.get(qt)])
                        eB = q.op("s", "activation",
                                  P_sb[:, 4 * (sg % 2) + j, 1024:2048],
                                  ps_scB[:, :], AF.Exp, scale=ISQ128,
                                  bias=mx[:, c0 + 3:c0 + 4],
                                  accum_out=rs[:, 2 * qt + 1:2 * qt + 2])
                        expA_g[qtg] = eA
                        expB_g[qtg] = eB
                        exp_g[qtg] = eB
                        mx_war[qt] = eA
                        deferred.append((qt, eB))

                    def emit_pv(sg, pairs, h):
                        for kp in pairs:
                            kg = sg * 8 + kp
                            first = []
                            if kp == 0:
                                first.append(exp_g[sg * 4 + 3])
                                first.append(aT_ev.get(sg - 1))
                            te = None
                            for kk2 in range(2):
                                kc = 2 * kp + kk2
                                for j in range(4):
                                    te = q.tr(
                                        ptb[kp % 2][:, kk2 * 512 + j * 128:
                                                    kk2 * 512 + (j + 1) * 128],
                                        P_sb[:, 4 * (sg % 2) + j,
                                             kc * 128:(kc + 1) * 128],
                                        ident_bf[:, :],
                                        deps=(first + [ptevac.get(kg - 2)])
                                        if (kk2 == 0 and j == 0) else ())
                            if kp == 7:
                                pt_last[sg] = te
                            if kp % 2 == 0:
                                pe = q.op("v", "tensor_copy",
                                          PT_sb[kp % 2][:, :],
                                          ptb[kp % 2][:, :],
                                          deps=[te, pv_ev.get((sg, 2 * kp - 3))])
                            else:
                                pe = q.op("s", "activation",
                                          PT_sb[kp % 2][:, :],
                                          ptb[kp % 2][:, :], AF.Identity,
                                          deps=[te, pv_ev.get((sg, 2 * kp - 3))])
                            ptevac[kg] = pe
                            for kk2 in range(2):
                                kc = 2 * kp + kk2
                                pv_ev[(sg, kc)] = q.mm(
                                    attP[:, :],
                                    vp[:, kc, h * 128:(h + 1) * 128],
                                    PT_sb[kp % 2][:, kk2 * 512:
                                                  (kk2 + 1) * 512],
                                    start=(kc == 0), stop=(kc == 15),
                                    deps=[pe] if kk2 == 0 else ())

                    def emit_tail(sg, h):
                        aT_ev[sg] = q.op("v", "tensor_copy",
                                         aT_sb[sg % 2][:, :], attP[:, :],
                                         deps=[pv_ev[(sg, 15)],
                                               tb_ev.get((sg - 2, 3))])
                        for j in range(4):
                            qt = (sg % 4) * 4 + j
                            tb_ev[(sg, j)] = q.tr(
                                ptf[:, :], aT_sb[sg % 2][:, j * 128:(j + 1) * 128],
                                ident_f32[:, :],
                                deps=[aT_ev[sg], tbe_ev.get((sg, j - 1))]
                                if j > 0 else
                                [aT_ev[sg], tbe_ev.get((sg - 1, 3))])
                            tbe_ev[(sg, j)] = q.op(
                                "s", "activation",
                                att_sb[sg % 2][:, j * 128:(j + 1) * 128],
                                ptf[:, :], AF.Identity,
                                scale=rrs[:, qt:qt + 1],
                                deps=[tb_ev[(sg, j)]] +
                                ([dma_ev.get(sg - 2)] if j == 0 else []))
                            rrs_war[qt] = tbe_ev[(sg, j)]
                        hh = sg // 4
                        ss = sg % 4
                        base = b * 2097152 + hh * 262144 + ss * 65536
                        seg = attn_flat[base:base + 65536]
                        dma_ev[sg] = q.dma(
                            "y", seg.rearrange("(j p m) -> p j m", p=128, m=128),
                            att_sb[sg % 2][:, :].rearrange(
                                "p (j m) -> p j m", m=128),
                            deps=[tbe_ev[(sg, 3)]])

                    NSLAB = H * 4
                    for sg in range(NSLAB + 1):
                        for j in range(4):
                            if sg < NSLAB:
                                emit_scores(sg, j, sg // 4)
                                if j > 0 or sg > 0:
                                    flush_deferred()
                            if sg > 0:
                                emit_pv(sg - 1, [j], (sg - 1) // 4)
                        if sg > 0:
                            flush_deferred()
                            emit_pv(sg - 1, [4, 5, 6, 7], (sg - 1) // 4)
                            emit_tail(sg - 1, (sg - 1) // 4)
                    flush_deferred()
                    q.dma_fence()
    return nc


def build_l0b(nc, q, x, attn, w_router, W1, b1, W2, b2, x2_out, consts,
              x1h_hbm, hT_hbm, row_hbm):
    ident_bf, ident_f32, ones_bf, ones_f32, cpool = consts
    eps = cpool[:, 0:1]
    NTILE, NITER = 32, 44

    x3 = x.rearrange("(i p) d -> i p d", p=128)
    at3 = attn.rearrange("(i p) d -> i p d", p=128)
    x1h3 = x1h_hbm.rearrange("(i p) d -> i p d", p=128)
    x23 = x2_out.rearrange("(i p) d -> i p d", p=128)
    w13 = W1.rearrange("(c p) f -> c p f", p=128)
    w23 = W2.rearrange("(c p) d -> c p d", p=128)

    with ExitStack() as st0:
        E = st0.enter_context
        w1_bf = E(nc.sbuf_tensor(uname("w1_bf"), [128, 8, 4096], BF16))
        w2_bf = E(nc.sbuf_tensor(uname("w2_bf"), [128, 32, 1024], BF16))
        b1_sb = E(nc.sbuf_tensor(uname("b1_sb"), [128, 32], F32))
        b2_rep = E(nc.sbuf_tensor(uname("b2_rep"), [128, 1024], F32))
        s128 = E(nc.sbuf_tensor(uname("s128"), [128, 32], F32))
        m128 = E(nc.sbuf_tensor(uname("m128"), [128, 32], F32))
        stWR = ExitStack()
        wr_rep = stWR.enter_context(
            nc.sbuf_tensor(uname("wr_rep"), [128, 1024], F32))

        with ExitStack() as st1:
            E1 = st1.enter_context
            E1(nc.Block())
            w1we = q.dma("g", w1_bf[:, :, :], w13.rearrange("c p f -> p c f"))
            w2we = q.dma("g", w2_bf[:, :, :], w23.rearrange("c p d -> p c d"))
            b1we = []
            for mt in range(32):
                b1we.append(q.dma("y", b1_sb[:, mt:mt + 1],
                            b1[mt * 128:(mt + 1) * 128].unsqueeze(1)))
            b2we = q.dma("y", b2_rep[:, :],
                         b2.unsqueeze(0).broadcast_to((128, 1024)))
            wrwe = q.dma("y", wr_rep[:, :],
                         w_router.unsqueeze(0).broadcast_to((128, 1024)))

        with ExitStack() as st1:
            E1 = st1.enter_context
            hT = E1(nc.sbuf_tensor(uname("hT_half"), [128, 8, 2048], BF16))
            xt = [E1(nc.sbuf_tensor(uname("xt"), [128, 1024], F32))
                  for _ in range(2)]
            att = [E1(nc.sbuf_tensor(uname("att"), [128, 1024], F32))
                   for _ in range(2)]
            zt = E1(nc.sbuf_tensor(uname("zt"), [128, 1024], F32))
            hbf = [E1(nc.sbuf_tensor(uname("hbf"), [128, 1024], BF16))
                   for _ in range(2)]
            st6 = [E1(nc.sbuf_tensor(uname("st6"), [128, 12], F32))
                   for _ in range(2)]
            mv = [E1(nc.sbuf_tensor(uname("mv"), [128, 4], F32))
                  for _ in range(2)]
            rstd = [E1(nc.sbuf_tensor(uname("rstd"), [128, 2], F32))
                    for _ in range(2)]
            ptr = [E1(nc.psum_tensor(uname("ptr"), [128, 1024], BF16))
                   for _ in range(2)]
            E1(nc.Block())
            x1de = {}
            adde = {}
            eve = {}
            tre = {}
            r1e = {}
            dume = None
            for i in range(NTILE + 1):
                if i < NTILE:
                    dx = q.dma("y", xt[i % 2][:, :], x3[i],
                               deps=[x1de.get(i - 2)])
                    da = q.dma("y", att[i % 2][:, :], at3[i],
                               deps=[adde.get(i - 2)])
                    q.op("v", "tensor_add", xt[i % 2][:, :], xt[i % 2][:, :],
                         att[i % 2][:, :], deps=[dx, da])
                    r1e[i], _ = ln_stats_comb(q, xt[i % 2][:, :], st6[i % 2],
                                              mv[i % 2], rstd[i % 2], eps,
                                              None)
                if i > 0:
                    j = i - 1
                    half, ih = j // 16, j % 16
                    q.op("v", "tensor_scalar", zt[:, :], xt[j % 2][:, :],
                         mv[j % 2][:, 0:1], rstd[j % 2][:, 0:1],
                         OP.subtract, OP.mult, deps=[r1e[j]])
                    cpv = q.op("v", "tensor_copy", hbf[j % 2][:, :], zt[:, :])
                    adde[j] = q.op("v", "tensor_mul", att[j % 2][:, :],
                                   zt[:, :], wr_rep[:, :],
                                   deps=[wrwe] if j == 0 else ())
                    q.op("v", "reduce_sum", s128[:, j:j + 1], att[j % 2][:, :],
                         axis=AX.X)
                    x1a = q.op("v", "tensor_add", xt[j % 2][:, :],
                               xt[j % 2][:, :], zt[:, :])
                    x1de[j] = q.dma("y", x1h3[j], xt[j % 2][:, :], deps=[x1a])
                    for c in range(8):
                        tre[(j, c)] = q.tr(
                            ptr[j % 2][:, c * 128:(c + 1) * 128],
                            hbf[j % 2][:, c * 128:(c + 1) * 128],
                            ident_bf[:, :],
                            deps=([cpv, eve.get(j - 2)] if c == 0 else ()))
                    eve[j] = q.op("v", "tensor_copy",
                                  hT[:, :, ih * 128:(ih + 1) * 128],
                                  ptr[j % 2][:, 0:1024].rearrange(
                                      "p (c n) -> p c n", n=128),
                                  deps=[tre[(j, 7)]] +
                                  ([dume] if ih == 0 and dume else []))
                    if ih == 15:
                        dume = q.dma(
                            "y", hT_hbm[:, :, half * 2048:(half + 1) * 2048],
                            hT[:, :, :], deps=[eve[j], dume])
            q.dma_fence()
        stWR.close()

        with ExitStack() as st1:
            E1 = st1.enter_context
            hT = E1(nc.sbuf_tensor(uname("hT_c"), [128, 8, 2048], BF16))
            h2 = E1(nc.sbuf_tensor(uname("h2"), [128, 32, 512], BF16))
            x1h_t = [E1(nc.sbuf_tensor(uname("x1h_t"), [128, 1024], F32))
                     for _ in range(1)]
            out_t = [E1(nc.sbuf_tensor(uname("out_t"), [128, 1024], F32))
                     for _ in range(1)]
            stt = E1(nc.sbuf_tensor(uname("stt"), [1, 16], F32))
            sti = E1(nc.sbuf_tensor(uname("sti"), [1, 4], I32))
            tpp = E1(nc.sbuf_tensor(uname("tpp"), [128, 2], F32))
            ge128 = E1(nc.sbuf_tensor(uname("ge128"), [128, 32], F32))
            red2 = E1(nc.sbuf_tensor(uname("red2"), [128, 2], F32))
            th128 = E1(nc.sbuf_tensor(uname("th128"), [128, 32], F32))
            ones_col = E1(nc.sbuf_tensor(uname("ones_col"), [128, 1], F32))
            ps1 = [E1(nc.psum_tensor(uname("ps1"), [128, 512], F32))
                   for _ in range(2)]
            ps2 = [E1(nc.psum_tensor(uname("ps2"), [128, 512], F32))
                   for _ in range(4)]
            psb = E1(nc.psum_tensor(uname("psb"), [128, 2], F32))
            E1(nc.Block())
            lo, hi, tt, cnt = (stt[:, 0:2], stt[:, 2:4], stt[:, 4:6],
                               stt[:, 6:8])
            mce = q.op("g", "memset", ones_col[:, :], 1.0)
            q.op("v", "memset", lo[:, :], -1000.0)
            q.op("v", "memset", hi[:, :], 1000.0)
            q.op("v", "tensor_add", tt, lo, hi)
            te = q.op("v", "tensor_scalar_mul", tt, tt, 0.5)
            bis_last_s = None
            for it in range(NITER):
                bc = q.mm(psb[:, :], ones_f32[0:1, :], tt, start=True,
                          stop=True, deps=[te, mce])
                bce = q.op("s", "activation", tpp[:, :], psb[:, :],
                           AF.Identity, deps=[bc])
                g1 = q.op("v", "tensor_scalar", ge128[:, 0:16],
                          s128[:, 0:16], tpp[:, 0:1], None, OP.is_ge,
                          deps=[bce])
                q.op("v", "tensor_scalar", ge128[:, 16:32], s128[:, 16:32],
                     tpp[:, 1:2], None, OP.is_ge)
                q.op("v", "reduce_sum", red2[:, 0:1], ge128[:, 0:16],
                     axis=AX.X)
                re = q.op("v", "reduce_sum", red2[:, 1:2], ge128[:, 16:32],
                          axis=AX.X)
                cm = q.mm(psb[0:1, :], ones_col[:, :], red2[:, :], start=True,
                          stop=True, deps=[re])
                ce = q.op("s", "activation", cnt, psb[0:1, :], AF.Identity,
                          deps=[cm])
                q.op("v", "tensor_scalar", sti[:, 0:2], cnt, 1024.0, None,
                     OP.is_ge, deps=[ce])
                q.op("v", "tensor_scalar", sti[:, 2:4], cnt, 1024.0, None,
                     OP.is_lt)
                q.op("v", "select", lo, sti[:, 0:2], tt, lo, add_drain=True)
                q.op("v", "select", hi, sti[:, 2:4], tt, hi, add_drain=True)
                q.op("v", "tensor_add", tt, lo, hi)
                te = q.op("v", "tensor_scalar_mul", tt, tt, 0.5)
            bc = q.mm(psb[:, :], ones_f32[0:1, :], lo, start=True, stop=True,
                      deps=[te])
            bce = q.op("s", "activation", tpp[:, :], psb[:, :], AF.Identity,
                       deps=[bc])
            q.op("v", "tensor_scalar", ge128[:, 0:16], s128[:, 0:16],
                 tpp[:, 0:1], None, OP.is_ge, deps=[bce])
            q.op("v", "tensor_scalar", ge128[:, 16:32], s128[:, 16:32],
                 tpp[:, 1:2], None, OP.is_ge)
            the = q.op("s", "activation", th128[:, :], s128[:, :], AF.Tanh,
                       scale=0.5, deps=[bce])
            q.op("v", "tensor_scalar", th128[:, :], th128[:, :], 0.5, 0.5,
                 OP.mult, OP.add, deps=[the])
            m_ev = q.op("v", "tensor_mul", m128[:, :], th128[:, :],
                        ge128[:, :])

            k1 = 0
            k2 = 0
            gele = {}
            w2le = {}
            combe = {}
            dmae = {}
            x1de = {}
            hle = None
            for half in range(2):
                hle = q.dma("y", hT[:, :, :],
                            hT_hbm[:, :, half * 2048:(half + 1) * 2048],
                            deps=[w2le.get(k2 - 1), hle])
                for sl in range(4):
                    tok0 = half * 2048 + sl * 512
                    slab_gel = []
                    for mt in range(32):
                        dep = [gele.get(k1 - 2)]
                        if mt == 0:
                            dep.append(hle)
                            dep.append(w2le.get(k2 - 4))
                            if k1 == 0:
                                dep.append(w1we)
                        ge_ = q.mm_group([
                            (ps1[k1 % 2][:, :],
                             w1_bf[:, c, mt * 128:(mt + 1) * 128],
                             hT[:, c, sl * 512:(sl + 1) * 512])
                            for c in range(8)], deps=dep)
                        gele[k1] = q.op("s", "activation", h2[:, mt, :],
                                        ps1[k1 % 2][:, :],
                                        AF.Gelu_apprx_tanh,
                                        bias=b1_sb[:, mt:mt + 1],
                                        deps=[ge_] + (b1we if k1 == 0 else []))
                        slab_gel.append(gele[k1])
                        k1 += 1
                    for ti in range(4):
                        gi = (tok0 // 128) + ti
                        xde = q.dma("y", x1h_t[0][:, :], x1h3[gi],
                                    deps=[combe.get((gi - 1, 1))])
                        for ds in range(2):
                            mms = [
                                (ps2[k2 % 4][:, :],
                                 h2[:, fc, ti * 128:(ti + 1) * 128],
                                 w2_bf[:, fc, ds * 512:(ds + 1) * 512])
                                for fc in range(32)]
                            dep = [combe.get((gi - 2, ds))]
                            if ti == 0 and ds == 0:
                                dep.append(slab_gel[-1])
                                if k2 == 0:
                                    dep.append(w2we)
                            w2le[k2] = q.mm_group(mms, deps=dep)
                            c1 = q.op("v", "scalar_tensor_tensor",
                                      out_t[0][:, ds * 512:(ds + 1) * 512],
                                      ps2[k2 % 4][:, :], m128[:, gi:gi + 1],
                                      x1h_t[0][:, ds * 512:(ds + 1) * 512],
                                      OP.mult, OP.add,
                                      deps=[w2le[k2], xde, m_ev,
                                            dmae.get(gi - 1)] +
                                      ([b2we] if k2 == 0 else []))
                            combe[(gi, ds)] = q.op(
                                "v", "scalar_tensor_tensor",
                                out_t[0][:, ds * 512:(ds + 1) * 512],
                                b2_rep[:, ds * 512:(ds + 1) * 512],
                                m128[:, gi:gi + 1],
                                out_t[0][:, ds * 512:(ds + 1) * 512],
                                OP.mult, OP.add)
                            k2 += 1
                        dmae[gi] = q.dma("y", x23[gi], out_t[0][:, :],
                                         deps=[combe[(gi, 1)]])
            q.dma_fence()
    return nc


def build_l1a(nc, q, x2, wdict, x3_out, consts, z_hbm, g_hbm, ao_hbm,
              x1i_hbm, hT_hbm):
    ident_bf, ident_f32, ones_bf, ones_f32, cpool = consts
    eps = cpool[:, 0:1]
    NB, T, NT, H, DH = 2, 2048, 16, 16, 64

    x3v = x2.rearrange("(i p) d -> i p d", p=128)
    zv = z_hbm.rearrange("(i p) d -> i p d", p=128)
    gv = g_hbm.rearrange("(i p) d -> i p d", p=128)
    x1iv = x1i_hbm.rearrange("(i p) d -> i p d", p=128)
    xov = x3_out.rearrange("(i p) d -> i p d", p=128)
    aov = ao_hbm.rearrange("(i p) d -> i p d", p=128)

    W = {k: v.rearrange("(c p) d -> c p d", p=128) if k in
         ("WQ", "WK", "WV", "WO", "Wg", "W1", "Bw") else v
         for k, v in wdict.items()}
    w23 = wdict["W2"].rearrange("(c p) d -> c p d", p=128)

    with ExitStack() as st0:
        E = st0.enter_context
        stAttn = ExitStack()
        EA = stAttn.enter_context
        ln1g = EA(nc.sbuf_tensor(uname("ln1g"), [128, 1024], F32))
        ln1b = EA(nc.sbuf_tensor(uname("ln1b"), [128, 1024], F32))
        bw_bf = EA(nc.sbuf_tensor(uname("bw_bf"), [128, 8, 64], BF16))
        cw_bf = EA(nc.sbuf_tensor(uname("cw_bf"), [64, 1024], BF16))
        apow = EA(nc.sbuf_tensor(uname("apow"), [64, 12], F32))
        wmask = EA(nc.sbuf_tensor(uname("wmask"), [128, 256], F32))

        with ExitStack() as st1:
            E1 = st1.enter_context
            zero128 = E1(nc.sbuf_tensor(uname("zero128"), [128, 128], F32))
            E1(nc.Block())
            l1ge = q.dma("y", ln1g[:, :],
                         wdict["ln1_g"].unsqueeze(0).broadcast_to((128, 1024)))
            l1be = q.dma("y", ln1b[:, :],
                         wdict["ln1_b"].unsqueeze(0).broadcast_to((128, 1024)))
            bwe = q.dma("g", bw_bf[:, :, :], W["Bw"].rearrange("c p d -> p c d"))
            cwe = q.dma("g", cw_bf[:, :], wdict["Cw"])
            ade = q.dma("y", apow[:, 0:1], wdict["A"].unsqueeze(1))
            for j in range(10):
                q.op("v", "tensor_mul", apow[0:64, j + 1:j + 2],
                     apow[0:64, j:j + 1], apow[0:64, j:j + 1],
                     deps=[ade] if j == 0 else ())
            ze = q.op("g", "memset", zero128[:, :], 0.0)
            q.op("g", "affine_select", wmask[:, 0:128], zero128[:, :],
                 pattern=[[1, 128]], compare_op=OP.is_ge, fill=NEG_BIG,
                 base=-1, channel_multiplier=-1, deps=[ze])
            q.op("g", "affine_select", wmask[:, 128:256], zero128[:, :],
                 pattern=[[-1, 128]], compare_op=OP.is_ge, fill=NEG_BIG,
                 base=0, channel_multiplier=1)

        for b in range(NB):
            with ExitStack() as stb:
                Eb = stb.enter_context
                qT = Eb(nc.sbuf_tensor(uname("qT"), [128, 8, T], BF16))
                kT = Eb(nc.sbuf_tensor(uname("kT"), [128, 8, T], BF16))
                v_sb = Eb(nc.sbuf_tensor(uname("v_sb"), [128, 16, 1024], BF16))
                u_sb = Eb(nc.sbuf_tensor(uname("u_sb"), [64, 2048], F32))
                stT_bf = Eb(nc.sbuf_tensor(uname("stT_bf"), [64, 2048], BF16))
                stAB = ExitStack()
                hT = stAB.enter_context(
                    nc.sbuf_tensor(uname("hT"), [128, 8, T], BF16))

                with ExitStack() as st1:
                    E1 = st1.enter_context
                    xt = [E1(nc.sbuf_tensor(uname("xt"), [128, 1024], F32))
                          for _ in range(2)]
                    zt = [E1(nc.sbuf_tensor(uname("zt"), [128, 1024], F32))
                          for _ in range(1)]
                    hbf = [E1(nc.sbuf_tensor(uname("hbf"), [128, 1024], BF16))
                           for _ in range(1)]
                    st6 = [E1(nc.sbuf_tensor(uname("st6"), [128, 12], F32))
                           for _ in range(2)]
                    mv = [E1(nc.sbuf_tensor(uname("mv"), [128, 4], F32))
                          for _ in range(2)]
                    rstd = [E1(nc.sbuf_tensor(uname("rstd"), [128, 2], F32))
                            for _ in range(2)]
                    ptr = [E1(nc.psum_tensor(uname("ptr"), [128, 1024], BF16))
                           for _ in range(2)]
                    E1(nc.Block())
                    zde = {}
                    eve = {}
                    tre = {}
                    rce = {}
                    for i in range(NT + 1):
                        if i < NT:
                            de = q.dma("y", xt[i % 2][:, :], x3v[b * NT + i],
                                       deps=[eve.get(i - 2)])
                            rce[i] = ln_stats_comb(q, xt[i % 2][:, :],
                                                   st6[i % 2], mv[i % 2],
                                                   rstd[i % 2], eps, de)
                        if i > 0:
                            j = i - 1
                            gi = b * NT + j
                            r1e, rcomb_e = rce[j]
                            zz = q.op("v", "tensor_scalar", zt[0][:, :],
                                      xt[j % 2][:, :], mv[j % 2][:, 0:1],
                                      rstd[j % 2][:, 0:1],
                                      OP.subtract, OP.mult,
                                      deps=[r1e, zde.get(j - 1)])
                            zde[j] = q.dma("y", zv[gi], zt[0][:, :],
                                           deps=[zz])
                            q.op("v", "tensor_scalar", xt[j % 2][:, :],
                                 xt[j % 2][:, :], mv[j % 2][:, 0:1],
                                 rstd[j % 2][:, 1:2], OP.subtract, OP.mult,
                                 deps=[rcomb_e])
                            q.op("v", "tensor_mul", xt[j % 2][:, :],
                                 xt[j % 2][:, :],
                                 ln1g[:, :], deps=[l1ge] if j == 0 else ())
                            he = q.op("v", "tensor_add", hbf[0][:, :],
                                      xt[j % 2][:, :], ln1b[:, :],
                                      deps=([l1be] if j == 0 else []) +
                                      [tre.get((j - 1, 7))])
                            for c in range(8):
                                tre[(j, c)] = q.tr(
                                    ptr[j % 2][:, c * 128:(c + 1) * 128],
                                    hbf[0][:, c * 128:(c + 1) * 128],
                                    ident_bf[:, :],
                                    deps=([he, eve.get(j - 2)]
                                          if c == 0 else ()))
                            eve[j] = q.op(
                                "v", "tensor_copy",
                                hT[:, :, j * 128:(j + 1) * 128],
                                ptr[j % 2][:, 0:1024].rearrange(
                                    "p (c n) -> p c n", n=128),
                                deps=[tre[(j, 7)]])
                    bb_sb = E1(nc.sbuf_tensor(uname("bb_sb"), [128, 16], F32))
                    psq = [E1(nc.psum_tensor(uname("psq"), [128, 512], F32))
                           for _ in range(2)]
                    bbe = {}
                    for mt in range(8):
                        bbe[mt] = q.dma(
                            "y", bb_sb[:, mt:mt + 1],
                            wdict["bQ"][mt * 128:(mt + 1) * 128].unsqueeze(1))
                        bbe[8 + mt] = q.dma(
                            "y", bb_sb[:, 8 + mt:9 + mt],
                            wdict["bK"][mt * 128:(mt + 1) * 128].unsqueeze(1))
                    k = 0
                    evace = {}
                    with ExitStack() as st2:
                        E2 = st2.enter_context
                        wq_bf = E2(nc.sbuf_tensor(uname("wq_bf"),
                                                  [128, 8, 1024], BF16))
                        wk_bf = E2(nc.sbuf_tensor(uname("wk_bf"),
                                                  [128, 8, 1024], BF16))
                        we1 = q.dma("g", wq_bf[:, :, :],
                                    W["WQ"].rearrange("c p d -> p c d"))
                        we2 = q.dma("g", wk_bf[:, :, :],
                                    W["WK"].rearrange("c p d -> p c d"))
                        for mt in range(8):
                            for which, wbuf, we in ((0, wq_bf, we1),
                                                    (1, wk_bf, we2)):
                                dst = qT if which == 0 else kT
                                boff = 0 if which == 0 else 8
                                for p in range(4):
                                    dep = [evace.get(k - 2)]
                                    if mt == 0 and p == 0:
                                        dep.append(we)
                                    if mt == 0 and which == 0:
                                        dep.append(eve[4 * p + 3])
                                    ge = q.mm_group([
                                        (psq[k % 2][:, :],
                                         wbuf[:, c, mt * 128:(mt + 1) * 128],
                                         hT[:, c, p * 512:(p + 1) * 512])
                                        for c in range(8)], deps=dep)
                                    evace[k] = q.op(
                                        "s", "activation",
                                        dst[:, mt, p * 512:(p + 1) * 512],
                                        psq[k % 2][:, :], AF.Identity,
                                        bias=bb_sb[:, boff + mt:boff + mt + 1],
                                        deps=[ge] +
                                        ([bbe[boff + mt]] if p == 0 else []))
                                    k += 1
                    with ExitStack() as st2:
                        E2 = st2.enter_context
                        wv_bf = E2(nc.sbuf_tensor(uname("wv_bf"),
                                                  [128, 8, 1024], BF16))
                        wg_bf = E2(nc.sbuf_tensor(uname("wg_bf"),
                                                  [128, 8, 1024], BF16))
                        bv_bf = E2(nc.sbuf_tensor(uname("bv_bf"), [1, 1024], BF16))
                        bg_bf = E2(nc.sbuf_tensor(uname("bg_bf"), [1, 1024], BF16))
                        gbf = [E2(nc.sbuf_tensor(uname("gbf"), [128, 1024], BF16))
                               for _ in range(1)]
                        we1 = q.dma("g", wv_bf[:, :, :],
                                    W["WV"].rearrange("c p d -> p c d"),
                                    deps=[evace[k - 1]])
                        we2 = q.dma("g", wg_bf[:, :, :],
                                    W["Wg"].rearrange("c p d -> p c d"))
                        be1 = q.dma("g", bv_bf[:, :], wdict["bV"].unsqueeze(0))
                        be2 = q.dma("g", bg_bf[:, :], wdict["bg"].unsqueeze(0))
                        gde = {}
                        for ti in range(16):
                            for n2 in range(2):
                                dep = [evace.get(k - 2)]
                                if ti == 0 and n2 == 0:
                                    dep += [we1, be1]
                                mms = [
                                    (psq[k % 2][:, :],
                                     hT[:, c, ti * 128:(ti + 1) * 128],
                                     wv_bf[:, c, n2 * 512:(n2 + 1) * 512])
                                    for c in range(8)]
                                mms.append((psq[k % 2][:, :], ones_bf[:, :],
                                            bv_bf[:, n2 * 512:(n2 + 1) * 512]))
                                ge = q.mm_group(mms, deps=dep)
                                evace[k] = q.op(
                                    "s", "activation",
                                    v_sb[:, ti, n2 * 512:(n2 + 1) * 512],
                                    psq[k % 2][:, :], AF.Identity, deps=[ge])
                                k += 1
                            for n2 in range(2):
                                dep = [evace.get(k - 2)]
                                if ti == 0 and n2 == 0:
                                    dep += [we2, be2]
                                if n2 == 0:
                                    dep += [gde.get(ti - 1)]
                                mms = [
                                    (psq[k % 2][:, :],
                                     hT[:, c, ti * 128:(ti + 1) * 128],
                                     wg_bf[:, c, n2 * 512:(n2 + 1) * 512])
                                    for c in range(8)]
                                mms.append((psq[k % 2][:, :], ones_bf[:, :],
                                            bg_bf[:, n2 * 512:(n2 + 1) * 512]))
                                ge = q.mm_group(mms, deps=dep)
                                evace[k] = q.op(
                                    "s", "activation",
                                    gbf[0][:, n2 * 512:(n2 + 1) * 512],
                                    psq[k % 2][:, :], AF.Sigmoid, deps=[ge])
                                k += 1
                            gde[ti] = q.dma("y", gv[b * NT + ti],
                                            gbf[0][:, :],
                                            deps=[evace[k - 1]])
                        for p in range(4):
                            ge = q.mm_group([
                                (psq[k % 2][0:64, :],
                                 bw_bf[:, c, 0:64],
                                 hT[:, c, p * 512:(p + 1) * 512])
                                for c in range(8)],
                                deps=[evace.get(k - 2)] +
                                ([bwe] if p == 0 else []))
                            evace[k] = q.op(
                                "v", "tensor_copy",
                                u_sb[:, p * 512:(p + 1) * 512],
                                psq[k % 2][0:64, :], deps=[ge])
                            k += 1
                        q.dma_fence()
                stAB.close()

                with ExitStack() as st1:
                    E1 = st1.enter_context
                    u_sb2 = E1(nc.sbuf_tensor(uname("u_sb2"), [64, 2048], F32))
                    ao_sb = E1(nc.sbuf_tensor(uname("ao_sb"), [128, 16, 1024], BF16))
                    msk = [E1(nc.sbuf_tensor(uname("msk"), [128, 256], F32))
                           for _ in range(2)]
                    P_bf = [E1(nc.sbuf_tensor(uname("P_bf"), [128, 256], BF16))
                            for _ in range(2)]
                    PT_sb = [E1(nc.sbuf_tensor(uname("PT_sb"), [128, 256], BF16))
                             for _ in range(2)]
                    mx = E1(nc.sbuf_tensor(uname("mx"), [128, 8], F32))
                    rs = E1(nc.sbuf_tensor(uname("rs"), [128, 8], F32))
                    ps_sc = [E1(nc.psum_tensor(uname("ps_sc"), [128, 256], F32))
                             for _ in range(2)]
                    ptb = [E1(nc.psum_tensor(uname("ptb"), [128, 256], BF16))
                           for _ in range(2)]
                    pao = [E1(nc.psum_tensor(uname("pao"), [128, 64], F32))
                           for _ in range(2)]
                    jnk = E1(nc.psum_tensor(uname("jnk"), [128, 256], F32))
                    E1(nc.Block())
                    src, dst = u_sb, u_sb2
                    for j in range(11):
                        o = 1 << j
                        q.op("v", "tensor_copy", dst[:, 0:o], src[:, 0:o])
                        q.op("v", "scalar_tensor_tensor", dst[:, o:2048],
                             src[:, 0:2048 - o], apow[0:64, j:j + 1],
                             src[:, o:2048], OP.mult, OP.add)
                        src, dst = dst, src
                    sce_scan = q.op("v", "tensor_copy", stT_bf[:, :], src[:, :])
                    sc_ev = {}
                    mske = {}
                    expe = {}
                    rece = {}
                    pte = {}
                    ptev = {}
                    aoe = {}
                    aoev = {}
                    kk = 0
                    for h in range(H):
                        mt, off = h // 2, (h % 2) * 64
                        for qb in range(16):
                            first = (qb == 0)
                            kw = 128 if first else 256
                            kb0 = qb if first else qb - 1
                            sc_ev[kk] = q.mm(
                                ps_sc[kk % 2][:, 0:kw],
                                qT[off:off + 64, mt, qb * 128:(qb + 1) * 128],
                                kT[off:off + 64, mt, kb0 * 128:kb0 * 128 + kw],
                                start=True, stop=True,
                                deps=[mske.get(kk - 2)])
                            mske[kk] = q.op("v", "tensor_add",
                                            msk[kk % 2][:, 0:kw],
                                            ps_sc[kk % 2][:, 0:kw],
                                            wmask[:, 256 - kw:256],
                                            deps=[sc_ev[kk],
                                                  expe.get(kk - 2)])
                            me = q.op("v", "reduce_max",
                                      mx[:, 2 * (kk % 2):2 * (kk % 2) + 1],
                                      msk[kk % 2][:, 0:kw], axis=AX.X,
                                      deps=[expe.get(kk - 2)])
                            nm = q.op("v", "tensor_scalar",
                                      mx[:, 2 * (kk % 2) + 1:2 * (kk % 2) + 2],
                                      mx[:, 2 * (kk % 2):2 * (kk % 2) + 1],
                                      -ISQ64, None, OP.mult)
                            expe[kk] = q.op(
                                "s", "activation", P_bf[kk % 2][:, 0:kw],
                                msk[kk % 2][:, 0:kw], AF.Exp, scale=ISQ64,
                                bias=mx[:, 2 * (kk % 2) + 1:2 * (kk % 2) + 2],
                                accum_out=rs[:, 2 * (kk % 2):2 * (kk % 2) + 1],
                                deps=[nm, pte.get(kk - 2),
                                      rece.get(kk - 2)])
                            rece[kk] = q.op(
                                "v", "reciprocal",
                                rs[:, 2 * (kk % 2) + 1:2 * (kk % 2) + 2],
                                rs[:, 2 * (kk % 2):2 * (kk % 2) + 1],
                                deps=[expe[kk], aoev.get(kk - 2)])
                            q.mm(jnk[:, :], ident_bf[:, :],
                                 v_sb[:, 0, 0:256], start=True, stop=True,
                                 deps=[mske[kk]])
                            q.mm(jnk[:, :], ident_bf[:, :],
                                 v_sb[:, 0, 0:256], start=True, stop=True,
                                 deps=[expe[kk]])
                            te = None
                            for kk2 in range(kw // 128):
                                te = q.tr(ptb[kk % 2][:, kk2 * 128:(kk2 + 1) * 128],
                                          P_bf[kk % 2][:, kk2 * 128:(kk2 + 1) * 128],
                                          ident_bf[:, :],
                                          deps=[expe[kk], ptev.get(kk - 2)]
                                          if kk2 == 0 else ())
                            pte[kk] = te
                            ptev[kk] = q.op("v", "tensor_copy",
                                            PT_sb[kk % 2][:, 0:kw],
                                            ptb[kk % 2][:, 0:kw],
                                            deps=[te, aoe.get(kk - 2)])
                            ae = None
                            for kk2 in range(kw // 128):
                                ae = q.mm(pao[kk % 2][:, :],
                                          PT_sb[kk % 2][:, kk2 * 128:(kk2 + 1) * 128],
                                          v_sb[:, kb0 + kk2, h * 64:(h + 1) * 64],
                                          start=(kk2 == 0),
                                          stop=(kk2 == kw // 128 - 1),
                                          deps=[ptev[kk], aoev.get(kk - 2)]
                                          if kk2 == 0 else ())
                            aoe[kk] = ae
                            aoev[kk] = q.op(
                                "s", "activation",
                                ao_sb[:, qb, h * 64:(h + 1) * 64],
                                pao[kk % 2][:, :], AF.Identity,
                                scale=rs[:, 2 * (kk % 2) + 1:2 * (kk % 2) + 2],
                                deps=[ae, rece[kk]])
                            kk += 1
                    dume = None
                    for ti in range(NT):
                        dume = q.dma("y", aov[b * NT + ti], ao_sb[:, ti, :],
                                     deps=[aoev[kk - 1], dume])
                    q.dma_fence()

                with ExitStack() as st1:
                    E1 = st1.enter_context
                    wo_bf = E1(nc.sbuf_tensor(uname("wo_bf"), [128, 8, 1024], BF16))
                    bo_bf = E1(nc.sbuf_tensor(uname("bo_bf"), [1, 1024], BF16))
                    aot = [E1(nc.sbuf_tensor(uname("aot"), [128, 1024], BF16))
                           for _ in range(2)]
                    aoT = [E1(nc.sbuf_tensor(uname("aoT"), [128, 8, 128], BF16))
                           for _ in range(2)]
                    zt = [E1(nc.sbuf_tensor(uname("zt2"), [128, 1024], F32))
                          for _ in range(2)]
                    gt = [E1(nc.sbuf_tensor(uname("gt"), [128, 1024], F32))
                          for _ in range(2)]
                    ssm_t = E1(nc.sbuf_tensor(uname("ssm_t"), [128, 512], F32))
                    t1 = E1(nc.sbuf_tensor(uname("t1"), [128, 512], F32))
                    x1i_t = [E1(nc.sbuf_tensor(uname("x1i_t"), [128, 1024], F32))
                             for _ in range(2)]
                    ptr = [E1(nc.psum_tensor(uname("ptr2"), [128, 1024], BF16))
                           for _ in range(2)]
                    pat = [E1(nc.psum_tensor(uname("pat"), [128, 512], F32))
                           for _ in range(2)]
                    pss = [E1(nc.psum_tensor(uname("pss"), [128, 512], F32))
                           for _ in range(2)]
                    E1(nc.Block())
                    woe = q.dma("g", wo_bf[:, :, :],
                                W["WO"].rearrange("c p d -> p c d"))
                    boe = q.dma("g", bo_bf[:, :], wdict["bO"].unsqueeze(0))
                    aode = {}
                    tre = {}
                    aoTe = {}
                    x1de = {}
                    cmbe = {}
                    ate = {}
                    sse2 = {}
                    for ti in range(NT):
                        gi = b * NT + ti
                        d1 = q.dma("y", aot[ti % 2][:, :], aov[gi],
                                   deps=[tre.get((ti - 2, 7))])
                        d2 = q.dma("y", zt[ti % 2][:, :], zv[gi],
                                   deps=[cmbe.get((ti - 2, 1))])
                        d3 = q.dma("g", gt[ti % 2][:, :], gv[gi],
                                   deps=[cmbe.get((ti - 2, 1))])
                        for c in range(8):
                            tre[(ti, c)] = q.tr(
                                ptr[ti % 2][:, c * 128:(c + 1) * 128],
                                aot[ti % 2][:, c * 128:(c + 1) * 128],
                                ident_bf[:, :],
                                deps=([d1, aoTe.get(ti - 2)] if c == 0
                                      else ()))
                        aoTe[ti] = q.op(
                            "v", "tensor_copy", aoT[ti % 2][:, :, :],
                            ptr[ti % 2][:, 0:1024].rearrange(
                                "p (c n) -> p c n", n=128),
                            deps=[tre[(ti, 7)]])
                        for ds in range(2):
                            mms = [(pat[ds][:, :],
                                    aoT[ti % 2][:, c, 0:128],
                                    wo_bf[:, c, ds * 512:(ds + 1) * 512])
                                   for c in range(8)]
                            mms.append((pat[ds][:, :], ones_bf[:, :],
                                        bo_bf[:, ds * 512:(ds + 1) * 512]))
                            dep = [aoTe[ti], cmbe.get((ti - 1, ds))]
                            if ti == 0:
                                dep.append(woe)
                                dep.append(boe)
                            ate[(ti, ds)] = q.mm_group(mms, deps=dep)
                            sse2[(ti, ds)] = q.mm(
                                pss[ds][:, :],
                                stT_bf[0:64, ti * 128:(ti + 1) * 128],
                                cw_bf[0:64, ds * 512:(ds + 1) * 512],
                                start=True, stop=True,
                                deps=[sce_scan if ti == 0 and ds == 0 else None,
                                      cwe if ti == 0 and ds == 0 else None,
                                      cmbe.get((ti - 1, ds))])
                            e1 = q.op("v", "tensor_copy", ssm_t[:, :],
                                      pss[ds][:, :], deps=[sse2[(ti, ds)]])
                            e2 = q.op("v", "tensor_sub", t1[:, :],
                                      pat[ds][:, :], ssm_t[:, :],
                                      deps=[ate[(ti, ds)]])
                            q.op("v", "tensor_mul", t1[:, :], t1[:, :],
                                 gt[ti % 2][:, ds * 512:(ds + 1) * 512],
                                 deps=[d3])
                            q.op("v", "tensor_add", ssm_t[:, :], ssm_t[:, :],
                                 zt[ti % 2][:, ds * 512:(ds + 1) * 512],
                                 deps=[d2])
                            cmbe[(ti, ds)] = q.op(
                                "v", "tensor_add",
                                x1i_t[ti % 2][:, ds * 512:(ds + 1) * 512],
                                t1[:, :], ssm_t[:, :],
                                deps=[x1de.get(ti - 2)])
                        x1de[ti] = q.dma("y", x1iv[gi], x1i_t[ti % 2][:, :],
                                         deps=[cmbe[(ti, 1)]])
                    q.dma_fence()
        stAttn.close()

        with ExitStack() as stm:
            Em = stm.enter_context
            w1_bf = Em(nc.sbuf_tensor(uname("w1_bf"), [128, 8, 4096], BF16))
            w2_bf = Em(nc.sbuf_tensor(uname("w2_bf"), [128, 32, 1024], BF16))
            b1_sb = Em(nc.sbuf_tensor(uname("b1_sb"), [128, 32], F32))
            b2_rep = Em(nc.sbuf_tensor(uname("b2_rep"), [128, 1024], F32))
            stLN = ExitStack()
            ln2g = stLN.enter_context(nc.sbuf_tensor(uname("ln2g"), [128, 1024], F32))
            ln2b = stLN.enter_context(nc.sbuf_tensor(uname("ln2b"), [128, 1024], F32))
            with ExitStack() as st1:
                E1 = st1.enter_context
                E1(nc.Block())
                l2ge = q.dma("y", ln2g[:, :],
                             wdict["ln2_g"].unsqueeze(0).broadcast_to(
                                 (128, 1024)))
                l2be = q.dma("y", ln2b[:, :],
                             wdict["ln2_b"].unsqueeze(0).broadcast_to(
                                 (128, 1024)))
                w1we = q.dma("g", w1_bf[:, :, :],
                             W["W1"].rearrange("c p d -> p c d"))
                w2we = q.dma("g", w2_bf[:, :, :],
                             w23.rearrange("c p d -> p c d"))
                b2we = q.dma("y", b2_rep[:, :],
                             wdict["b2"].unsqueeze(0).broadcast_to(
                                 (128, 1024)))
                b1we = []
                for mt in range(32):
                    b1we.append(q.dma(
                        "y", b1_sb[:, mt:mt + 1],
                        wdict["b1"][mt * 128:(mt + 1) * 128].unsqueeze(1)))

            with ExitStack() as st1:
                E1 = st1.enter_context
                hT = E1(nc.sbuf_tensor(uname("h2T"), [128, 8, 2048], BF16))
                xt = [E1(nc.sbuf_tensor(uname("xt2"), [128, 1024], F32))
                      for _ in range(2)]
                zt = E1(nc.sbuf_tensor(uname("zt3"), [128, 1024], F32))
                hbf = [E1(nc.sbuf_tensor(uname("hbf2"), [128, 1024], BF16))
                       for _ in range(2)]
                st6 = [E1(nc.sbuf_tensor(uname("st62"), [128, 12], F32))
                       for _ in range(2)]
                mv = [E1(nc.sbuf_tensor(uname("mv2"), [128, 4], F32))
                      for _ in range(2)]
                rstd = [E1(nc.sbuf_tensor(uname("rstd2"), [128, 2], F32))
                        for _ in range(2)]
                ptr = [E1(nc.psum_tensor(uname("ptr3"), [128, 1024], BF16))
                       for _ in range(2)]
                E1(nc.Block())
                eve = {}
                tre = {}
                rce = {}
                dume = None
                for i in range(33):
                    if i < 32:
                        de = q.dma("y", xt[i % 2][:, :], x1iv[i],
                                   deps=[eve.get(i - 2)])
                        rce[i] = ln_stats_comb(q, xt[i % 2][:, :], st6[i % 2],
                                               mv[i % 2], rstd[i % 2], eps, de)
                    if i > 0:
                        j = i - 1
                        half, ih = j // 16, j % 16
                        q.op("v", "tensor_scalar", zt[:, :], xt[j % 2][:, :],
                             mv[j % 2][:, 0:1], rstd[j % 2][:, 1:2],
                             OP.subtract, OP.mult, deps=[rce[j][1]])
                        q.op("v", "tensor_mul", zt[:, :], zt[:, :],
                             ln2g[:, :], deps=[l2ge] if j == 0 else ())
                        he = q.op("v", "tensor_add", hbf[j % 2][:, :],
                                  zt[:, :], ln2b[:, :],
                                  deps=[l2be] if j == 0 else ())
                        for c in range(8):
                            tre[(j, c)] = q.tr(
                                ptr[j % 2][:, c * 128:(c + 1) * 128],
                                hbf[j % 2][:, c * 128:(c + 1) * 128],
                                ident_bf[:, :],
                                deps=([he, eve.get(j - 2)] if c == 0 else ()))
                        eve[j] = q.op(
                            "v", "tensor_copy",
                            hT[:, :, ih * 128:(ih + 1) * 128],
                            ptr[j % 2][:, 0:1024].rearrange(
                                "p (c n) -> p c n", n=128),
                            deps=[tre[(j, 7)]] +
                            ([dume] if ih == 0 and dume else []))
                        if ih == 15:
                            dume = q.dma(
                                "y",
                                hT_hbm[:, :, half * 2048:(half + 1) * 2048],
                                hT[:, :, :], deps=[eve[j], dume])
                q.dma_fence()
            stLN.close()

            with ExitStack() as st1:
                E1 = st1.enter_context
                hT = E1(nc.sbuf_tensor(uname("h2T2"), [128, 8, 1024], BF16))
                h3 = E1(nc.sbuf_tensor(uname("h3"), [128, 32, 512], BF16))
                x2_t = [E1(nc.sbuf_tensor(uname("x2_t"), [128, 1024], F32))
                        for _ in range(1)]
                x1i_t2 = [E1(nc.sbuf_tensor(uname("x1i_t2"), [128, 1024], F32))
                          for _ in range(1)]
                out_t = [E1(nc.sbuf_tensor(uname("out_t2"), [128, 1024], F32))
                         for _ in range(1)]
                ps1 = [E1(nc.psum_tensor(uname("ps1b"), [128, 512], F32))
                       for _ in range(2)]
                ps2 = [E1(nc.psum_tensor(uname("ps2b"), [128, 512], F32))
                       for _ in range(4)]
                E1(nc.Block())
                k1 = 0
                k2 = 0
                gele = {}
                w2le = {}
                combe = {}
                dmae = {}
                hle = None
                for half in range(4):
                    hle = q.dma("y", hT[:, :, :],
                                hT_hbm[:, :, half * 1024:(half + 1) * 1024],
                                deps=[w2le.get(k2 - 1), hle])
                    for sl in range(2):
                        tok0 = half * 1024 + sl * 512
                        slab_gel = None
                        for mt in range(32):
                            dep = [gele.get(k1 - 2)]
                            if mt == 0:
                                dep.append(hle)
                                dep.append(w2le.get(k2 - 4))
                                if k1 == 0:
                                    dep.append(w1we)
                            ge_ = q.mm_group([
                                (ps1[k1 % 2][:, :],
                                 w1_bf[:, c, mt * 128:(mt + 1) * 128],
                                 hT[:, c, sl * 512:(sl + 1) * 512])
                                for c in range(8)], deps=dep)
                            gele[k1] = q.op("s", "activation", h3[:, mt, :],
                                            ps1[k1 % 2][:, :], AF.Gelu,
                                            bias=b1_sb[:, mt:mt + 1],
                                            deps=[ge_] + (b1we if k1 == 0
                                                          else []))
                            slab_gel = gele[k1]
                            k1 += 1
                        for ti in range(4):
                            gi = (tok0 // 128) + ti
                            xde = q.dma("y", x2_t[0][:, :], x3v[gi],
                                        deps=[combe.get((gi - 1, 1))])
                            xie = q.dma("y", x1i_t2[0][:, :], x1iv[gi],
                                        deps=[combe.get((gi - 1, 1))])
                            for ds in range(2):
                                mms = [
                                    (ps2[k2 % 4][:, :],
                                     h3[:, fc, ti * 128:(ti + 1) * 128],
                                     w2_bf[:, fc, ds * 512:(ds + 1) * 512])
                                    for fc in range(32)]
                                dep = [combe.get((gi - 2, ds))]
                                if ti == 0 and ds == 0:
                                    dep.append(slab_gel)
                                    if k2 == 0:
                                        dep.append(w2we)
                                w2le[k2] = q.mm_group(mms, deps=dep)
                                c1 = q.op("v", "scalar_tensor_tensor",
                                          out_t[0][:, ds * 512:(ds + 1) * 512],
                                          ps2[k2 % 4][:, :], 1.0,
                                          x2_t[0][:, ds * 512:(ds + 1) * 512],
                                          OP.mult, OP.add,
                                          deps=[w2le[k2], xde, xie,
                                                dmae.get(gi - 1)] +
                                          ([b2we] if k2 == 0 else []))
                                c2 = q.op("v", "scalar_tensor_tensor",
                                          out_t[0][:, ds * 512:(ds + 1) * 512],
                                          b2_rep[:, ds * 512:(ds + 1) * 512],
                                          1.0,
                                          out_t[0][:, ds * 512:(ds + 1) * 512],
                                          OP.mult, OP.add)
                                combe[(gi, ds)] = q.op(
                                    "v", "tensor_add",
                                    out_t[0][:, ds * 512:(ds + 1) * 512],
                                    out_t[0][:, ds * 512:(ds + 1) * 512],
                                    x1i_t2[0][:, ds * 512:(ds + 1) * 512])
                                k2 += 1
                            dmae[gi] = q.dma("y", xov[gi],
                                             out_t[0][:, :],
                                             deps=[combe[(gi, 1)]])
                q.dma_fence()
    return nc


def build_l1b(nc, q, x3, router_w, w1, b1, w2, b2, out, consts,
              hT_hbm, w1bf_hbm, w2bf_hbm, crow_hbm, stage_ev=None):
    ident_bf, ident_f32, ones_bf, ones_f32, cpool = consts
    eps = cpool[:, 0:1]
    NT = 32
    E_ = 8

    x3v = x3.rearrange("(i p) d -> i p d", p=128)
    outv = out.rearrange("(i p) d -> i p d", p=128)

    with ExitStack() as st0:
        E = st0.enter_context
        rwT_bf = E(nc.sbuf_tensor(uname("rwT_bf"), [128, 8, 8], BF16))
        c_sb = E(nc.sbuf_tensor(uname("c_sb"), [128, 32, 8], F32))
        c8 = E(nc.sbuf_tensor(uname("c8"), [128, 8], F32))
        b2all_bf = E(nc.sbuf_tensor(uname("b2all_bf"), [8, 1024], BF16))

        if stage_ev is None:
            stage_ev = {}
        with ExitStack() as st1:
            E1 = st1.enter_context
            rw_sb = E1(nc.sbuf_tensor(uname("rw_sb"), [8, 1024], F32))
            prt = E1(nc.psum_tensor(uname("prt"), [128, 8], F32))
            E1(nc.Block())
            if not stage_ev:
                for e in range(E_):
                    e1 = q.dma("g", w1bf_hbm[e], w1[e])
                    e2 = q.dma("g", w2bf_hbm[e], w2[e])
                    stage_ev[e] = (e1, e2)
            rde = q.dma("y", rw_sb[:, :], router_w)
            pv = None
            for c in range(8):
                t_ = q.tr(prt[:, :], rw_sb[:, c * 128:(c + 1) * 128],
                          ident_f32[0:8, 0:8], deps=[rde, pv])
                pv = q.op("s", "activation", rwT_bf[:, c, :], prt[:, :],
                          AF.Identity, deps=[t_])

        with ExitStack() as st1:
            E1 = st1.enter_context
            cT_sb = E1(nc.sbuf_tensor(uname("cT_sb"), [8, 4096], BF16))
            hT = E1(nc.sbuf_tensor(uname("hTq"), [128, 8, 2048], BF16))
            oinit = [E1(nc.sbuf_tensor(uname("oinit"), [128, 1024], F32))
                     for _ in range(2)]
            xt = [E1(nc.sbuf_tensor(uname("xt3"), [128, 1024], F32))
                  for _ in range(2)]
            zt = E1(nc.sbuf_tensor(uname("zt4"), [128, 1024], F32))
            hbf = E1(nc.sbuf_tensor(uname("hbf3"), [128, 1024], BF16))
            st6 = E1(nc.sbuf_tensor(uname("st63"), [128, 12], F32))
            mv = E1(nc.sbuf_tensor(uname("mv3"), [128, 4], F32))
            rstd = E1(nc.sbuf_tensor(uname("rstd3"), [128, 2], F32))
            lgt = E1(nc.sbuf_tensor(uname("lgt"), [128, 8], F32))
            mx8 = E1(nc.sbuf_tensor(uname("mx8"), [128, 8], F32))
            sc8 = E1(nc.sbuf_tensor(uname("sc8"), [128, 16], F32))
            junk = E1(nc.sbuf_tensor(uname("junk"), [128, 8], F32))
            ptr = [E1(nc.psum_tensor(uname("ptr4"), [128, 1024], BF16))
                   for _ in range(2)]
            plg = E1(nc.psum_tensor(uname("plg"), [128, 8], F32))
            pcr = [E1(nc.psum_tensor(uname("pcr"), [8, 128], F32))
                   for _ in range(2)]
            pini = E1(nc.psum_tensor(uname("pini"), [128, 1024], F32))
            E1(nc.Block())
            b2de = q.dma("g", b2all_bf[:, :], b2)
            lne = {}
            oevi = {}
            sse = {}
            eve = {}
            tre = {}
            oie = {}
            lge = {}
            tope = {}
            cre = {}
            dume = None
            for i in range(NT):
                half, ih = i // 16, i % 16
                de = q.dma("y", xt[i % 2][:, :], x3v[i],
                           deps=[lne.get(i - 2), oevi.get(i - 2)])
                r1e, _ = ln_stats_comb(q, xt[i % 2][:, :], st6, mv, rstd,
                                       zt[:, :][0:128, 0:1] if False else eps,
                                       de) if False else ln_stats_comb(
                                           q, xt[i % 2][:, :], st6, mv, rstd,
                                           eps, de)
                cpe = q.op("v", "tensor_scalar", hbf[:, :], xt[i % 2][:, :],
                           mv[:, 0:1], rstd[:, 0:1], OP.subtract, OP.mult,
                           deps=[r1e])
                lne[i] = cpe
                for c in range(8):
                    tre[(i, c)] = q.tr(
                        ptr[i % 2][:, c * 128:(c + 1) * 128],
                        hbf[:, c * 128:(c + 1) * 128], ident_bf[:, :],
                        deps=([cpe, eve.get(i - 2)] if c == 0 else ()))
                eve[i] = q.op(
                    "v", "tensor_copy", hT[:, :, ih * 128:(ih + 1) * 128],
                    ptr[i % 2][:, 0:1024].rearrange("p (c n) -> p c n", n=128),
                    deps=[tre[(i, 7)]] + ([dume] if ih == 0 and dume else []))
                lge[i] = q.mm_group([
                    (plg[:, :], hT[:, c, ih * 128:(ih + 1) * 128],
                     rwT_bf[:, c, :]) for c in range(8)],
                    deps=[eve[i], tope.get(i - 1)])
                le = q.op("s", "activation", lgt[:, :], plg[:, :], AF.Identity,
                          deps=[lge[i], tope.get(i - 1)])
                q.op("v", "max", mx8[:, :], lgt[:, :], deps=[le])
                q.op("v", "tensor_scalar", sc8[:, 0:1], mx8[:, 0:1], -1.0,
                     None, OP.mult)
                ze = q.op("s", "activation", junk[:, :], lgt[:, :], AF.Exp,
                          bias=sc8[:, 0:1], accum_out=sc8[:, 1:2],
                          deps=[(q.ch["v"][0], q.ch["v"][1])])
                de2 = q.op("v", "tensor_sub", sc8[:, 2:3], mx8[:, 1:2],
                           mx8[:, 0:1])
                e2e = q.op("s", "activation", sc8[:, 3:4], sc8[:, 2:3],
                           AF.Exp, deps=[de2])
                q.op("v", "reciprocal", sc8[:, 4:5], sc8[:, 1:2], deps=[ze])
                q.op("v", "tensor_mul", sc8[:, 5:6], sc8[:, 3:4], sc8[:, 4:5],
                     deps=[e2e])
                q.op("v", "tensor_add", sc8[:, 6:7], sc8[:, 4:5], sc8[:, 5:6])
                q.op("v", "tensor_scalar", sc8[:, 6:7], sc8[:, 6:7], 1e-9,
                     None, OP.add)
                q.op("v", "reciprocal", sc8[:, 7:8], sc8[:, 6:7])
                q.op("v", "tensor_mul", sc8[:, 8:9], sc8[:, 4:5], sc8[:, 7:8])
                q.op("v", "tensor_mul", sc8[:, 9:10], sc8[:, 5:6], sc8[:, 7:8])
                q.op("v", "tensor_scalar", c8[:, :], lgt[:, :], mx8[:, 0:1],
                     None, OP.is_equal)
                q.op("v", "tensor_scalar", c8[:, :], c8[:, :], sc8[:, 8:9],
                     None, OP.mult)
                q.op("v", "tensor_scalar", junk[:, :], lgt[:, :], mx8[:, 1:2],
                     None, OP.is_equal)
                q.op("v", "tensor_scalar", junk[:, :], junk[:, :],
                     sc8[:, 9:10], None, OP.mult)
                tope[i] = q.op("v", "tensor_add", c_sb[:, i, :], c8[:, :],
                               junk[:, :])
                ct = q.tr(pcr[i % 2][:, :], c_sb[:, i, :], ident_f32[:, :],
                          deps=[tope[i], cre.get(i - 2)])
                cre[i] = q.op("s", "activation",
                              cT_sb[:, i * 128:(i + 1) * 128],
                              pcr[i % 2][0:8, :], AF.Identity, deps=[ct])
                ibe = q.mm_group([(pini[:, 0:512],
                                   cT_sb[:, i * 128:(i + 1) * 128],
                                   b2all_bf[:, 0:512])],
                                 deps=[cre[i], oevi.get(i - 2),
                                       b2de if i == 0 else None])
                ibe2 = q.mm_group([(pini[:, 512:1024],
                                    cT_sb[:, i * 128:(i + 1) * 128],
                                    b2all_bf[:, 512:1024])])
                ov = q.op("v", "tensor_add", oinit[i % 2][:, :],
                          pini[:, :].rearrange("p n -> p n"), xt[i % 2][:, :],
                          deps=[ibe2, oie.get(i - 2)])
                oevi[i] = ov
                oie[i] = q.dma("y", outv[i], oinit[i % 2][:, :], deps=[ov])
                if ih == 15:
                    dume = q.dma("y",
                                 hT_hbm[:, :, half * 2048:(half + 1) * 2048],
                                 hT[:, :, :], deps=[eve[i], dume])
            q.dma_fence()

        with ExitStack() as st1:
            E1 = st1.enter_context
            w1T = E1(nc.sbuf_tensor(uname("w1T"), [128, 8, 4096], BF16))
            w2T = E1(nc.sbuf_tensor(uname("w2T"), [128, 32, 1024], BF16))
            hT = [E1(nc.sbuf_tensor(uname("hTq2"), [128, 8, 1024], BF16))
                  for _ in range(2)]
            h2cT = E1(nc.sbuf_tensor(uname("h2cT"), [128, 32, 512], BF16))
            b1c = E1(nc.sbuf_tensor(uname("b1c"), [128, 32], F32))
            out_sb = [E1(nc.sbuf_tensor(uname("out_sb"), [128, 1024], F32))
                      for _ in range(2)]
            ps1 = [E1(nc.psum_tensor(uname("ps1c"), [128, 512], F32))
                   for _ in range(3)]
            ps2 = [E1(nc.psum_tensor(uname("ps2c"), [128, 512], F32))
                   for _ in range(4)]
            E1(nc.Block())
            k1 = 0
            k2 = 0
            rele = {}
            w2le = {}
            oevac = {}
            acc_ev = {}
            prev_acc = None
            hT_free = {}
            hle = None
            w1_free = None
            w2_free = None
            misc_free = None
            for e in range(E_):
                wde = []
                for cc in range(8):
                    wde.append(q.dma("y", w1T[:, cc, :],
                                     w1bf_hbm[e][:, cc * 128:(cc + 1) * 128],
                                     transpose=True,
                                     deps=[stage_ev[e][0], w1_free]
                                     if cc == 0 else [stage_ev[e][0]]))
                w2de = []
                for fc in range(32):
                    w2de.append(q.dma("y", w2T[:, fc, :],
                                      w2bf_hbm[e][:, fc * 128:(fc + 1) * 128],
                                      transpose=True,
                                      deps=[stage_ev[e][1], w2_free]
                                      if fc == 0 else [stage_ev[e][1]]))
                b1de = []
                for mt in range(32):
                    b1de.append(q.dma(
                        "y", b1c[:, mt:mt + 1],
                        b1[e][mt * 128:(mt + 1) * 128].unsqueeze(1),
                        deps=[misc_free] if mt == 0 else ()))
                for qtr in range(4):
                    gq = e * 4 + qtr
                    hle_prev = hle
                    hle = q.dma("y", hT[gq % 2][:, :, :],
                                hT_hbm[:, :, qtr * 1024:(qtr + 1) * 1024],
                                deps=[hT_free.get(gq - 2), hle_prev])
                    if gq == 0:
                        hcur = hle
                    for sl in range(2):
                        tok0 = qtr * 1024 + sl * 512
                        sloc = sl * 512
                        for mt in range(32):
                            dep = [rele.get(k1 - 3)]
                            if mt == 0:
                                dep.append(wde)
                                dep.append(hle)
                                dep.append(b1de)
                                dep.append(w2le.get(k2 - 4))
                            mms = [
                                (ps1[k1 % 3][:, :],
                                 w1T[:, c, mt * 128:(mt + 1) * 128],
                                 hT[gq % 2][:, c, sloc:sloc + 512])
                                for c in range(8)]
                            ge_ = q.mm_group(mms, deps=dep)
                            rele[k1] = q.op("s", "activation", h2cT[:, mt, :],
                                            ps1[k1 % 3][:, :], AF.Relu,
                                            bias=b1c[:, mt:mt + 1],
                                            deps=[ge_])
                            last_rel = rele[k1]
                            k1 += 1
                        w1_free = ge_
                        if sl == 1:
                            hT_free[gq] = ge_
                        for ti in range(4):
                            gi = (tok0 // 128) + ti
                            for ds in range(2):
                                mms = [
                                    (ps2[k2 % 4][:, :],
                                     h2cT[:, fc, ti * 128:(ti + 1) * 128],
                                     w2T[:, fc, ds * 512:(ds + 1) * 512])
                                    for fc in range(32)]
                                dep = [oevac.get(k2 - 4)]
                                if ti == 0 and ds == 0:
                                    dep.append(last_rel)
                                    dep.append(w2de)
                                w2le[k2] = q.mm_group(mms, deps=dep)
                                oevac[k2] = q.op(
                                    "s", "activation",
                                    out_sb[gi % 2][:, ds * 512:(ds + 1) * 512],
                                    ps2[k2 % 4][:, :], AF.Identity,
                                    scale=c_sb[:, gi, e:e + 1],
                                    deps=[w2le[k2], prev_acc])
                                k2 += 1
                            prev_acc = q.dma(
                                "g", outv[gi], out_sb[gi % 2][:, :],
                                accum_op=OP.add,
                                deps=[oevac[k2 - 1], acc_ev.get(gi)])
                            acc_ev[gi] = prev_acc
                        w2_free = w2le[k2 - 1]
                        misc_free = w2le[k2 - 1]
            q.dma_fence()
    return nc


from concourse.bass_utils import run_bass_kernel_spmd

N_CORES = 8
TOK = 4096


def _consts_alloc(nc, q):
    st = []
    ident_bf = nc.sbuf_tensor("ident_bf", [128, 128], BF16); st.append(ident_bf)
    ident_f32 = nc.sbuf_tensor("ident_f32", [128, 128], F32); st.append(ident_f32)
    ones_bf = nc.sbuf_tensor("ones_bf", [1, 128], BF16); st.append(ones_bf)
    ones_f32 = nc.sbuf_tensor("ones_f32", [128, 128], F32); st.append(ones_f32)
    cpool = nc.sbuf_tensor("cpool", [128, 4], F32); st.append(cpool)
    objs = [t.__enter__() for t in st]
    with nc.Block() as _b:
        emit_constants(q, *objs)
    return tuple(objs)


def _final_fence(nc, q):
    with nc.Block() as _b:
        q.dma_fence()


def _prog_l0a():
    nc = bass.Bass("TRN2", target_bir_lowering=False)
    X = nc.dram_tensor("x", [TOK, 1024], F32, kind="ExternalInput").ap()
    WQ = nc.dram_tensor("Wq", [1024, 1024], F32, kind="ExternalInput").ap()
    BQ = nc.dram_tensor("bq", [1024], F32, kind="ExternalInput").ap()
    WK = nc.dram_tensor("Wk", [1024, 1024], F32, kind="ExternalInput").ap()
    BK = nc.dram_tensor("bk", [1024], F32, kind="ExternalInput").ap()
    WV = nc.dram_tensor("Wv", [1024, 8192], F32, kind="ExternalInput").ap()
    BV = nc.dram_tensor("bv", [8192], F32, kind="ExternalInput").ap()
    GM = nc.dram_tensor("gamma", [1024], F32, kind="ExternalInput").ap()
    BT = nc.dram_tensor("beta", [1024], F32, kind="ExternalInput").ap()
    ATT = nc.dram_tensor("attn", [TOK, 1024], F32, kind="ExternalOutput").ap()
    q = Par(nc)
    consts = _consts_alloc(nc, q)
    build_l0a(nc, q, X, WQ, BQ, WK, BK, WV, BV, GM, BT, ATT, consts)
    _final_fence(nc, q)
    return nc


def _prog_l0b():
    nc = bass.Bass("TRN2", target_bir_lowering=False)
    X = nc.dram_tensor("x", [TOK, 1024], F32, kind="ExternalInput").ap()
    ATT = nc.dram_tensor("attn", [TOK, 1024], F32, kind="ExternalInput").ap()
    WR = nc.dram_tensor("wr", [1024], F32, kind="ExternalInput").ap()
    W1 = nc.dram_tensor("W1", [1024, 4096], F32, kind="ExternalInput").ap()
    B1 = nc.dram_tensor("b1", [4096], F32, kind="ExternalInput").ap()
    W2 = nc.dram_tensor("W2", [4096, 1024], F32, kind="ExternalInput").ap()
    B2 = nc.dram_tensor("b2", [1024], F32, kind="ExternalInput").ap()
    X2 = nc.dram_tensor("x2", [TOK, 1024], F32, kind="ExternalOutput").ap()
    X1H = nc.dram_tensor("x1h", [TOK, 1024], F32, kind="Internal").ap()
    HTH = nc.dram_tensor("hth", [128, 8, TOK], BF16, kind="Internal").ap()
    ROW = nc.dram_tensor("rowh", [TOK], F32, kind="Internal").ap()
    q = Par(nc)
    consts = _consts_alloc(nc, q)
    build_l0b(nc, q, X, ATT, WR, W1, B1, W2, B2, X2, consts, X1H, HTH, ROW)
    _final_fence(nc, q)
    return nc


_L1A_NAMES = ["ln1_g", "ln1_b", "WQ", "bQ", "WK", "bK", "WV", "bV", "WO",
              "bO", "Wg", "bg", "A", "Bw", "Cw", "ln2_g", "ln2_b", "W1",
              "b1", "W2", "b2"]
_L1A_SHAPES = {"ln1_g": [1024], "ln1_b": [1024], "WQ": [1024, 1024],
               "bQ": [1024], "WK": [1024, 1024], "bK": [1024],
               "WV": [1024, 1024], "bV": [1024], "WO": [1024, 1024],
               "bO": [1024], "Wg": [1024, 1024], "bg": [1024], "A": [64],
               "Bw": [1024, 64], "Cw": [64, 1024], "ln2_g": [1024],
               "ln2_b": [1024], "W1": [1024, 4096], "b1": [4096],
               "W2": [4096, 1024], "b2": [1024]}


def _prog_l1a():
    nc = bass.Bass("TRN2", target_bir_lowering=False)
    X2 = nc.dram_tensor("x2", [TOK, 1024], F32, kind="ExternalInput").ap()
    dram = {n: nc.dram_tensor(n, _L1A_SHAPES[n], F32,
                              kind="ExternalInput").ap() for n in _L1A_NAMES}
    X3 = nc.dram_tensor("x3", [TOK, 1024], F32, kind="ExternalOutput").ap()
    ZH = nc.dram_tensor("zh", [TOK, 1024], F32, kind="Internal").ap()
    GH = nc.dram_tensor("gh", [TOK, 1024], BF16, kind="Internal").ap()
    AOH = nc.dram_tensor("aoh", [TOK, 1024], BF16, kind="Internal").ap()
    X1IH = nc.dram_tensor("x1ih", [TOK, 1024], F32, kind="Internal").ap()
    HTH = nc.dram_tensor("hth", [128, 8, TOK], BF16, kind="Internal").ap()
    q = Par(nc)
    consts = _consts_alloc(nc, q)
    build_l1a(nc, q, X2, dram, X3, consts, ZH, GH, AOH, X1IH, HTH)
    _final_fence(nc, q)
    return nc


def _prog_l1b():
    nc = bass.Bass("TRN2", target_bir_lowering=False)
    X3 = nc.dram_tensor("x3", [TOK, 1024], F32, kind="ExternalInput").ap()
    RW = nc.dram_tensor("rw", [8, 1024], F32, kind="ExternalInput").ap()
    W1 = nc.dram_tensor("w1", [8, 4096, 1024], F32, kind="ExternalInput").ap()
    B1 = nc.dram_tensor("b1", [8, 4096], F32, kind="ExternalInput").ap()
    W2 = nc.dram_tensor("w2", [8, 1024, 4096], F32, kind="ExternalInput").ap()
    B2 = nc.dram_tensor("b2", [8, 1024], F32, kind="ExternalInput").ap()
    OUT = nc.dram_tensor("out", [TOK, 1024], F32, kind="ExternalOutput").ap()
    HTH = nc.dram_tensor("hth", [128, 8, TOK], BF16, kind="Internal").ap()
    W1BF = nc.dram_tensor("w1bf", [8, 4096, 1024], BF16, kind="Internal").ap()
    W2BF = nc.dram_tensor("w2bf", [8, 1024, 4096], BF16, kind="Internal").ap()
    CROW = nc.dram_tensor("crow", [8, TOK], BF16, kind="Internal").ap()
    q = Par(nc)
    consts = _consts_alloc(nc, q)
    build_l1b(nc, q, X3, RW, W1, B1, W2, B2, OUT, consts,
              HTH, W1BF, W2BF, CROW)
    _final_fence(nc, q)
    return nc


def _prog_fused():
    nc = bass.Bass("TRN2", target_bir_lowering=False)
    I = lambda name, shape: nc.dram_tensor(name, shape, F32,
                                           kind="ExternalInput").ap()
    X = I("x", [TOK, 1024])
    WQ0, BQ0 = I("l0Wq", [1024, 1024]), I("l0bq", [1024])
    WK0, BK0 = I("l0Wk", [1024, 1024]), I("l0bk", [1024])
    WV0, BV0 = I("l0Wv", [1024, 8192]), I("l0bv", [8192])
    GM0, BT0 = I("l0gamma", [1024]), I("l0beta", [1024])
    WRm, W1m, B1m = I("mwr", [1024]), I("mW1", [1024, 4096]), I("mb1", [4096])
    W2m, B2m = I("mW2", [4096, 1024]), I("mb2", [1024])
    dram1 = {n: I("a_" + n, _L1A_SHAPES[n]) for n in _L1A_NAMES}
    RW = I("erw", [8, 1024])
    W1e, B1e = I("ew1", [8, 4096, 1024]), I("eb1", [8, 4096])
    W2e, B2e = I("ew2", [8, 1024, 4096]), I("eb2", [8, 1024])
    OUT = nc.dram_tensor("out", [TOK, 1024], F32, kind="ExternalOutput").ap()

    S = lambda name, shape, dt=F32: nc.dram_tensor(name, shape, dt,
                                                   kind="Internal").ap()
    ATT = S("s_att", [TOK, 1024])
    X2 = S("s_x2", [TOK, 1024])
    X3 = S("s_x3", [TOK, 1024])
    X1H = S("s_x1h", [TOK, 1024])
    HTH = S("s_hth", [128, 8, TOK], BF16)
    ROW = S("s_row", [TOK])
    ZH = S("s_zh", [TOK, 1024])
    GH = S("s_gh", [TOK, 1024], BF16)
    AOH = S("s_aoh", [TOK, 1024], BF16)
    X1IH = S("s_x1ih", [TOK, 1024])
    W1BF = S("s_w1bf", [8, 4096, 1024], BF16)
    W2BF = S("s_w2bf", [8, 1024, 4096], BF16)
    CROW = S("s_crow", [8, TOK], BF16)

    q = Par(nc)
    consts = _consts_alloc(nc, q)
    stage_ev = {}
    with nc.Block() as _bstg:
        for e in range(8):
            e1 = q.dma("g", W1BF[e], W1e[e], pool="stg")
            e2 = q.dma("g", W2BF[e], W2e[e], pool="stg")
            stage_ev[e] = (e1, e2)
    build_l0a(nc, q, X, WQ0, BQ0, WK0, BK0, WV0, BV0, GM0, BT0, ATT, consts)
    build_l0b(nc, q, X, ATT, WRm, W1m, B1m, W2m, B2m, X2, consts,
              X1H, HTH, ROW)
    build_l1a(nc, q, X2, dram1, X3, consts, ZH, GH, AOH, X1IH, HTH)
    build_l1b(nc, q, X3, RW, W1e, B1e, W2e, B2e, OUT, consts,
              HTH, W1BF, W2BF, CROW, stage_ev=stage_ev)
    _final_fence(nc, q)
    return nc


FUSED = True


def _kernel_fused(inp):
    x = inp["L0_attn_x"].reshape(N_CORES, TOK, 1024)
    nc = _prog_fused()
    base = {"l0Wq": inp["L0_attn_Wq"], "l0bq": inp["L0_attn_bq"],
            "l0Wk": inp["L0_attn_Wk"], "l0bk": inp["L0_attn_bk"],
            "l0Wv": inp["L0_attn_Wv"], "l0bv": inp["L0_attn_bv"],
            "l0gamma": inp["L0_attn_gamma"], "l0beta": inp["L0_attn_beta"],
            "mwr": inp["L0_ffn_w_router"], "mW1": inp["L0_ffn_W1"],
            "mb1": inp["L0_ffn_b1"], "mW2": inp["L0_ffn_W2"],
            "mb2": inp["L0_ffn_b2"],
            "erw": inp["L1_ffn_router_w"], "ew1": inp["L1_ffn_w1"],
            "eb1": inp["L1_ffn_b1"], "ew2": inp["L1_ffn_w2"],
            "eb2": inp["L1_ffn_b2"]}
    base.update({"a_" + n: inp[f"L1_attn_{n}"] for n in _L1A_NAMES})
    maps = [dict(x=x[c], **base) for c in range(N_CORES)]
    res = run_bass_kernel_spmd(nc, maps, core_ids=list(range(N_CORES)))
    out = np.stack([r["out"] for r in res.results])
    return out.reshape(16, 2048, 1024).astype(np.float32)


def kernel(**inputs):
    inp = {k: np.ascontiguousarray(v) for k, v in inputs.items()}
    if FUSED:
        return _kernel_fused(inp)
    x = inp["L0_attn_x"].reshape(N_CORES, TOK, 1024)

    nc = _prog_l0a()
    maps = [{"x": x[c], "Wq": inp["L0_attn_Wq"], "bq": inp["L0_attn_bq"],
             "Wk": inp["L0_attn_Wk"], "bk": inp["L0_attn_bk"],
             "Wv": inp["L0_attn_Wv"], "bv": inp["L0_attn_bv"],
             "gamma": inp["L0_attn_gamma"], "beta": inp["L0_attn_beta"]}
            for c in range(N_CORES)]
    res = run_bass_kernel_spmd(nc, maps, core_ids=list(range(N_CORES)))
    attn = [r["attn"] for r in res.results]

    nc = _prog_l0b()
    maps = [{"x": x[c], "attn": attn[c], "wr": inp["L0_ffn_w_router"],
             "W1": inp["L0_ffn_W1"], "b1": inp["L0_ffn_b1"],
             "W2": inp["L0_ffn_W2"], "b2": inp["L0_ffn_b2"]}
            for c in range(N_CORES)]
    res = run_bass_kernel_spmd(nc, maps, core_ids=list(range(N_CORES)))
    x2 = [r["x2"] for r in res.results]

    nc = _prog_l1a()
    wmap = {n: inp[f"L1_attn_{n}"] for n in _L1A_NAMES}
    maps = [dict(x2=x2[c], **wmap) for c in range(N_CORES)]
    res = run_bass_kernel_spmd(nc, maps, core_ids=list(range(N_CORES)))
    x3 = [r["x3"] for r in res.results]

    nc = _prog_l1b()
    maps = [{"x3": x3[c], "rw": inp["L1_ffn_router_w"],
             "w1": inp["L1_ffn_w1"], "b1": inp["L1_ffn_b1"],
             "w2": inp["L1_ffn_w2"], "b2": inp["L1_ffn_b2"]}
            for c in range(N_CORES)]
    res = run_bass_kernel_spmd(nc, maps, core_ids=list(range(N_CORES)))
    out = np.stack([r["out"] for r in res.results])
    return out.reshape(16, 2048, 1024).astype(np.float32)


# version 22
# speedup vs baseline: 1.0413x; 1.0413x over previous
import numpy as np
import concourse.bass as bass
import concourse.mybir as mybir
from contextlib import ExitStack

F32 = mybir.dt.float32
BF16 = mybir.dt.bfloat16
I32 = mybir.dt.int32
AF = mybir.ActivationFunctionType
OP = mybir.AluOpType
AX = mybir.AxisListType

LN_EPS = 1e-5
ISQ128 = 0.08838834764831845
ISQ64 = 0.125
NEG_BIG = -1e30

_uid = [0]


def uname(s):
    _uid[0] += 1
    return f"{s}_u{_uid[0]}"




def ln_stats_comb(q, x_ap, st6, mv, rstd, eps, dep_in, war=None):
    q.op("v", "bn_stats", st6[:, 0:6], x_ap[:, 0:512], deps=[dep_in])
    q.op("v", "bn_stats", st6[:, 6:12], x_ap[:, 512:1024])
    ag = q.op("v", "bn_aggr", mv[:, 0:2], st6[:, 0:12], deps=[war])
    q.op("s", "activation", mv[:, 2:3], mv[:, 1:2], AF.Ln, bias=eps,
         deps=[ag])
    r1 = q.op("s", "activation", rstd[:, 0:1], mv[:, 2:3], AF.Exp, scale=-0.5)
    q.op("v", "tensor_mul", mv[:, 3:4], rstd[:, 0:1], rstd[:, 0:1], deps=[r1])
    q.op("v", "tensor_mul", mv[:, 3:4], mv[:, 3:4], mv[:, 1:2])
    q.op("s", "activation", mv[:, 3:4], mv[:, 3:4], AF.Ln, bias=eps,
         deps=[(q.ch["v"][0], q.ch["v"][1])])
    q.op("s", "activation", mv[:, 3:4], mv[:, 3:4], AF.Exp, scale=-0.5)
    rc = q.op("v", "tensor_mul", rstd[:, 1:2], rstd[:, 0:1], mv[:, 3:4],
              deps=[(q.ch["s"][0], q.ch["s"][1])])
    return r1, rc


class Par:

    def __init__(self, nc):
        self.nc = nc
        self.cm = []
        def sem(name):
            cm = nc.semaphore(name)
            self.cm.append(cm)
            return [cm.__enter__(), 0]
        self.ch = {k: sem("ch_" + k) for k in "tvsg"}
        self.dpool = {"y": [sem(f"dqy{i}") for i in range(10)],
                      "g": [sem(f"dqg{i}") for i in range(6)],
                      "s": [sem(f"dqs{i}") for i in range(6)],
                      "stg": [sem(f"dqstg{i}") for i in range(2)]}
        self.didx = {"y": 0, "g": 0, "s": 0, "stg": 0}
        self.eng = dict(t=nc.tensor, v=nc.vector, s=nc.scalar,
                        g=nc.gpsimd, y=nc.sync)

    def _waits(self, e, deps):
        best = {}
        for d in deps:
            if d is None:
                continue
            if isinstance(d, list):
                for dd in d:
                    if dd is not None:
                        k = id(dd[0])
                        if k not in best or dd[1] > best[k][1]:
                            best[k] = dd
                continue
            k = id(d[0])
            if k not in best or d[1] > best[k][1]:
                best[k] = d
        for d in best.values():
            e.wait_ge(d[0], d[1])

    def op(self, which, name, *a, deps=(), **k):
        e = self.eng[which]
        c = self.ch[which]
        if which != "t":
            e.wait_ge(c[0], c[1])
        self._waits(e, deps)
        ins = getattr(e, name)(*a, **k)
        ins.then_inc(c[0], 1)
        c[1] += 1
        return (c[0], c[1])

    def mm(self, out, lhsT, rhs, start, stop, deps=()):
        e = self.eng["t"]
        self._waits(e, deps)
        ins = e.matmul(out, lhsT, rhs, start=start, stop=stop)
        c = self.ch["t"]
        ins.then_inc(c[0], 1)
        c[1] += 1
        return (c[0], c[1])

    def mm_group(self, mms, deps=()):
        ev = None
        for i, (out, lhsT, rhs) in enumerate(mms):
            ev = self.mm(out, lhsT, rhs, start=(i == 0),
                         stop=(i == len(mms) - 1),
                         deps=deps if i == 0 else ())
        return ev

    def tr(self, out, in_, ident, deps=()):
        e = self.eng["t"]
        self._waits(e, deps)
        ins = e.transpose(out, in_, ident)
        c = self.ch["t"]
        ins.then_inc(c[0], 1)
        c[1] += 1
        return (c[0], c[1])

    def dma(self, which, out, in_, deps=(), pool=None, **k):
        e = self.eng[which]
        qn = pool or ("g" if which == "g" else
                      ("s" if which == "s" else "y"))
        pool = self.dpool[qn]
        d = pool[self.didx[qn] % len(pool)]
        self.didx[qn] += 1
        if d[1] > 0:
            e.wait_ge(d[0], d[1])
        self._waits(e, deps)
        ins = e.dma_start(out=out, in_=in_, **k)
        ins.then_inc(d[0], 16)
        d[1] += 16
        return (d[0], d[1])

    def dma_fence(self):
        e = self.eng["y"]
        for qn, pool in self.dpool.items():
            for d in pool:
                if d[1] > 0:
                    e.wait_ge(d[0], d[1])


def emit_constants(q, ident_bf, ident_f32, ones_bf, ones_f32, cpool):
    e1 = q.op("g", "memset", ones_f32[:, :], 1.0)
    q.op("g", "memset", ones_bf[:, :], 1.0)
    q.op("g", "memset", cpool[:, 0:1], LN_EPS)
    q.op("g", "affine_select", ident_f32[:, :], ones_f32[:, :],
         pattern=[[-1, 128]], compare_op=OP.is_equal, fill=0.0,
         base=0, channel_multiplier=1, deps=[e1])
    q.op("v", "tensor_copy", ident_bf[:, :], ident_f32[:, :],
         deps=[(q.ch["g"][0], q.ch["g"][1])])


def ln_tile(q, x_ap, st6, mv, rstd, z_f32, eps, dep_in, affine=None,
            out_bf=None, war=None):
    q.op("v", "bn_stats", st6[:, 0:6], x_ap[:, 0:512], deps=[dep_in])
    q.op("v", "bn_stats", st6[:, 6:12], x_ap[:, 512:1024])
    ag = q.op("v", "bn_aggr", mv[:, 0:2], st6[:, 0:12], deps=[war])
    q.op("s", "activation", mv[:, 1:2], mv[:, 1:2], AF.Ln, bias=eps,
         deps=[ag])
    r1 = q.op("s", "activation", rstd[:, 0:1], mv[:, 1:2], AF.Exp, scale=-0.5)
    sev = r1
    ev = q.op("v", "tensor_scalar", z_f32, x_ap, mv[:, 0:1], rstd[:, 0:1],
              OP.subtract, OP.mult, deps=[r1])
    if affine is not None:
        g_rep, b_rep = affine
        q.op("v", "bn_stats", st6[:, 0:6], z_f32[:, 0:512])
        q.op("v", "bn_stats", st6[:, 6:12], z_f32[:, 512:1024])
        ag = q.op("v", "bn_aggr", mv[:, 2:4], st6[:, 0:12])
        q.op("s", "activation", mv[:, 3:4], mv[:, 3:4], AF.Ln, bias=eps,
             deps=[ag])
        r2 = q.op("s", "activation", rstd[:, 1:2], mv[:, 3:4], AF.Exp,
                  scale=-0.5)
        sev = r2
        q.op("v", "tensor_scalar", z_f32, z_f32, mv[:, 2:3], rstd[:, 1:2],
             OP.subtract, OP.mult, deps=[r2])
        q.op("v", "tensor_mul", z_f32, z_f32, g_rep)
        ev = q.op("v", "tensor_add", out_bf if out_bf is not None else z_f32,
                  z_f32, b_rep)
    elif out_bf is not None:
        ev = q.op("v", "tensor_copy", out_bf, z_f32)
    return ev, sev


def build_l0a(nc, q, x, Wq, bq, Wk, bk, Wv, bv, gamma, beta, attn_out, consts):
    ident_bf, ident_f32, ones_bf, ones_f32, cpool = consts
    eps = cpool[:, 0:1]
    NB, T, NT, H = 2, 2048, 16, 8

    x3 = x.rearrange("(i p) d -> i p d", p=128)
    wq3 = Wq.rearrange("(c p) d -> c p d", p=128)
    wk3 = Wk.rearrange("(c p) d -> c p d", p=128)
    wv3 = Wv.rearrange("(c p) d -> c p d", p=128)
    attn_flat = attn_out.rearrange("a d -> (a d)")

    with ExitStack() as st0:
        E = st0.enter_context
        wq_bf = E(nc.sbuf_tensor(uname("wq_bf"), [128, 8, 1024], BF16))
        wk_bf = E(nc.sbuf_tensor(uname("wk_bf"), [128, 8, 1024], BF16))
        wvp_bf = E(nc.sbuf_tensor(uname("wvp_bf"), [128, 8, 1024], BF16))
        bq_sb = E(nc.sbuf_tensor(uname("bq_sb"), [128, 8], F32))
        bk_sb = E(nc.sbuf_tensor(uname("bk_sb"), [128, 8], F32))
        g_rep = E(nc.sbuf_tensor(uname("g_rep"), [128, 1024], F32))
        b_rep = E(nc.sbuf_tensor(uname("b_rep"), [128, 1024], F32))
        bvp_bf = E(nc.sbuf_tensor(uname("bvp_bf"), [1, 1024], BF16))

        with ExitStack() as st1:
            E1 = st1.enter_context
            E1(nc.Block())
            wqe = q.dma("g", wq_bf[:, :, :], wq3.rearrange("c p d -> p c d"))
            wke = q.dma("g", wk_bf[:, :, :], wk3.rearrange("c p d -> p c d"))
            bqke = []
            for h in range(H):
                bqke.append(q.dma("y", bq_sb[:, h:h + 1],
                            bq[h * 128:(h + 1) * 128].unsqueeze(1)))
                bqke.append(q.dma("y", bk_sb[:, h:h + 1],
                            bk[h * 128:(h + 1) * 128].unsqueeze(1)))
            grpe = q.dma("y", g_rep[:, :],
                         gamma.unsqueeze(0).broadcast_to((128, 1024)))
            brpe = q.dma("y", b_rep[:, :],
                         beta.unsqueeze(0).broadcast_to((128, 1024)))

        pool_done = [None]

        def emit_wv_pooling(wv_buf, pool_f32, free_ev):
            dmae = {}
            addse = {}
            for cq in range(64):
                c, qq = cq // 8, cq % 8
                dmae[cq] = q.dma("s", wv_buf[cq % 2][:, :],
                                 wv3[c][:, qq * 1024:(qq + 1) * 1024],
                                 deps=[addse.get(cq - 2),
                                       free_ev if cq < 2 else None])
                wvg = wv_buf[cq % 2][:, :].rearrange("p (g e) -> p e g", e=8)
                dst = pool_f32[:, qq * 128:(qq + 1) * 128]
                q.op("v", "tensor_add", dst, wvg[:, 0, :], wvg[:, 1, :],
                     deps=[dmae[cq]])
                for j in range(2, 8):
                    addse[cq] = q.op("v", "tensor_add", dst, dst,
                                     wvg[:, j, :])
                if qq == 7:
                    pool_done[0] = q.op("v", "tensor_copy", wvp_bf[:, c, :],
                                        pool_f32[:, :])
            for p in range(8):
                cq = 64 + p
                dmae[cq] = q.dma("s", wv_buf[cq % 2][0:1, :],
                                 bv[p * 1024:(p + 1) * 1024].unsqueeze(0),
                                 deps=[addse.get(cq - 2)])
                bvg = wv_buf[cq % 2][0:1, :].rearrange("p (g e) -> p e g", e=8)
                dst = pool_f32[0:1, p * 128:(p + 1) * 128]
                q.op("v", "tensor_add", dst, bvg[:, 0, :], bvg[:, 1, :],
                     deps=[dmae[cq]])
                for j in range(2, 8):
                    addse[cq] = q.op("v", "tensor_add", dst, dst,
                                     bvg[:, j, :])
            pool_done[0] = q.op("v", "tensor_copy", bvp_bf[:, :],
                                pool_f32[0:1, 0:1024])

        for b in range(NB):
            with ExitStack() as st2:
                E2 = st2.enter_context
                qT = E2(nc.sbuf_tensor(uname("qT"), [128, 8, T], BF16))
                kT = E2(nc.sbuf_tensor(uname("kT"), [128, 8, T], BF16))
                vp = E2(nc.sbuf_tensor(uname("vp"), [128, 16, 1024], BF16))
                stAB = ExitStack()
                xnT = stAB.enter_context(
                    nc.sbuf_tensor(uname("xnT"), [128, 8, T], BF16))

                with ExitStack() as st3:
                    E3 = st3.enter_context
                    xt = [E3(nc.sbuf_tensor(uname("xt"), [128, 1024], F32))
                          for _ in range(2)]
                    zt = E3(nc.sbuf_tensor(uname("zt"), [128, 1024], F32))
                    xnbf = [E3(nc.sbuf_tensor(uname("xnbf"), [128, 1024], BF16))
                            for _ in range(2)]
                    st6 = [E3(nc.sbuf_tensor(uname("st6"), [128, 12], F32))
                           for _ in range(2)]
                    mv = [E3(nc.sbuf_tensor(uname("mv"), [128, 4], F32))
                          for _ in range(2)]
                    rstd = [E3(nc.sbuf_tensor(uname("rstd"), [128, 2], F32))
                            for _ in range(2)]
                    ptr = [E3(nc.psum_tensor(uname("ptr"), [128, 1024], BF16))
                           for _ in range(2)]
                    E3(nc.Block())
                    lne = {}
                    tre = {}
                    eve = {}
                    rce = {}
                    for i in range(NT + 1):
                        if i < NT:
                            de = q.dma("y", xt[i % 2][:, :], x3[b * NT + i],
                                       deps=[lne.get(i - 2)])
                            _, rce[i] = ln_stats_comb(
                                q, xt[i % 2][:, :], st6[i % 2], mv[i % 2],
                                rstd[i % 2], eps, de)
                        if i > 0:
                            j = i - 1
                            q.op("v", "tensor_scalar", zt[:, :],
                                 xt[j % 2][:, :], mv[j % 2][:, 0:1],
                                 rstd[j % 2][:, 1:2], OP.subtract, OP.mult,
                                 deps=[rce[j]])
                            q.op("v", "tensor_mul", zt[:, :], zt[:, :],
                                 g_rep[:, :], deps=[grpe] if j == 0 else ())
                            lne[j] = q.op("v", "tensor_add", xnbf[j % 2][:, :],
                                          zt[:, :], b_rep[:, :],
                                          deps=[brpe] if j == 0 else ())
                            for c in range(8):
                                tre[(j, c)] = q.tr(
                                    ptr[j % 2][:, c * 128:(c + 1) * 128],
                                    xnbf[j % 2][:, c * 128:(c + 1) * 128],
                                    ident_bf[:, :],
                                    deps=([lne[j], eve.get(j - 2)]
                                          if c == 0 else ()))
                            eve[j] = q.op(
                                "v", "tensor_copy",
                                xnT[:, :, j * 128:(j + 1) * 128],
                                ptr[j % 2][:, 0:1024].rearrange(
                                    "p (c n) -> p c n", n=128),
                                deps=[tre[(j, 7)]])
                    if b == 0:
                        emit_wv_pooling(xt, zt, lne[NT - 1])
                    psq = [E3(nc.psum_tensor(uname("psq"), [128, 512], F32))
                           for _ in range(2)]
                    k = 0
                    evace = {}
                    for h in range(H):
                        for which, wbuf, bias, wev in ((0, wq_bf, bq_sb, wqe),
                                                       (1, wk_bf, bk_sb, wke)):
                            dst = qT if which == 0 else kT
                            for p in range(4):
                                dep = [evace.get(k - 2)]
                                if h == 0 and p == 0:
                                    dep.append(wev)
                                if h == 0 and which == 0:
                                    dep.append(eve[4 * p + 3])
                                ge = q.mm_group([
                                    (psq[k % 2][:, :],
                                     wbuf[:, c, h * 128:(h + 1) * 128],
                                     xnT[:, c, p * 512:(p + 1) * 512])
                                    for c in range(8)],
                                    deps=dep)
                                evace[k] = q.op(
                                    "s", "activation",
                                    dst[:, h, p * 512:(p + 1) * 512],
                                    psq[k % 2][:, :], AF.Identity,
                                    bias=bias[:, h:h + 1],
                                    deps=[ge] + (bqke if h == 0 and p == 0
                                                 else []))
                                k += 1
                    for ti in range(16):
                        for n2 in range(2):
                            mms = [
                                (psq[k % 2][:, :],
                                 xnT[:, c, ti * 128:(ti + 1) * 128],
                                 wvp_bf[:, c, n2 * 512:(n2 + 1) * 512])
                                for c in range(8)]
                            mms.append((psq[k % 2][:, :], ones_bf[:, :],
                                        bvp_bf[:, n2 * 512:(n2 + 1) * 512]))
                            ge = q.mm_group(
                                mms, deps=[evace.get(k - 2)] +
                                ([pool_done[0]] if ti == 0 and n2 == 0
                                 else []))
                            evace[k] = q.op(
                                "s", "activation",
                                vp[:, ti, n2 * 512:(n2 + 1) * 512],
                                psq[k % 2][:, :], AF.Identity, deps=[ge])
                            k += 1
                    q.dma_fence()
                stAB.close()

                with ExitStack() as st3:
                    E3 = st3.enter_context
                    P_sb = E3(nc.sbuf_tensor(uname("P_sb"), [128, 8, 2048], BF16))
                    PT_sb = [E3(nc.sbuf_tensor(uname("PT_sb"), [128, 1024], BF16))
                             for _ in range(2)]
                    aT_sb = [E3(nc.sbuf_tensor(uname("aT_sb"), [128, 512], F32))
                             for _ in range(2)]
                    att_sb = [E3(nc.sbuf_tensor(uname("att_sb"), [128, 512], F32))
                              for _ in range(2)]
                    mx = E3(nc.sbuf_tensor(uname("mx"), [128, 64], F32))
                    rs = E3(nc.sbuf_tensor(uname("rs"), [128, 32], F32))
                    rrs = E3(nc.sbuf_tensor(uname("rrs"), [128, 16], F32))
                    ps_scA = E3(nc.psum_tensor(uname("ps_scA"), [128, 1024], F32))
                    ps_scB = E3(nc.psum_tensor(uname("ps_scB"), [128, 1024], F32))
                    ptb = [E3(nc.psum_tensor(uname("ptb"), [128, 1024], BF16))
                           for _ in range(2)]
                    attP = E3(nc.psum_tensor(uname("attP"), [128, 512], F32))
                    ptf = E3(nc.psum_tensor(uname("ptf"), [128, 128], F32))
                    E3(nc.Block())
                    exp_g = {}
                    mx_war = {}
                    rrs_war = {}
                    rs_war = {}
                    pt_last = {}
                    ptevac = {}
                    pv_ev = {}
                    aT_ev = {}
                    tb_ev = {}
                    tbe_ev = {}
                    dma_ev = {}
                    recip_ev = {}

                    expA_g = {}
                    expB_g = {}
                    deferred = []

                    def flush_deferred():
                        while deferred:
                            qt, eB = deferred.pop(0)
                            q.op("v", "tensor_add", rs[:, 2 * qt:2 * qt + 1],
                                 rs[:, 2 * qt:2 * qt + 1],
                                 rs[:, 2 * qt + 1:2 * qt + 2],
                                 deps=[eB, rrs_war.get(qt)])
                            recip_ev[qt] = q.op("v", "reciprocal",
                                                rrs[:, qt:qt + 1],
                                                rs[:, 2 * qt:2 * qt + 1])
                            rs_war[qt] = recip_ev[qt]

                    def emit_scores(sg, j, h):
                        qt = (sg % 4) * 4 + j
                        qtg = sg * 4 + j
                        hh = sg // 4
                        c0 = 4 * qt
                        sA = None
                        sB = None
                        for p in range(2):
                            sB = q.mm(
                                ps_scB[:, p * 512:(p + 1) * 512],
                                qT[:, hh, qt * 128:(qt + 1) * 128],
                                kT[:, hh, 1024 + p * 512:1024 + (p + 1) * 512],
                                start=True, stop=True,
                                deps=([expB_g.get(qtg - 1)] if p == 0 else ()))
                        for p in range(2):
                            sA = q.mm(
                                ps_scA[:, p * 512:(p + 1) * 512],
                                qT[:, hh, qt * 128:(qt + 1) * 128],
                                kT[:, hh, p * 512:(p + 1) * 512],
                                start=True, stop=True,
                                deps=([expA_g.get(qtg - 1)] if p == 0 else ()))
                        q.op("v", "reduce_max", mx[:, c0 + 1:c0 + 2],
                             ps_scB[:, :], axis=AX.X,
                             deps=[sB, mx_war.get(qt)])
                        q.op("v", "reduce_max", mx[:, c0:c0 + 1],
                             ps_scA[:, :], axis=AX.X, deps=[sA])
                        q.op("v", "tensor_max", mx[:, c0 + 2:c0 + 3],
                             mx[:, c0:c0 + 1], mx[:, c0 + 1:c0 + 2])
                        nm = q.op("v", "tensor_scalar", mx[:, c0 + 3:c0 + 4],
                                  mx[:, c0 + 2:c0 + 3], -ISQ128, None,
                                  OP.mult)
                        eA = q.op("s", "activation",
                                  P_sb[:, 4 * (sg % 2) + j, 0:1024],
                                  ps_scA[:, :], AF.Exp, scale=ISQ128,
                                  bias=mx[:, c0 + 3:c0 + 4],
                                  accum_out=rs[:, 2 * qt:2 * qt + 1],
                                  deps=[nm, pt_last.get(sg - 2),
                                        rs_war.get(qt)] if j == 0 else
                                  [nm, rs_war.get(qt)])
                        eB = q.op("s", "activation",
                                  P_sb[:, 4 * (sg % 2) + j, 1024:2048],
                                  ps_scB[:, :], AF.Exp, scale=ISQ128,
                                  bias=mx[:, c0 + 3:c0 + 4],
                                  accum_out=rs[:, 2 * qt + 1:2 * qt + 2])
                        expA_g[qtg] = eA
                        expB_g[qtg] = eB
                        exp_g[qtg] = eB
                        mx_war[qt] = eA
                        deferred.append((qt, eB))

                    def emit_pv(sg, pairs, h):
                        for kp in pairs:
                            kg = sg * 8 + kp
                            first = []
                            if kp == 0:
                                first.append(exp_g[sg * 4 + 3])
                                first.append(aT_ev.get(sg - 1))
                            te = None
                            for kk2 in range(2):
                                kc = 2 * kp + kk2
                                for j in range(4):
                                    te = q.tr(
                                        ptb[kp % 2][:, kk2 * 512 + j * 128:
                                                    kk2 * 512 + (j + 1) * 128],
                                        P_sb[:, 4 * (sg % 2) + j,
                                             kc * 128:(kc + 1) * 128],
                                        ident_bf[:, :],
                                        deps=(first + [ptevac.get(kg - 2)])
                                        if (kk2 == 0 and j == 0) else ())
                            if kp == 7:
                                pt_last[sg] = te
                            if kp % 2 == 0:
                                pe = q.op("v", "tensor_copy",
                                          PT_sb[kp % 2][:, :],
                                          ptb[kp % 2][:, :],
                                          deps=[te, pv_ev.get((sg, 2 * kp - 3))])
                            else:
                                pe = q.op("s", "activation",
                                          PT_sb[kp % 2][:, :],
                                          ptb[kp % 2][:, :], AF.Identity,
                                          deps=[te, pv_ev.get((sg, 2 * kp - 3))])
                            ptevac[kg] = pe
                            for kk2 in range(2):
                                kc = 2 * kp + kk2
                                pv_ev[(sg, kc)] = q.mm(
                                    attP[:, :],
                                    vp[:, kc, h * 128:(h + 1) * 128],
                                    PT_sb[kp % 2][:, kk2 * 512:
                                                  (kk2 + 1) * 512],
                                    start=(kc == 0), stop=(kc == 15),
                                    deps=[pe] if kk2 == 0 else ())

                    def emit_tail(sg, h):
                        aT_ev[sg] = q.op("v", "tensor_copy",
                                         aT_sb[sg % 2][:, :], attP[:, :],
                                         deps=[pv_ev[(sg, 15)],
                                               tb_ev.get((sg - 2, 3))])
                        for j in range(4):
                            qt = (sg % 4) * 4 + j
                            tb_ev[(sg, j)] = q.tr(
                                ptf[:, :], aT_sb[sg % 2][:, j * 128:(j + 1) * 128],
                                ident_f32[:, :],
                                deps=[aT_ev[sg], tbe_ev.get((sg, j - 1))]
                                if j > 0 else
                                [aT_ev[sg], tbe_ev.get((sg - 1, 3))])
                            tbe_ev[(sg, j)] = q.op(
                                "s", "activation",
                                att_sb[sg % 2][:, j * 128:(j + 1) * 128],
                                ptf[:, :], AF.Identity,
                                scale=rrs[:, qt:qt + 1],
                                deps=[tb_ev[(sg, j)]] +
                                ([dma_ev.get(sg - 2)] if j == 0 else []))
                            rrs_war[qt] = tbe_ev[(sg, j)]
                        hh = sg // 4
                        ss = sg % 4
                        base = b * 2097152 + hh * 262144 + ss * 65536
                        seg = attn_flat[base:base + 65536]
                        dma_ev[sg] = q.dma(
                            "y", seg.rearrange("(j p m) -> p j m", p=128, m=128),
                            att_sb[sg % 2][:, :].rearrange(
                                "p (j m) -> p j m", m=128),
                            deps=[tbe_ev[(sg, 3)]])

                    NSLAB = H * 4
                    for sg in range(NSLAB + 1):
                        for j in range(4):
                            if sg < NSLAB:
                                emit_scores(sg, j, sg // 4)
                                if j > 0 or sg > 0:
                                    flush_deferred()
                            if sg > 0:
                                emit_pv(sg - 1, [2 * j, 2 * j + 1],
                                        (sg - 1) // 4)
                        if sg > 0:
                            flush_deferred()
                            emit_tail(sg - 1, (sg - 1) // 4)
                    flush_deferred()
                    q.dma_fence()
    return nc


def build_l0b(nc, q, x, attn, w_router, W1, b1, W2, b2, x2_out, consts,
              x1h_hbm, hT_hbm, row_hbm):
    ident_bf, ident_f32, ones_bf, ones_f32, cpool = consts
    eps = cpool[:, 0:1]
    NTILE, NITER = 32, 44

    x3 = x.rearrange("(i p) d -> i p d", p=128)
    at3 = attn.rearrange("(i p) d -> i p d", p=128)
    x1h3 = x1h_hbm.rearrange("(i p) d -> i p d", p=128)
    x23 = x2_out.rearrange("(i p) d -> i p d", p=128)
    w13 = W1.rearrange("(c p) f -> c p f", p=128)
    w23 = W2.rearrange("(c p) d -> c p d", p=128)

    with ExitStack() as st0:
        E = st0.enter_context
        w1_bf = E(nc.sbuf_tensor(uname("w1_bf"), [128, 8, 4096], BF16))
        w2_bf = E(nc.sbuf_tensor(uname("w2_bf"), [128, 32, 1024], BF16))
        b1_sb = E(nc.sbuf_tensor(uname("b1_sb"), [128, 32], F32))
        b2_rep = E(nc.sbuf_tensor(uname("b2_rep"), [128, 1024], F32))
        s128 = E(nc.sbuf_tensor(uname("s128"), [128, 32], F32))
        m128 = E(nc.sbuf_tensor(uname("m128"), [128, 32], F32))
        stWR = ExitStack()
        wr_rep = stWR.enter_context(
            nc.sbuf_tensor(uname("wr_rep"), [128, 1024], F32))

        with ExitStack() as st1:
            E1 = st1.enter_context
            E1(nc.Block())
            w1we = q.dma("g", w1_bf[:, :, :], w13.rearrange("c p f -> p c f"))
            w2we = q.dma("g", w2_bf[:, :, :], w23.rearrange("c p d -> p c d"))
            b1we = []
            for mt in range(32):
                b1we.append(q.dma("y", b1_sb[:, mt:mt + 1],
                            b1[mt * 128:(mt + 1) * 128].unsqueeze(1)))
            b2we = q.dma("y", b2_rep[:, :],
                         b2.unsqueeze(0).broadcast_to((128, 1024)))
            wrwe = q.dma("y", wr_rep[:, :],
                         w_router.unsqueeze(0).broadcast_to((128, 1024)))

        with ExitStack() as st1:
            E1 = st1.enter_context
            hT = E1(nc.sbuf_tensor(uname("hT_half"), [128, 8, 2048], BF16))
            xt = [E1(nc.sbuf_tensor(uname("xt"), [128, 1024], F32))
                  for _ in range(2)]
            att = [E1(nc.sbuf_tensor(uname("att"), [128, 1024], F32))
                   for _ in range(2)]
            zt = E1(nc.sbuf_tensor(uname("zt"), [128, 1024], F32))
            hbf = [E1(nc.sbuf_tensor(uname("hbf"), [128, 1024], BF16))
                   for _ in range(2)]
            st6 = [E1(nc.sbuf_tensor(uname("st6"), [128, 12], F32))
                   for _ in range(2)]
            mv = [E1(nc.sbuf_tensor(uname("mv"), [128, 4], F32))
                  for _ in range(2)]
            rstd = [E1(nc.sbuf_tensor(uname("rstd"), [128, 2], F32))
                    for _ in range(2)]
            ptr = [E1(nc.psum_tensor(uname("ptr"), [128, 1024], BF16))
                   for _ in range(2)]
            E1(nc.Block())
            x1de = {}
            adde = {}
            eve = {}
            tre = {}
            r1e = {}
            dume = None
            for i in range(NTILE + 1):
                if i < NTILE:
                    dx = q.dma("y", xt[i % 2][:, :], x3[i],
                               deps=[x1de.get(i - 2)])
                    da = q.dma("y", att[i % 2][:, :], at3[i],
                               deps=[adde.get(i - 2)])
                    q.op("v", "tensor_add", xt[i % 2][:, :], xt[i % 2][:, :],
                         att[i % 2][:, :], deps=[dx, da])
                    r1e[i], _ = ln_stats_comb(q, xt[i % 2][:, :], st6[i % 2],
                                              mv[i % 2], rstd[i % 2], eps,
                                              None)
                if i > 0:
                    j = i - 1
                    half, ih = j // 16, j % 16
                    q.op("v", "tensor_scalar", zt[:, :], xt[j % 2][:, :],
                         mv[j % 2][:, 0:1], rstd[j % 2][:, 0:1],
                         OP.subtract, OP.mult, deps=[r1e[j]])
                    cpv = q.op("v", "tensor_copy", hbf[j % 2][:, :], zt[:, :])
                    adde[j] = q.op("v", "tensor_mul", att[j % 2][:, :],
                                   zt[:, :], wr_rep[:, :],
                                   deps=[wrwe] if j == 0 else ())
                    q.op("v", "reduce_sum", s128[:, j:j + 1], att[j % 2][:, :],
                         axis=AX.X)
                    x1a = q.op("v", "tensor_add", xt[j % 2][:, :],
                               xt[j % 2][:, :], zt[:, :])
                    x1de[j] = q.dma("y", x1h3[j], xt[j % 2][:, :], deps=[x1a])
                    for c in range(8):
                        tre[(j, c)] = q.tr(
                            ptr[j % 2][:, c * 128:(c + 1) * 128],
                            hbf[j % 2][:, c * 128:(c + 1) * 128],
                            ident_bf[:, :],
                            deps=([cpv, eve.get(j - 2)] if c == 0 else ()))
                    eve[j] = q.op("v", "tensor_copy",
                                  hT[:, :, ih * 128:(ih + 1) * 128],
                                  ptr[j % 2][:, 0:1024].rearrange(
                                      "p (c n) -> p c n", n=128),
                                  deps=[tre[(j, 7)]] +
                                  ([dume] if ih == 0 and dume else []))
                    if ih == 15:
                        dume = q.dma(
                            "y", hT_hbm[:, :, half * 2048:(half + 1) * 2048],
                            hT[:, :, :], deps=[eve[j], dume])
            q.dma_fence()
        stWR.close()

        with ExitStack() as st1:
            E1 = st1.enter_context
            hT = E1(nc.sbuf_tensor(uname("hT_c"), [128, 8, 2048], BF16))
            h2 = E1(nc.sbuf_tensor(uname("h2"), [128, 32, 512], BF16))
            x1h_t = [E1(nc.sbuf_tensor(uname("x1h_t"), [128, 1024], F32))
                     for _ in range(1)]
            out_t = [E1(nc.sbuf_tensor(uname("out_t"), [128, 1024], F32))
                     for _ in range(1)]
            stt = E1(nc.sbuf_tensor(uname("stt"), [1, 16], F32))
            sti = E1(nc.sbuf_tensor(uname("sti"), [1, 4], I32))
            tpp = E1(nc.sbuf_tensor(uname("tpp"), [128, 2], F32))
            ge128 = E1(nc.sbuf_tensor(uname("ge128"), [128, 32], F32))
            red2 = E1(nc.sbuf_tensor(uname("red2"), [128, 2], F32))
            th128 = E1(nc.sbuf_tensor(uname("th128"), [128, 32], F32))
            ones_col = E1(nc.sbuf_tensor(uname("ones_col"), [128, 1], F32))
            ps1 = [E1(nc.psum_tensor(uname("ps1"), [128, 512], F32))
                   for _ in range(2)]
            ps2 = [E1(nc.psum_tensor(uname("ps2"), [128, 512], F32))
                   for _ in range(4)]
            psb = E1(nc.psum_tensor(uname("psb"), [128, 2], F32))
            E1(nc.Block())
            lo, hi, tt, cnt = (stt[:, 0:2], stt[:, 2:4], stt[:, 4:6],
                               stt[:, 6:8])
            mce = q.op("g", "memset", ones_col[:, :], 1.0)
            q.op("v", "memset", lo[:, :], -1000.0)
            q.op("v", "memset", hi[:, :], 1000.0)
            q.op("v", "tensor_add", tt, lo, hi)
            te = q.op("v", "tensor_scalar_mul", tt, tt, 0.5)
            bis_last_s = None
            for it in range(NITER):
                bc = q.mm(psb[:, :], ones_f32[0:1, :], tt, start=True,
                          stop=True, deps=[te, mce])
                bce = q.op("s", "activation", tpp[:, :], psb[:, :],
                           AF.Identity, deps=[bc])
                g1 = q.op("v", "tensor_scalar", ge128[:, 0:16],
                          s128[:, 0:16], tpp[:, 0:1], None, OP.is_ge,
                          deps=[bce])
                q.op("v", "tensor_scalar", ge128[:, 16:32], s128[:, 16:32],
                     tpp[:, 1:2], None, OP.is_ge)
                q.op("v", "reduce_sum", red2[:, 0:1], ge128[:, 0:16],
                     axis=AX.X)
                re = q.op("v", "reduce_sum", red2[:, 1:2], ge128[:, 16:32],
                          axis=AX.X)
                cm = q.mm(psb[0:1, :], ones_col[:, :], red2[:, :], start=True,
                          stop=True, deps=[re])
                ce = q.op("s", "activation", cnt, psb[0:1, :], AF.Identity,
                          deps=[cm])
                q.op("v", "tensor_scalar", sti[:, 0:2], cnt, 1024.0, None,
                     OP.is_ge, deps=[ce])
                q.op("v", "tensor_scalar", sti[:, 2:4], cnt, 1024.0, None,
                     OP.is_lt)
                q.op("v", "select", lo, sti[:, 0:2], tt, lo, add_drain=True)
                q.op("v", "select", hi, sti[:, 2:4], tt, hi, add_drain=True)
                q.op("v", "tensor_add", tt, lo, hi)
                te = q.op("v", "tensor_scalar_mul", tt, tt, 0.5)
            bc = q.mm(psb[:, :], ones_f32[0:1, :], lo, start=True, stop=True,
                      deps=[te])
            bce = q.op("s", "activation", tpp[:, :], psb[:, :], AF.Identity,
                       deps=[bc])
            q.op("v", "tensor_scalar", ge128[:, 0:16], s128[:, 0:16],
                 tpp[:, 0:1], None, OP.is_ge, deps=[bce])
            q.op("v", "tensor_scalar", ge128[:, 16:32], s128[:, 16:32],
                 tpp[:, 1:2], None, OP.is_ge)
            the = q.op("s", "activation", th128[:, :], s128[:, :], AF.Tanh,
                       scale=0.5, deps=[bce])
            q.op("v", "tensor_scalar", th128[:, :], th128[:, :], 0.5, 0.5,
                 OP.mult, OP.add, deps=[the])
            m_ev = q.op("v", "tensor_mul", m128[:, :], th128[:, :],
                        ge128[:, :])

            k1 = 0
            k2 = 0
            gele = {}
            w2le = {}
            combe = {}
            dmae = {}
            x1de = {}
            hle = None
            for half in range(2):
                hle = q.dma("y", hT[:, :, :],
                            hT_hbm[:, :, half * 2048:(half + 1) * 2048],
                            deps=[w2le.get(k2 - 1), hle])
                for sl in range(4):
                    tok0 = half * 2048 + sl * 512
                    slab_gel = []
                    for mt in range(32):
                        dep = [gele.get(k1 - 2)]
                        if mt == 0:
                            dep.append(hle)
                            dep.append(w2le.get(k2 - 4))
                            if k1 == 0:
                                dep.append(w1we)
                        ge_ = q.mm_group([
                            (ps1[k1 % 2][:, :],
                             w1_bf[:, c, mt * 128:(mt + 1) * 128],
                             hT[:, c, sl * 512:(sl + 1) * 512])
                            for c in range(8)], deps=dep)
                        gele[k1] = q.op("s", "activation", h2[:, mt, :],
                                        ps1[k1 % 2][:, :],
                                        AF.Gelu_apprx_tanh,
                                        bias=b1_sb[:, mt:mt + 1],
                                        deps=[ge_] + (b1we if k1 == 0 else []))
                        slab_gel.append(gele[k1])
                        k1 += 1
                    for ti in range(4):
                        gi = (tok0 // 128) + ti
                        xde = q.dma("y", x1h_t[0][:, :], x1h3[gi],
                                    deps=[combe.get((gi - 1, 1))])
                        for ds in range(2):
                            mms = [
                                (ps2[k2 % 4][:, :],
                                 h2[:, fc, ti * 128:(ti + 1) * 128],
                                 w2_bf[:, fc, ds * 512:(ds + 1) * 512])
                                for fc in range(32)]
                            dep = [combe.get((gi - 2, ds))]
                            if ti == 0 and ds == 0:
                                dep.append(slab_gel[-1])
                                if k2 == 0:
                                    dep.append(w2we)
                            w2le[k2] = q.mm_group(mms, deps=dep)
                            c1 = q.op("v", "scalar_tensor_tensor",
                                      out_t[0][:, ds * 512:(ds + 1) * 512],
                                      ps2[k2 % 4][:, :], m128[:, gi:gi + 1],
                                      x1h_t[0][:, ds * 512:(ds + 1) * 512],
                                      OP.mult, OP.add,
                                      deps=[w2le[k2], xde, m_ev,
                                            dmae.get(gi - 1)] +
                                      ([b2we] if k2 == 0 else []))
                            combe[(gi, ds)] = q.op(
                                "v", "scalar_tensor_tensor",
                                out_t[0][:, ds * 512:(ds + 1) * 512],
                                b2_rep[:, ds * 512:(ds + 1) * 512],
                                m128[:, gi:gi + 1],
                                out_t[0][:, ds * 512:(ds + 1) * 512],
                                OP.mult, OP.add)
                            k2 += 1
                        dmae[gi] = q.dma("y", x23[gi], out_t[0][:, :],
                                         deps=[combe[(gi, 1)]])
            q.dma_fence()
    return nc


def build_l1a(nc, q, x2, wdict, x3_out, consts, z_hbm, g_hbm, ao_hbm,
              x1i_hbm, hT_hbm):
    ident_bf, ident_f32, ones_bf, ones_f32, cpool = consts
    eps = cpool[:, 0:1]
    NB, T, NT, H, DH = 2, 2048, 16, 16, 64

    x3v = x2.rearrange("(i p) d -> i p d", p=128)
    zv = z_hbm.rearrange("(i p) d -> i p d", p=128)
    gv = g_hbm.rearrange("(i p) d -> i p d", p=128)
    x1iv = x1i_hbm.rearrange("(i p) d -> i p d", p=128)
    xov = x3_out.rearrange("(i p) d -> i p d", p=128)
    aov = ao_hbm.rearrange("(i p) d -> i p d", p=128)

    W = {k: v.rearrange("(c p) d -> c p d", p=128) if k in
         ("WQ", "WK", "WV", "WO", "Wg", "W1", "Bw") else v
         for k, v in wdict.items()}
    w23 = wdict["W2"].rearrange("(c p) d -> c p d", p=128)

    with ExitStack() as st0:
        E = st0.enter_context
        stAttn = ExitStack()
        EA = stAttn.enter_context
        ln1g = EA(nc.sbuf_tensor(uname("ln1g"), [128, 1024], F32))
        ln1b = EA(nc.sbuf_tensor(uname("ln1b"), [128, 1024], F32))
        bw_bf = EA(nc.sbuf_tensor(uname("bw_bf"), [128, 8, 64], BF16))
        cw_bf = EA(nc.sbuf_tensor(uname("cw_bf"), [64, 1024], BF16))
        apow = EA(nc.sbuf_tensor(uname("apow"), [64, 12], F32))
        wmask = EA(nc.sbuf_tensor(uname("wmask"), [128, 256], F32))

        with ExitStack() as st1:
            E1 = st1.enter_context
            zero128 = E1(nc.sbuf_tensor(uname("zero128"), [128, 128], F32))
            E1(nc.Block())
            l1ge = q.dma("y", ln1g[:, :],
                         wdict["ln1_g"].unsqueeze(0).broadcast_to((128, 1024)))
            l1be = q.dma("y", ln1b[:, :],
                         wdict["ln1_b"].unsqueeze(0).broadcast_to((128, 1024)))
            bwe = q.dma("g", bw_bf[:, :, :], W["Bw"].rearrange("c p d -> p c d"))
            cwe = q.dma("g", cw_bf[:, :], wdict["Cw"])
            ade = q.dma("y", apow[:, 0:1], wdict["A"].unsqueeze(1))
            for j in range(10):
                q.op("v", "tensor_mul", apow[0:64, j + 1:j + 2],
                     apow[0:64, j:j + 1], apow[0:64, j:j + 1],
                     deps=[ade] if j == 0 else ())
            ze = q.op("g", "memset", zero128[:, :], 0.0)
            q.op("g", "affine_select", wmask[:, 0:128], zero128[:, :],
                 pattern=[[1, 128]], compare_op=OP.is_ge, fill=NEG_BIG,
                 base=-1, channel_multiplier=-1, deps=[ze])
            q.op("g", "affine_select", wmask[:, 128:256], zero128[:, :],
                 pattern=[[-1, 128]], compare_op=OP.is_ge, fill=NEG_BIG,
                 base=0, channel_multiplier=1)

        for b in range(NB):
            with ExitStack() as stb:
                Eb = stb.enter_context
                qT = Eb(nc.sbuf_tensor(uname("qT"), [128, 8, T], BF16))
                kT = Eb(nc.sbuf_tensor(uname("kT"), [128, 8, T], BF16))
                v_sb = Eb(nc.sbuf_tensor(uname("v_sb"), [128, 16, 1024], BF16))
                u_sb = Eb(nc.sbuf_tensor(uname("u_sb"), [64, 2048], F32))
                stT_bf = Eb(nc.sbuf_tensor(uname("stT_bf"), [64, 2048], BF16))
                stAB = ExitStack()
                hT = stAB.enter_context(
                    nc.sbuf_tensor(uname("hT"), [128, 8, T], BF16))

                with ExitStack() as st1:
                    E1 = st1.enter_context
                    xt = [E1(nc.sbuf_tensor(uname("xt"), [128, 1024], F32))
                          for _ in range(2)]
                    zt = [E1(nc.sbuf_tensor(uname("zt"), [128, 1024], F32))
                          for _ in range(1)]
                    hbf = [E1(nc.sbuf_tensor(uname("hbf"), [128, 1024], BF16))
                           for _ in range(1)]
                    st6 = [E1(nc.sbuf_tensor(uname("st6"), [128, 12], F32))
                           for _ in range(2)]
                    mv = [E1(nc.sbuf_tensor(uname("mv"), [128, 4], F32))
                          for _ in range(2)]
                    rstd = [E1(nc.sbuf_tensor(uname("rstd"), [128, 2], F32))
                            for _ in range(2)]
                    ptr = [E1(nc.psum_tensor(uname("ptr"), [128, 1024], BF16))
                           for _ in range(2)]
                    E1(nc.Block())
                    zde = {}
                    eve = {}
                    tre = {}
                    rce = {}
                    for i in range(NT + 1):
                        if i < NT:
                            de = q.dma("y", xt[i % 2][:, :], x3v[b * NT + i],
                                       deps=[eve.get(i - 2)])
                            rce[i] = ln_stats_comb(q, xt[i % 2][:, :],
                                                   st6[i % 2], mv[i % 2],
                                                   rstd[i % 2], eps, de)
                        if i > 0:
                            j = i - 1
                            gi = b * NT + j
                            r1e, rcomb_e = rce[j]
                            zz = q.op("v", "tensor_scalar", zt[0][:, :],
                                      xt[j % 2][:, :], mv[j % 2][:, 0:1],
                                      rstd[j % 2][:, 0:1],
                                      OP.subtract, OP.mult,
                                      deps=[r1e, zde.get(j - 1)])
                            zde[j] = q.dma("y", zv[gi], zt[0][:, :],
                                           deps=[zz])
                            q.op("v", "tensor_scalar", xt[j % 2][:, :],
                                 xt[j % 2][:, :], mv[j % 2][:, 0:1],
                                 rstd[j % 2][:, 1:2], OP.subtract, OP.mult,
                                 deps=[rcomb_e])
                            q.op("v", "tensor_mul", xt[j % 2][:, :],
                                 xt[j % 2][:, :],
                                 ln1g[:, :], deps=[l1ge] if j == 0 else ())
                            he = q.op("v", "tensor_add", hbf[0][:, :],
                                      xt[j % 2][:, :], ln1b[:, :],
                                      deps=([l1be] if j == 0 else []) +
                                      [tre.get((j - 1, 7))])
                            for c in range(8):
                                tre[(j, c)] = q.tr(
                                    ptr[j % 2][:, c * 128:(c + 1) * 128],
                                    hbf[0][:, c * 128:(c + 1) * 128],
                                    ident_bf[:, :],
                                    deps=([he, eve.get(j - 2)]
                                          if c == 0 else ()))
                            eve[j] = q.op(
                                "v", "tensor_copy",
                                hT[:, :, j * 128:(j + 1) * 128],
                                ptr[j % 2][:, 0:1024].rearrange(
                                    "p (c n) -> p c n", n=128),
                                deps=[tre[(j, 7)]])
                    bb_sb = E1(nc.sbuf_tensor(uname("bb_sb"), [128, 16], F32))
                    psq = [E1(nc.psum_tensor(uname("psq"), [128, 512], F32))
                           for _ in range(2)]
                    bbe = {}
                    for mt in range(8):
                        bbe[mt] = q.dma(
                            "y", bb_sb[:, mt:mt + 1],
                            wdict["bQ"][mt * 128:(mt + 1) * 128].unsqueeze(1))
                        bbe[8 + mt] = q.dma(
                            "y", bb_sb[:, 8 + mt:9 + mt],
                            wdict["bK"][mt * 128:(mt + 1) * 128].unsqueeze(1))
                    k = 0
                    evace = {}
                    with ExitStack() as st2:
                        E2 = st2.enter_context
                        wq_bf = E2(nc.sbuf_tensor(uname("wq_bf"),
                                                  [128, 8, 1024], BF16))
                        wk_bf = E2(nc.sbuf_tensor(uname("wk_bf"),
                                                  [128, 8, 1024], BF16))
                        we1 = q.dma("g", wq_bf[:, :, :],
                                    W["WQ"].rearrange("c p d -> p c d"))
                        we2 = q.dma("g", wk_bf[:, :, :],
                                    W["WK"].rearrange("c p d -> p c d"))
                        for mt in range(8):
                            for which, wbuf, we in ((0, wq_bf, we1),
                                                    (1, wk_bf, we2)):
                                dst = qT if which == 0 else kT
                                boff = 0 if which == 0 else 8
                                for p in range(4):
                                    dep = [evace.get(k - 2)]
                                    if mt == 0 and p == 0:
                                        dep.append(we)
                                    if mt == 0 and which == 0:
                                        dep.append(eve[4 * p + 3])
                                    ge = q.mm_group([
                                        (psq[k % 2][:, :],
                                         wbuf[:, c, mt * 128:(mt + 1) * 128],
                                         hT[:, c, p * 512:(p + 1) * 512])
                                        for c in range(8)], deps=dep)
                                    evace[k] = q.op(
                                        "s", "activation",
                                        dst[:, mt, p * 512:(p + 1) * 512],
                                        psq[k % 2][:, :], AF.Identity,
                                        bias=bb_sb[:, boff + mt:boff + mt + 1],
                                        deps=[ge] +
                                        ([bbe[boff + mt]] if p == 0 else []))
                                    k += 1
                    with ExitStack() as st2:
                        E2 = st2.enter_context
                        wv_bf = E2(nc.sbuf_tensor(uname("wv_bf"),
                                                  [128, 8, 1024], BF16))
                        wg_bf = E2(nc.sbuf_tensor(uname("wg_bf"),
                                                  [128, 8, 1024], BF16))
                        bv_bf = E2(nc.sbuf_tensor(uname("bv_bf"), [1, 1024], BF16))
                        bg_bf = E2(nc.sbuf_tensor(uname("bg_bf"), [1, 1024], BF16))
                        gbf = [E2(nc.sbuf_tensor(uname("gbf"), [128, 1024], BF16))
                               for _ in range(1)]
                        we1 = q.dma("g", wv_bf[:, :, :],
                                    W["WV"].rearrange("c p d -> p c d"),
                                    deps=[evace[k - 1]])
                        we2 = q.dma("g", wg_bf[:, :, :],
                                    W["Wg"].rearrange("c p d -> p c d"))
                        be1 = q.dma("g", bv_bf[:, :], wdict["bV"].unsqueeze(0))
                        be2 = q.dma("g", bg_bf[:, :], wdict["bg"].unsqueeze(0))
                        gde = {}
                        for ti in range(16):
                            for n2 in range(2):
                                dep = [evace.get(k - 2)]
                                if ti == 0 and n2 == 0:
                                    dep += [we1, be1]
                                mms = [
                                    (psq[k % 2][:, :],
                                     hT[:, c, ti * 128:(ti + 1) * 128],
                                     wv_bf[:, c, n2 * 512:(n2 + 1) * 512])
                                    for c in range(8)]
                                mms.append((psq[k % 2][:, :], ones_bf[:, :],
                                            bv_bf[:, n2 * 512:(n2 + 1) * 512]))
                                ge = q.mm_group(mms, deps=dep)
                                evace[k] = q.op(
                                    "s", "activation",
                                    v_sb[:, ti, n2 * 512:(n2 + 1) * 512],
                                    psq[k % 2][:, :], AF.Identity, deps=[ge])
                                k += 1
                            for n2 in range(2):
                                dep = [evace.get(k - 2)]
                                if ti == 0 and n2 == 0:
                                    dep += [we2, be2]
                                if n2 == 0:
                                    dep += [gde.get(ti - 1)]
                                mms = [
                                    (psq[k % 2][:, :],
                                     hT[:, c, ti * 128:(ti + 1) * 128],
                                     wg_bf[:, c, n2 * 512:(n2 + 1) * 512])
                                    for c in range(8)]
                                mms.append((psq[k % 2][:, :], ones_bf[:, :],
                                            bg_bf[:, n2 * 512:(n2 + 1) * 512]))
                                ge = q.mm_group(mms, deps=dep)
                                evace[k] = q.op(
                                    "s", "activation",
                                    gbf[0][:, n2 * 512:(n2 + 1) * 512],
                                    psq[k % 2][:, :], AF.Sigmoid, deps=[ge])
                                k += 1
                            gde[ti] = q.dma("y", gv[b * NT + ti],
                                            gbf[0][:, :],
                                            deps=[evace[k - 1]])
                        for p in range(4):
                            ge = q.mm_group([
                                (psq[k % 2][0:64, :],
                                 bw_bf[:, c, 0:64],
                                 hT[:, c, p * 512:(p + 1) * 512])
                                for c in range(8)],
                                deps=[evace.get(k - 2)] +
                                ([bwe] if p == 0 else []))
                            evace[k] = q.op(
                                "v", "tensor_copy",
                                u_sb[:, p * 512:(p + 1) * 512],
                                psq[k % 2][0:64, :], deps=[ge])
                            k += 1
                        q.dma_fence()
                stAB.close()

                with ExitStack() as st1:
                    E1 = st1.enter_context
                    u_sb2 = E1(nc.sbuf_tensor(uname("u_sb2"), [64, 2048], F32))
                    ao_sb = E1(nc.sbuf_tensor(uname("ao_sb"), [128, 16, 1024], BF16))
                    msk = [E1(nc.sbuf_tensor(uname("msk"), [128, 256], F32))
                           for _ in range(2)]
                    P_bf = [E1(nc.sbuf_tensor(uname("P_bf"), [128, 256], BF16))
                            for _ in range(2)]
                    PT_sb = [E1(nc.sbuf_tensor(uname("PT_sb"), [128, 256], BF16))
                             for _ in range(2)]
                    mx = E1(nc.sbuf_tensor(uname("mx"), [128, 8], F32))
                    rs = E1(nc.sbuf_tensor(uname("rs"), [128, 16], F32))
                    ps_sc = [E1(nc.psum_tensor(uname("ps_sc"), [128, 256], F32))
                             for _ in range(2)]
                    ptb = [E1(nc.psum_tensor(uname("ptb"), [128, 256], BF16))
                           for _ in range(2)]
                    pao = [E1(nc.psum_tensor(uname("pao"), [128, 64], F32))
                           for _ in range(2)]
                    jnk = E1(nc.psum_tensor(uname("jnk"), [128, 256], F32))
                    E1(nc.Block())
                    src, dst = u_sb, u_sb2
                    for j in range(11):
                        o = 1 << j
                        q.op("v", "tensor_copy", dst[:, 0:o], src[:, 0:o])
                        q.op("v", "scalar_tensor_tensor", dst[:, o:2048],
                             src[:, 0:2048 - o], apow[0:64, j:j + 1],
                             src[:, o:2048], OP.mult, OP.add)
                        src, dst = dst, src
                    sce_scan = q.op("v", "tensor_copy", stT_bf[:, :], src[:, :])
                    sc_ev = {}
                    mske = {}
                    expe = {}
                    rece = {}
                    pte = {}
                    ptev = {}
                    aoe = {}
                    aoev = {}
                    rs = E1(nc.sbuf_tensor(uname("rs4"), [128, 8], F32)) \
                        if False else rs

                    def c_part1(kk):
                        h, qb = kk // 16, kk % 16
                        mt, off = h // 2, (h % 2) * 64
                        first = (qb == 0)
                        kw = 128 if first else 256
                        kb0 = qb if first else qb - 1
                        sc_ev[kk] = q.mm(
                            ps_sc[kk % 2][:, 0:kw],
                            qT[off:off + 64, mt, qb * 128:(qb + 1) * 128],
                            kT[off:off + 64, mt, kb0 * 128:kb0 * 128 + kw],
                            start=True, stop=True,
                            deps=[mske.get(kk - 2)])
                        mske[kk] = q.op("v", "tensor_add",
                                        msk[kk % 2][:, 0:kw],
                                        ps_sc[kk % 2][:, 0:kw],
                                        wmask[:, 256 - kw:256],
                                        deps=[sc_ev[kk], expe.get(kk - 2)])
                        me = q.op("v", "reduce_max",
                                  mx[:, 2 * (kk % 2):2 * (kk % 2) + 1],
                                  msk[kk % 2][:, 0:kw], axis=AX.X,
                                  deps=[expe.get(kk - 2)])
                        nm = q.op("v", "tensor_scalar",
                                  mx[:, 2 * (kk % 2) + 1:2 * (kk % 2) + 2],
                                  mx[:, 2 * (kk % 2):2 * (kk % 2) + 1],
                                  -ISQ64, None, OP.mult)
                        expe[kk] = q.op(
                            "s", "activation", P_bf[kk % 2][:, 0:kw],
                            msk[kk % 2][:, 0:kw], AF.Exp, scale=ISQ64,
                            bias=mx[:, 2 * (kk % 2) + 1:2 * (kk % 2) + 2],
                            accum_out=rs[:, 4 * (kk % 4):4 * (kk % 4) + 1],
                            deps=[nm, pte.get(kk - 2), rece.get(kk - 4)])
                        rece[kk] = q.op(
                            "v", "reciprocal",
                            rs[:, 4 * (kk % 4) + 1:4 * (kk % 4) + 2],
                            rs[:, 4 * (kk % 4):4 * (kk % 4) + 1],
                            deps=[expe[kk], aoev.get(kk - 4)])
                        q.mm(jnk[:, :], ident_bf[:, :],
                             v_sb[:, 0, 0:256], start=True, stop=True,
                             deps=[mske[kk]])

                    def c_part2(kk):
                        h, qb = kk // 16, kk % 16
                        first = (qb == 0)
                        kw = 128 if first else 256
                        kb0 = qb if first else qb - 1
                        te = None
                        for kk2 in range(kw // 128):
                            te = q.tr(ptb[kk % 2][:, kk2 * 128:(kk2 + 1) * 128],
                                      P_bf[kk % 2][:, kk2 * 128:(kk2 + 1) * 128],
                                      ident_bf[:, :],
                                      deps=[expe[kk], ptev.get(kk - 2)]
                                      if kk2 == 0 else ())
                        pte[kk] = te
                        ptev[kk] = q.op("v", "tensor_copy",
                                        PT_sb[kk % 2][:, 0:kw],
                                        ptb[kk % 2][:, 0:kw],
                                        deps=[te, aoe.get(kk - 2)])
                        ae = None
                        for kk2 in range(kw // 128):
                            ae = q.mm(pao[kk % 2][:, :],
                                      PT_sb[kk % 2][:, kk2 * 128:(kk2 + 1) * 128],
                                      v_sb[:, kb0 + kk2, h * 64:(h + 1) * 64],
                                      start=(kk2 == 0),
                                      stop=(kk2 == kw // 128 - 1),
                                      deps=[ptev[kk], aoev.get(kk - 2)]
                                      if kk2 == 0 else ())
                        aoe[kk] = ae
                        aoev[kk] = q.op(
                            "s", "activation",
                            ao_sb[:, qb, h * 64:(h + 1) * 64],
                            pao[kk % 2][:, :], AF.Identity,
                            scale=rs[:, 4 * (kk % 4) + 1:4 * (kk % 4) + 2],
                            deps=[ae, rece[kk]])

                    for kk in range(H * 16 + 1):
                        if kk < H * 16:
                            c_part1(kk)
                        if kk > 0:
                            c_part2(kk - 1)
                    kk = H * 16
                    dume = None
                    for ti in range(NT):
                        dume = q.dma("y", aov[b * NT + ti], ao_sb[:, ti, :],
                                     deps=[aoev[kk - 1], dume])
                    q.dma_fence()

                with ExitStack() as st1:
                    E1 = st1.enter_context
                    wo_bf = E1(nc.sbuf_tensor(uname("wo_bf"), [128, 8, 1024], BF16))
                    bo_bf = E1(nc.sbuf_tensor(uname("bo_bf"), [1, 1024], BF16))
                    aot = [E1(nc.sbuf_tensor(uname("aot"), [128, 1024], BF16))
                           for _ in range(2)]
                    aoT = [E1(nc.sbuf_tensor(uname("aoT"), [128, 8, 128], BF16))
                           for _ in range(2)]
                    zt = [E1(nc.sbuf_tensor(uname("zt2"), [128, 1024], F32))
                          for _ in range(2)]
                    gt = [E1(nc.sbuf_tensor(uname("gt"), [128, 1024], F32))
                          for _ in range(2)]
                    ssm_t = E1(nc.sbuf_tensor(uname("ssm_t"), [128, 512], F32))
                    t1 = E1(nc.sbuf_tensor(uname("t1"), [128, 512], F32))
                    x1i_t = [E1(nc.sbuf_tensor(uname("x1i_t"), [128, 1024], F32))
                             for _ in range(2)]
                    ptr = [E1(nc.psum_tensor(uname("ptr2"), [128, 1024], BF16))
                           for _ in range(2)]
                    pat = [E1(nc.psum_tensor(uname("pat"), [128, 512], F32))
                           for _ in range(2)]
                    pss = [E1(nc.psum_tensor(uname("pss"), [128, 512], F32))
                           for _ in range(2)]
                    E1(nc.Block())
                    woe = q.dma("g", wo_bf[:, :, :],
                                W["WO"].rearrange("c p d -> p c d"))
                    boe = q.dma("g", bo_bf[:, :], wdict["bO"].unsqueeze(0))
                    aode = {}
                    tre = {}
                    aoTe = {}
                    x1de = {}
                    cmbe = {}
                    ate = {}
                    sse2 = {}
                    for ti in range(NT):
                        gi = b * NT + ti
                        d1 = q.dma("y", aot[ti % 2][:, :], aov[gi],
                                   deps=[tre.get((ti - 2, 7))])
                        d2 = q.dma("y", zt[ti % 2][:, :], zv[gi],
                                   deps=[cmbe.get((ti - 2, 1))])
                        d3 = q.dma("g", gt[ti % 2][:, :], gv[gi],
                                   deps=[cmbe.get((ti - 2, 1))])
                        for c in range(8):
                            tre[(ti, c)] = q.tr(
                                ptr[ti % 2][:, c * 128:(c + 1) * 128],
                                aot[ti % 2][:, c * 128:(c + 1) * 128],
                                ident_bf[:, :],
                                deps=([d1, aoTe.get(ti - 2)] if c == 0
                                      else ()))
                        aoTe[ti] = q.op(
                            "v", "tensor_copy", aoT[ti % 2][:, :, :],
                            ptr[ti % 2][:, 0:1024].rearrange(
                                "p (c n) -> p c n", n=128),
                            deps=[tre[(ti, 7)]])
                        for ds in range(2):
                            mms = [(pat[ds][:, :],
                                    aoT[ti % 2][:, c, 0:128],
                                    wo_bf[:, c, ds * 512:(ds + 1) * 512])
                                   for c in range(8)]
                            mms.append((pat[ds][:, :], ones_bf[:, :],
                                        bo_bf[:, ds * 512:(ds + 1) * 512]))
                            dep = [aoTe[ti], cmbe.get((ti - 1, ds))]
                            if ti == 0:
                                dep.append(woe)
                                dep.append(boe)
                            ate[(ti, ds)] = q.mm_group(mms, deps=dep)
                            sse2[(ti, ds)] = q.mm(
                                pss[ds][:, :],
                                stT_bf[0:64, ti * 128:(ti + 1) * 128],
                                cw_bf[0:64, ds * 512:(ds + 1) * 512],
                                start=True, stop=True,
                                deps=[sce_scan if ti == 0 and ds == 0 else None,
                                      cwe if ti == 0 and ds == 0 else None,
                                      cmbe.get((ti - 1, ds))])
                            e1 = q.op("v", "tensor_copy", ssm_t[:, :],
                                      pss[ds][:, :], deps=[sse2[(ti, ds)]])
                            e2 = q.op("v", "tensor_sub", t1[:, :],
                                      pat[ds][:, :], ssm_t[:, :],
                                      deps=[ate[(ti, ds)]])
                            q.op("v", "tensor_mul", t1[:, :], t1[:, :],
                                 gt[ti % 2][:, ds * 512:(ds + 1) * 512],
                                 deps=[d3])
                            q.op("v", "tensor_add", ssm_t[:, :], ssm_t[:, :],
                                 zt[ti % 2][:, ds * 512:(ds + 1) * 512],
                                 deps=[d2])
                            cmbe[(ti, ds)] = q.op(
                                "v", "tensor_add",
                                x1i_t[ti % 2][:, ds * 512:(ds + 1) * 512],
                                t1[:, :], ssm_t[:, :],
                                deps=[x1de.get(ti - 2)])
                        x1de[ti] = q.dma("y", x1iv[gi], x1i_t[ti % 2][:, :],
                                         deps=[cmbe[(ti, 1)]])
                    q.dma_fence()
        stAttn.close()

        with ExitStack() as stm:
            Em = stm.enter_context
            w1_bf = Em(nc.sbuf_tensor(uname("w1_bf"), [128, 8, 4096], BF16))
            w2_bf = Em(nc.sbuf_tensor(uname("w2_bf"), [128, 32, 1024], BF16))
            b1_sb = Em(nc.sbuf_tensor(uname("b1_sb"), [128, 32], F32))
            b2_rep = Em(nc.sbuf_tensor(uname("b2_rep"), [128, 1024], F32))
            stLN = ExitStack()
            ln2g = stLN.enter_context(nc.sbuf_tensor(uname("ln2g"), [128, 1024], F32))
            ln2b = stLN.enter_context(nc.sbuf_tensor(uname("ln2b"), [128, 1024], F32))
            with ExitStack() as st1:
                E1 = st1.enter_context
                E1(nc.Block())
                l2ge = q.dma("y", ln2g[:, :],
                             wdict["ln2_g"].unsqueeze(0).broadcast_to(
                                 (128, 1024)))
                l2be = q.dma("y", ln2b[:, :],
                             wdict["ln2_b"].unsqueeze(0).broadcast_to(
                                 (128, 1024)))
                w1we = q.dma("g", w1_bf[:, :, :],
                             W["W1"].rearrange("c p d -> p c d"))
                w2we = q.dma("g", w2_bf[:, :, :],
                             w23.rearrange("c p d -> p c d"))
                b2we = q.dma("y", b2_rep[:, :],
                             wdict["b2"].unsqueeze(0).broadcast_to(
                                 (128, 1024)))
                b1we = []
                for mt in range(32):
                    b1we.append(q.dma(
                        "y", b1_sb[:, mt:mt + 1],
                        wdict["b1"][mt * 128:(mt + 1) * 128].unsqueeze(1)))

            with ExitStack() as st1:
                E1 = st1.enter_context
                hT = E1(nc.sbuf_tensor(uname("h2T"), [128, 8, 2048], BF16))
                xt = [E1(nc.sbuf_tensor(uname("xt2"), [128, 1024], F32))
                      for _ in range(2)]
                zt = E1(nc.sbuf_tensor(uname("zt3"), [128, 1024], F32))
                hbf = [E1(nc.sbuf_tensor(uname("hbf2"), [128, 1024], BF16))
                       for _ in range(2)]
                st6 = [E1(nc.sbuf_tensor(uname("st62"), [128, 12], F32))
                       for _ in range(2)]
                mv = [E1(nc.sbuf_tensor(uname("mv2"), [128, 4], F32))
                      for _ in range(2)]
                rstd = [E1(nc.sbuf_tensor(uname("rstd2"), [128, 2], F32))
                        for _ in range(2)]
                ptr = [E1(nc.psum_tensor(uname("ptr3"), [128, 1024], BF16))
                       for _ in range(2)]
                E1(nc.Block())
                eve = {}
                tre = {}
                rce = {}
                dume = None
                for i in range(33):
                    if i < 32:
                        de = q.dma("y", xt[i % 2][:, :], x1iv[i],
                                   deps=[eve.get(i - 2)])
                        rce[i] = ln_stats_comb(q, xt[i % 2][:, :], st6[i % 2],
                                               mv[i % 2], rstd[i % 2], eps, de)
                    if i > 0:
                        j = i - 1
                        half, ih = j // 16, j % 16
                        q.op("v", "tensor_scalar", zt[:, :], xt[j % 2][:, :],
                             mv[j % 2][:, 0:1], rstd[j % 2][:, 1:2],
                             OP.subtract, OP.mult, deps=[rce[j][1]])
                        q.op("v", "tensor_mul", zt[:, :], zt[:, :],
                             ln2g[:, :], deps=[l2ge] if j == 0 else ())
                        he = q.op("v", "tensor_add", hbf[j % 2][:, :],
                                  zt[:, :], ln2b[:, :],
                                  deps=[l2be] if j == 0 else ())
                        for c in range(8):
                            tre[(j, c)] = q.tr(
                                ptr[j % 2][:, c * 128:(c + 1) * 128],
                                hbf[j % 2][:, c * 128:(c + 1) * 128],
                                ident_bf[:, :],
                                deps=([he, eve.get(j - 2)] if c == 0 else ()))
                        eve[j] = q.op(
                            "v", "tensor_copy",
                            hT[:, :, ih * 128:(ih + 1) * 128],
                            ptr[j % 2][:, 0:1024].rearrange(
                                "p (c n) -> p c n", n=128),
                            deps=[tre[(j, 7)]] +
                            ([dume] if ih == 0 and dume else []))
                        if ih == 15:
                            dume = q.dma(
                                "y",
                                hT_hbm[:, :, half * 2048:(half + 1) * 2048],
                                hT[:, :, :], deps=[eve[j], dume])
                q.dma_fence()
            stLN.close()

            with ExitStack() as st1:
                E1 = st1.enter_context
                hT = E1(nc.sbuf_tensor(uname("h2T2"), [128, 8, 1024], BF16))
                h3 = E1(nc.sbuf_tensor(uname("h3"), [128, 32, 512], BF16))
                x2_t = [E1(nc.sbuf_tensor(uname("x2_t"), [128, 1024], F32))
                        for _ in range(1)]
                x1i_t2 = [E1(nc.sbuf_tensor(uname("x1i_t2"), [128, 1024], F32))
                          for _ in range(1)]
                out_t = [E1(nc.sbuf_tensor(uname("out_t2"), [128, 1024], F32))
                         for _ in range(1)]
                ps1 = [E1(nc.psum_tensor(uname("ps1b"), [128, 512], F32))
                       for _ in range(2)]
                ps2 = [E1(nc.psum_tensor(uname("ps2b"), [128, 512], F32))
                       for _ in range(4)]
                E1(nc.Block())
                k1 = 0
                k2 = 0
                gele = {}
                w2le = {}
                combe = {}
                dmae = {}
                hle = None
                for half in range(4):
                    hle = q.dma("y", hT[:, :, :],
                                hT_hbm[:, :, half * 1024:(half + 1) * 1024],
                                deps=[w2le.get(k2 - 1), hle])
                    for sl in range(2):
                        tok0 = half * 1024 + sl * 512
                        slab_gel = None
                        for mt in range(32):
                            dep = [gele.get(k1 - 2)]
                            if mt == 0:
                                dep.append(hle)
                                dep.append(w2le.get(k2 - 4))
                                if k1 == 0:
                                    dep.append(w1we)
                            ge_ = q.mm_group([
                                (ps1[k1 % 2][:, :],
                                 w1_bf[:, c, mt * 128:(mt + 1) * 128],
                                 hT[:, c, sl * 512:(sl + 1) * 512])
                                for c in range(8)], deps=dep)
                            gele[k1] = q.op("s", "activation", h3[:, mt, :],
                                            ps1[k1 % 2][:, :], AF.Gelu,
                                            bias=b1_sb[:, mt:mt + 1],
                                            deps=[ge_] + (b1we if k1 == 0
                                                          else []))
                            slab_gel = gele[k1]
                            k1 += 1
                        for ti in range(4):
                            gi = (tok0 // 128) + ti
                            xde = q.dma("y", x2_t[0][:, :], x3v[gi],
                                        deps=[combe.get((gi - 1, 1))])
                            xie = q.dma("y", x1i_t2[0][:, :], x1iv[gi],
                                        deps=[combe.get((gi - 1, 1))])
                            for ds in range(2):
                                mms = [
                                    (ps2[k2 % 4][:, :],
                                     h3[:, fc, ti * 128:(ti + 1) * 128],
                                     w2_bf[:, fc, ds * 512:(ds + 1) * 512])
                                    for fc in range(32)]
                                dep = [combe.get((gi - 2, ds))]
                                if ti == 0 and ds == 0:
                                    dep.append(slab_gel)
                                    if k2 == 0:
                                        dep.append(w2we)
                                w2le[k2] = q.mm_group(mms, deps=dep)
                                c1 = q.op("v", "scalar_tensor_tensor",
                                          out_t[0][:, ds * 512:(ds + 1) * 512],
                                          ps2[k2 % 4][:, :], 1.0,
                                          x2_t[0][:, ds * 512:(ds + 1) * 512],
                                          OP.mult, OP.add,
                                          deps=[w2le[k2], xde, xie,
                                                dmae.get(gi - 1)] +
                                          ([b2we] if k2 == 0 else []))
                                c2 = q.op("v", "scalar_tensor_tensor",
                                          out_t[0][:, ds * 512:(ds + 1) * 512],
                                          b2_rep[:, ds * 512:(ds + 1) * 512],
                                          1.0,
                                          out_t[0][:, ds * 512:(ds + 1) * 512],
                                          OP.mult, OP.add)
                                combe[(gi, ds)] = q.op(
                                    "v", "tensor_add",
                                    out_t[0][:, ds * 512:(ds + 1) * 512],
                                    out_t[0][:, ds * 512:(ds + 1) * 512],
                                    x1i_t2[0][:, ds * 512:(ds + 1) * 512])
                                k2 += 1
                            dmae[gi] = q.dma("y", xov[gi],
                                             out_t[0][:, :],
                                             deps=[combe[(gi, 1)]])
                q.dma_fence()
    return nc


def build_l1b(nc, q, x3, router_w, w1, b1, w2, b2, out, consts,
              hT_hbm, w1bf_hbm, w2bf_hbm, crow_hbm, stage_ev=None):
    ident_bf, ident_f32, ones_bf, ones_f32, cpool = consts
    eps = cpool[:, 0:1]
    NT = 32
    E_ = 8

    x3v = x3.rearrange("(i p) d -> i p d", p=128)
    outv = out.rearrange("(i p) d -> i p d", p=128)

    with ExitStack() as st0:
        E = st0.enter_context
        rwT_bf = E(nc.sbuf_tensor(uname("rwT_bf"), [128, 8, 8], BF16))
        c_sb = E(nc.sbuf_tensor(uname("c_sb"), [128, 32, 8], F32))
        c8 = E(nc.sbuf_tensor(uname("c8"), [128, 8], F32))
        b2all_bf = E(nc.sbuf_tensor(uname("b2all_bf"), [8, 1024], BF16))

        if stage_ev is None:
            stage_ev = {}
        with ExitStack() as st1:
            E1 = st1.enter_context
            rw_sb = E1(nc.sbuf_tensor(uname("rw_sb"), [8, 1024], F32))
            prt = E1(nc.psum_tensor(uname("prt"), [128, 8], F32))
            E1(nc.Block())
            if not stage_ev:
                for e in range(E_):
                    e1 = q.dma("g", w1bf_hbm[e], w1[e])
                    e2 = q.dma("g", w2bf_hbm[e], w2[e])
                    stage_ev[e] = (e1, e2)
            rde = q.dma("y", rw_sb[:, :], router_w)
            pv = None
            for c in range(8):
                t_ = q.tr(prt[:, :], rw_sb[:, c * 128:(c + 1) * 128],
                          ident_f32[0:8, 0:8], deps=[rde, pv])
                pv = q.op("s", "activation", rwT_bf[:, c, :], prt[:, :],
                          AF.Identity, deps=[t_])

        with ExitStack() as st1:
            E1 = st1.enter_context
            cT_sb = E1(nc.sbuf_tensor(uname("cT_sb"), [8, 4096], BF16))
            hT = E1(nc.sbuf_tensor(uname("hTq"), [128, 8, 2048], BF16))
            oinit = [E1(nc.sbuf_tensor(uname("oinit"), [128, 1024], F32))
                     for _ in range(2)]
            xt = [E1(nc.sbuf_tensor(uname("xt3"), [128, 1024], F32))
                  for _ in range(2)]
            zt = E1(nc.sbuf_tensor(uname("zt4"), [128, 1024], F32))
            hbf = E1(nc.sbuf_tensor(uname("hbf3"), [128, 1024], BF16))
            st6 = E1(nc.sbuf_tensor(uname("st63"), [128, 12], F32))
            mv = E1(nc.sbuf_tensor(uname("mv3"), [128, 4], F32))
            rstd = E1(nc.sbuf_tensor(uname("rstd3"), [128, 2], F32))
            lgt = E1(nc.sbuf_tensor(uname("lgt"), [128, 8], F32))
            mx8 = E1(nc.sbuf_tensor(uname("mx8"), [128, 8], F32))
            sc8 = E1(nc.sbuf_tensor(uname("sc8"), [128, 16], F32))
            junk = E1(nc.sbuf_tensor(uname("junk"), [128, 8], F32))
            ptr = [E1(nc.psum_tensor(uname("ptr4"), [128, 1024], BF16))
                   for _ in range(2)]
            plg = E1(nc.psum_tensor(uname("plg"), [128, 8], F32))
            pcr = [E1(nc.psum_tensor(uname("pcr"), [8, 128], F32))
                   for _ in range(2)]
            pini = E1(nc.psum_tensor(uname("pini"), [128, 1024], F32))
            E1(nc.Block())
            b2de = q.dma("g", b2all_bf[:, :], b2)
            lne = {}
            oevi = {}
            sse = {}
            eve = {}
            tre = {}
            oie = {}
            lge = {}
            tope = {}
            cre = {}
            dume = None
            for i in range(NT):
                half, ih = i // 16, i % 16
                de = q.dma("y", xt[i % 2][:, :], x3v[i],
                           deps=[lne.get(i - 2), oevi.get(i - 2)])
                r1e, _ = ln_stats_comb(q, xt[i % 2][:, :], st6, mv, rstd,
                                       zt[:, :][0:128, 0:1] if False else eps,
                                       de) if False else ln_stats_comb(
                                           q, xt[i % 2][:, :], st6, mv, rstd,
                                           eps, de)
                cpe = q.op("v", "tensor_scalar", hbf[:, :], xt[i % 2][:, :],
                           mv[:, 0:1], rstd[:, 0:1], OP.subtract, OP.mult,
                           deps=[r1e])
                lne[i] = cpe
                for c in range(8):
                    tre[(i, c)] = q.tr(
                        ptr[i % 2][:, c * 128:(c + 1) * 128],
                        hbf[:, c * 128:(c + 1) * 128], ident_bf[:, :],
                        deps=([cpe, eve.get(i - 2)] if c == 0 else ()))
                eve[i] = q.op(
                    "v", "tensor_copy", hT[:, :, ih * 128:(ih + 1) * 128],
                    ptr[i % 2][:, 0:1024].rearrange("p (c n) -> p c n", n=128),
                    deps=[tre[(i, 7)]] + ([dume] if ih == 0 and dume else []))
                lge[i] = q.mm_group([
                    (plg[:, :], hT[:, c, ih * 128:(ih + 1) * 128],
                     rwT_bf[:, c, :]) for c in range(8)],
                    deps=[eve[i], tope.get(i - 1)])
                le = q.op("s", "activation", lgt[:, :], plg[:, :], AF.Identity,
                          deps=[lge[i], tope.get(i - 1)])
                q.op("v", "max", mx8[:, :], lgt[:, :], deps=[le])
                q.op("v", "tensor_scalar", sc8[:, 0:1], mx8[:, 0:1], -1.0,
                     None, OP.mult)
                ze = q.op("s", "activation", junk[:, :], lgt[:, :], AF.Exp,
                          bias=sc8[:, 0:1], accum_out=sc8[:, 1:2],
                          deps=[(q.ch["v"][0], q.ch["v"][1])])
                de2 = q.op("v", "tensor_sub", sc8[:, 2:3], mx8[:, 1:2],
                           mx8[:, 0:1])
                e2e = q.op("s", "activation", sc8[:, 3:4], sc8[:, 2:3],
                           AF.Exp, deps=[de2])
                q.op("v", "reciprocal", sc8[:, 4:5], sc8[:, 1:2], deps=[ze])
                q.op("v", "tensor_mul", sc8[:, 5:6], sc8[:, 3:4], sc8[:, 4:5],
                     deps=[e2e])
                q.op("v", "tensor_add", sc8[:, 6:7], sc8[:, 4:5], sc8[:, 5:6])
                q.op("v", "tensor_scalar", sc8[:, 6:7], sc8[:, 6:7], 1e-9,
                     None, OP.add)
                q.op("v", "reciprocal", sc8[:, 7:8], sc8[:, 6:7])
                q.op("v", "tensor_mul", sc8[:, 8:9], sc8[:, 4:5], sc8[:, 7:8])
                q.op("v", "tensor_mul", sc8[:, 9:10], sc8[:, 5:6], sc8[:, 7:8])
                q.op("v", "tensor_scalar", c8[:, :], lgt[:, :], mx8[:, 0:1],
                     None, OP.is_equal)
                q.op("v", "tensor_scalar", c8[:, :], c8[:, :], sc8[:, 8:9],
                     None, OP.mult)
                q.op("v", "tensor_scalar", junk[:, :], lgt[:, :], mx8[:, 1:2],
                     None, OP.is_equal)
                q.op("v", "tensor_scalar", junk[:, :], junk[:, :],
                     sc8[:, 9:10], None, OP.mult)
                tope[i] = q.op("v", "tensor_add", c_sb[:, i, :], c8[:, :],
                               junk[:, :])
                ct = q.tr(pcr[i % 2][:, :], c_sb[:, i, :], ident_f32[:, :],
                          deps=[tope[i], cre.get(i - 2)])
                cre[i] = q.op("s", "activation",
                              cT_sb[:, i * 128:(i + 1) * 128],
                              pcr[i % 2][0:8, :], AF.Identity, deps=[ct])
                ibe = q.mm_group([(pini[:, 0:512],
                                   cT_sb[:, i * 128:(i + 1) * 128],
                                   b2all_bf[:, 0:512])],
                                 deps=[cre[i], oevi.get(i - 2),
                                       b2de if i == 0 else None])
                ibe2 = q.mm_group([(pini[:, 512:1024],
                                    cT_sb[:, i * 128:(i + 1) * 128],
                                    b2all_bf[:, 512:1024])])
                ov = q.op("v", "tensor_add", oinit[i % 2][:, :],
                          pini[:, :].rearrange("p n -> p n"), xt[i % 2][:, :],
                          deps=[ibe2, oie.get(i - 2)])
                oevi[i] = ov
                oie[i] = q.dma("y", outv[i], oinit[i % 2][:, :], deps=[ov])
                if ih == 15:
                    dume = q.dma("y",
                                 hT_hbm[:, :, half * 2048:(half + 1) * 2048],
                                 hT[:, :, :], deps=[eve[i], dume])
            q.dma_fence()

        with ExitStack() as st1:
            E1 = st1.enter_context
            w1T = E1(nc.sbuf_tensor(uname("w1T"), [128, 8, 4096], BF16))
            w2T = E1(nc.sbuf_tensor(uname("w2T"), [128, 32, 1024], BF16))
            hT = [E1(nc.sbuf_tensor(uname("hTq2"), [128, 8, 1024], BF16))
                  for _ in range(2)]
            h2cT = E1(nc.sbuf_tensor(uname("h2cT"), [128, 32, 512], BF16))
            b1c = E1(nc.sbuf_tensor(uname("b1c"), [128, 32], F32))
            out_sb = [E1(nc.sbuf_tensor(uname("out_sb"), [128, 1024], F32))
                      for _ in range(2)]
            ps1 = [E1(nc.psum_tensor(uname("ps1c"), [128, 512], F32))
                   for _ in range(3)]
            ps2 = [E1(nc.psum_tensor(uname("ps2c"), [128, 512], F32))
                   for _ in range(4)]
            E1(nc.Block())
            k1 = 0
            k2 = 0
            rele = {}
            w2le = {}
            oevac = {}
            acc_ev = {}
            prev_acc = None
            hT_free = {}
            hle = None
            w1_free = None
            w2_free = None
            misc_free = None
            for e in range(E_):
                wde = []
                for cc in range(8):
                    wde.append(q.dma("y", w1T[:, cc, :],
                                     w1bf_hbm[e][:, cc * 128:(cc + 1) * 128],
                                     transpose=True,
                                     deps=[stage_ev[e][0], w1_free]
                                     if cc == 0 else [stage_ev[e][0]]))
                w2de = []
                for fc in range(32):
                    w2de.append(q.dma("y", w2T[:, fc, :],
                                      w2bf_hbm[e][:, fc * 128:(fc + 1) * 128],
                                      transpose=True,
                                      deps=[stage_ev[e][1], w2_free]
                                      if fc == 0 else [stage_ev[e][1]]))
                b1de = []
                for mt in range(32):
                    b1de.append(q.dma(
                        "y", b1c[:, mt:mt + 1],
                        b1[e][mt * 128:(mt + 1) * 128].unsqueeze(1),
                        deps=[misc_free] if mt == 0 else ()))
                for qtr in range(4):
                    gq = e * 4 + qtr
                    hle_prev = hle
                    hle = q.dma("y", hT[gq % 2][:, :, :],
                                hT_hbm[:, :, qtr * 1024:(qtr + 1) * 1024],
                                deps=[hT_free.get(gq - 2), hle_prev])
                    if gq == 0:
                        hcur = hle
                    for sl in range(2):
                        tok0 = qtr * 1024 + sl * 512
                        sloc = sl * 512
                        for mt in range(32):
                            dep = [rele.get(k1 - 3)]
                            if mt == 0:
                                dep.append(wde)
                                dep.append(hle)
                                dep.append(b1de)
                                dep.append(w2le.get(k2 - 4))
                            mms = [
                                (ps1[k1 % 3][:, :],
                                 w1T[:, c, mt * 128:(mt + 1) * 128],
                                 hT[gq % 2][:, c, sloc:sloc + 512])
                                for c in range(8)]
                            ge_ = q.mm_group(mms, deps=dep)
                            rele[k1] = q.op("s", "activation", h2cT[:, mt, :],
                                            ps1[k1 % 3][:, :], AF.Relu,
                                            bias=b1c[:, mt:mt + 1],
                                            deps=[ge_])
                            last_rel = rele[k1]
                            k1 += 1
                        w1_free = ge_
                        if sl == 1:
                            hT_free[gq] = ge_
                        for ti in range(4):
                            gi = (tok0 // 128) + ti
                            for ds in range(2):
                                mms = [
                                    (ps2[k2 % 4][:, :],
                                     h2cT[:, fc, ti * 128:(ti + 1) * 128],
                                     w2T[:, fc, ds * 512:(ds + 1) * 512])
                                    for fc in range(32)]
                                dep = [oevac.get(k2 - 4)]
                                if ti == 0 and ds == 0:
                                    dep.append(last_rel)
                                    dep.append(w2de)
                                w2le[k2] = q.mm_group(mms, deps=dep)
                                oevac[k2] = q.op(
                                    "s", "activation",
                                    out_sb[gi % 2][:, ds * 512:(ds + 1) * 512],
                                    ps2[k2 % 4][:, :], AF.Identity,
                                    scale=c_sb[:, gi, e:e + 1],
                                    deps=[w2le[k2], prev_acc])
                                k2 += 1
                            prev_acc = q.dma(
                                "g", outv[gi], out_sb[gi % 2][:, :],
                                accum_op=OP.add,
                                deps=[oevac[k2 - 1], acc_ev.get(gi)])
                            acc_ev[gi] = prev_acc
                        w2_free = w2le[k2 - 1]
                        misc_free = w2le[k2 - 1]
            q.dma_fence()
    return nc


from concourse.bass_utils import run_bass_kernel_spmd

N_CORES = 8
TOK = 4096


def _consts_alloc(nc, q):
    st = []
    ident_bf = nc.sbuf_tensor("ident_bf", [128, 128], BF16); st.append(ident_bf)
    ident_f32 = nc.sbuf_tensor("ident_f32", [128, 128], F32); st.append(ident_f32)
    ones_bf = nc.sbuf_tensor("ones_bf", [1, 128], BF16); st.append(ones_bf)
    ones_f32 = nc.sbuf_tensor("ones_f32", [128, 128], F32); st.append(ones_f32)
    cpool = nc.sbuf_tensor("cpool", [128, 4], F32); st.append(cpool)
    objs = [t.__enter__() for t in st]
    with nc.Block() as _b:
        emit_constants(q, *objs)
    return tuple(objs)


def _final_fence(nc, q):
    with nc.Block() as _b:
        q.dma_fence()


def _prog_l0a():
    nc = bass.Bass("TRN2", target_bir_lowering=False)
    X = nc.dram_tensor("x", [TOK, 1024], F32, kind="ExternalInput").ap()
    WQ = nc.dram_tensor("Wq", [1024, 1024], F32, kind="ExternalInput").ap()
    BQ = nc.dram_tensor("bq", [1024], F32, kind="ExternalInput").ap()
    WK = nc.dram_tensor("Wk", [1024, 1024], F32, kind="ExternalInput").ap()
    BK = nc.dram_tensor("bk", [1024], F32, kind="ExternalInput").ap()
    WV = nc.dram_tensor("Wv", [1024, 8192], F32, kind="ExternalInput").ap()
    BV = nc.dram_tensor("bv", [8192], F32, kind="ExternalInput").ap()
    GM = nc.dram_tensor("gamma", [1024], F32, kind="ExternalInput").ap()
    BT = nc.dram_tensor("beta", [1024], F32, kind="ExternalInput").ap()
    ATT = nc.dram_tensor("attn", [TOK, 1024], F32, kind="ExternalOutput").ap()
    q = Par(nc)
    consts = _consts_alloc(nc, q)
    build_l0a(nc, q, X, WQ, BQ, WK, BK, WV, BV, GM, BT, ATT, consts)
    _final_fence(nc, q)
    return nc


def _prog_l0b():
    nc = bass.Bass("TRN2", target_bir_lowering=False)
    X = nc.dram_tensor("x", [TOK, 1024], F32, kind="ExternalInput").ap()
    ATT = nc.dram_tensor("attn", [TOK, 1024], F32, kind="ExternalInput").ap()
    WR = nc.dram_tensor("wr", [1024], F32, kind="ExternalInput").ap()
    W1 = nc.dram_tensor("W1", [1024, 4096], F32, kind="ExternalInput").ap()
    B1 = nc.dram_tensor("b1", [4096], F32, kind="ExternalInput").ap()
    W2 = nc.dram_tensor("W2", [4096, 1024], F32, kind="ExternalInput").ap()
    B2 = nc.dram_tensor("b2", [1024], F32, kind="ExternalInput").ap()
    X2 = nc.dram_tensor("x2", [TOK, 1024], F32, kind="ExternalOutput").ap()
    X1H = nc.dram_tensor("x1h", [TOK, 1024], F32, kind="Internal").ap()
    HTH = nc.dram_tensor("hth", [128, 8, TOK], BF16, kind="Internal").ap()
    ROW = nc.dram_tensor("rowh", [TOK], F32, kind="Internal").ap()
    q = Par(nc)
    consts = _consts_alloc(nc, q)
    build_l0b(nc, q, X, ATT, WR, W1, B1, W2, B2, X2, consts, X1H, HTH, ROW)
    _final_fence(nc, q)
    return nc


_L1A_NAMES = ["ln1_g", "ln1_b", "WQ", "bQ", "WK", "bK", "WV", "bV", "WO",
              "bO", "Wg", "bg", "A", "Bw", "Cw", "ln2_g", "ln2_b", "W1",
              "b1", "W2", "b2"]
_L1A_SHAPES = {"ln1_g": [1024], "ln1_b": [1024], "WQ": [1024, 1024],
               "bQ": [1024], "WK": [1024, 1024], "bK": [1024],
               "WV": [1024, 1024], "bV": [1024], "WO": [1024, 1024],
               "bO": [1024], "Wg": [1024, 1024], "bg": [1024], "A": [64],
               "Bw": [1024, 64], "Cw": [64, 1024], "ln2_g": [1024],
               "ln2_b": [1024], "W1": [1024, 4096], "b1": [4096],
               "W2": [4096, 1024], "b2": [1024]}


def _prog_l1a():
    nc = bass.Bass("TRN2", target_bir_lowering=False)
    X2 = nc.dram_tensor("x2", [TOK, 1024], F32, kind="ExternalInput").ap()
    dram = {n: nc.dram_tensor(n, _L1A_SHAPES[n], F32,
                              kind="ExternalInput").ap() for n in _L1A_NAMES}
    X3 = nc.dram_tensor("x3", [TOK, 1024], F32, kind="ExternalOutput").ap()
    ZH = nc.dram_tensor("zh", [TOK, 1024], F32, kind="Internal").ap()
    GH = nc.dram_tensor("gh", [TOK, 1024], BF16, kind="Internal").ap()
    AOH = nc.dram_tensor("aoh", [TOK, 1024], BF16, kind="Internal").ap()
    X1IH = nc.dram_tensor("x1ih", [TOK, 1024], F32, kind="Internal").ap()
    HTH = nc.dram_tensor("hth", [128, 8, TOK], BF16, kind="Internal").ap()
    q = Par(nc)
    consts = _consts_alloc(nc, q)
    build_l1a(nc, q, X2, dram, X3, consts, ZH, GH, AOH, X1IH, HTH)
    _final_fence(nc, q)
    return nc


def _prog_l1b():
    nc = bass.Bass("TRN2", target_bir_lowering=False)
    X3 = nc.dram_tensor("x3", [TOK, 1024], F32, kind="ExternalInput").ap()
    RW = nc.dram_tensor("rw", [8, 1024], F32, kind="ExternalInput").ap()
    W1 = nc.dram_tensor("w1", [8, 4096, 1024], F32, kind="ExternalInput").ap()
    B1 = nc.dram_tensor("b1", [8, 4096], F32, kind="ExternalInput").ap()
    W2 = nc.dram_tensor("w2", [8, 1024, 4096], F32, kind="ExternalInput").ap()
    B2 = nc.dram_tensor("b2", [8, 1024], F32, kind="ExternalInput").ap()
    OUT = nc.dram_tensor("out", [TOK, 1024], F32, kind="ExternalOutput").ap()
    HTH = nc.dram_tensor("hth", [128, 8, TOK], BF16, kind="Internal").ap()
    W1BF = nc.dram_tensor("w1bf", [8, 4096, 1024], BF16, kind="Internal").ap()
    W2BF = nc.dram_tensor("w2bf", [8, 1024, 4096], BF16, kind="Internal").ap()
    CROW = nc.dram_tensor("crow", [8, TOK], BF16, kind="Internal").ap()
    q = Par(nc)
    consts = _consts_alloc(nc, q)
    build_l1b(nc, q, X3, RW, W1, B1, W2, B2, OUT, consts,
              HTH, W1BF, W2BF, CROW)
    _final_fence(nc, q)
    return nc


def _prog_fused():
    nc = bass.Bass("TRN2", target_bir_lowering=False)
    I = lambda name, shape: nc.dram_tensor(name, shape, F32,
                                           kind="ExternalInput").ap()
    X = I("x", [TOK, 1024])
    WQ0, BQ0 = I("l0Wq", [1024, 1024]), I("l0bq", [1024])
    WK0, BK0 = I("l0Wk", [1024, 1024]), I("l0bk", [1024])
    WV0, BV0 = I("l0Wv", [1024, 8192]), I("l0bv", [8192])
    GM0, BT0 = I("l0gamma", [1024]), I("l0beta", [1024])
    WRm, W1m, B1m = I("mwr", [1024]), I("mW1", [1024, 4096]), I("mb1", [4096])
    W2m, B2m = I("mW2", [4096, 1024]), I("mb2", [1024])
    dram1 = {n: I("a_" + n, _L1A_SHAPES[n]) for n in _L1A_NAMES}
    RW = I("erw", [8, 1024])
    W1e, B1e = I("ew1", [8, 4096, 1024]), I("eb1", [8, 4096])
    W2e, B2e = I("ew2", [8, 1024, 4096]), I("eb2", [8, 1024])
    OUT = nc.dram_tensor("out", [TOK, 1024], F32, kind="ExternalOutput").ap()

    S = lambda name, shape, dt=F32: nc.dram_tensor(name, shape, dt,
                                                   kind="Internal").ap()
    ATT = S("s_att", [TOK, 1024])
    X2 = S("s_x2", [TOK, 1024])
    X3 = S("s_x3", [TOK, 1024])
    X1H = S("s_x1h", [TOK, 1024])
    HTH = S("s_hth", [128, 8, TOK], BF16)
    ROW = S("s_row", [TOK])
    ZH = S("s_zh", [TOK, 1024])
    GH = S("s_gh", [TOK, 1024], BF16)
    AOH = S("s_aoh", [TOK, 1024], BF16)
    X1IH = S("s_x1ih", [TOK, 1024])
    W1BF = S("s_w1bf", [8, 4096, 1024], BF16)
    W2BF = S("s_w2bf", [8, 1024, 4096], BF16)
    CROW = S("s_crow", [8, TOK], BF16)

    q = Par(nc)
    consts = _consts_alloc(nc, q)
    stage_ev = {}
    with nc.Block() as _bstg:
        for e in range(8):
            e1 = q.dma("g", W1BF[e], W1e[e], pool="stg")
            e2 = q.dma("g", W2BF[e], W2e[e], pool="stg")
            stage_ev[e] = (e1, e2)
    build_l0a(nc, q, X, WQ0, BQ0, WK0, BK0, WV0, BV0, GM0, BT0, ATT, consts)
    build_l0b(nc, q, X, ATT, WRm, W1m, B1m, W2m, B2m, X2, consts,
              X1H, HTH, ROW)
    build_l1a(nc, q, X2, dram1, X3, consts, ZH, GH, AOH, X1IH, HTH)
    build_l1b(nc, q, X3, RW, W1e, B1e, W2e, B2e, OUT, consts,
              HTH, W1BF, W2BF, CROW, stage_ev=stage_ev)
    _final_fence(nc, q)
    return nc


FUSED = True


def _kernel_fused(inp):
    x = inp["L0_attn_x"].reshape(N_CORES, TOK, 1024)
    nc = _prog_fused()
    base = {"l0Wq": inp["L0_attn_Wq"], "l0bq": inp["L0_attn_bq"],
            "l0Wk": inp["L0_attn_Wk"], "l0bk": inp["L0_attn_bk"],
            "l0Wv": inp["L0_attn_Wv"], "l0bv": inp["L0_attn_bv"],
            "l0gamma": inp["L0_attn_gamma"], "l0beta": inp["L0_attn_beta"],
            "mwr": inp["L0_ffn_w_router"], "mW1": inp["L0_ffn_W1"],
            "mb1": inp["L0_ffn_b1"], "mW2": inp["L0_ffn_W2"],
            "mb2": inp["L0_ffn_b2"],
            "erw": inp["L1_ffn_router_w"], "ew1": inp["L1_ffn_w1"],
            "eb1": inp["L1_ffn_b1"], "ew2": inp["L1_ffn_w2"],
            "eb2": inp["L1_ffn_b2"]}
    base.update({"a_" + n: inp[f"L1_attn_{n}"] for n in _L1A_NAMES})
    maps = [dict(x=x[c], **base) for c in range(N_CORES)]
    res = run_bass_kernel_spmd(nc, maps, core_ids=list(range(N_CORES)))
    out = np.stack([r["out"] for r in res.results])
    return out.reshape(16, 2048, 1024).astype(np.float32)


def kernel(**inputs):
    inp = {k: np.ascontiguousarray(v) for k, v in inputs.items()}
    if FUSED:
        return _kernel_fused(inp)
    x = inp["L0_attn_x"].reshape(N_CORES, TOK, 1024)

    nc = _prog_l0a()
    maps = [{"x": x[c], "Wq": inp["L0_attn_Wq"], "bq": inp["L0_attn_bq"],
             "Wk": inp["L0_attn_Wk"], "bk": inp["L0_attn_bk"],
             "Wv": inp["L0_attn_Wv"], "bv": inp["L0_attn_bv"],
             "gamma": inp["L0_attn_gamma"], "beta": inp["L0_attn_beta"]}
            for c in range(N_CORES)]
    res = run_bass_kernel_spmd(nc, maps, core_ids=list(range(N_CORES)))
    attn = [r["attn"] for r in res.results]

    nc = _prog_l0b()
    maps = [{"x": x[c], "attn": attn[c], "wr": inp["L0_ffn_w_router"],
             "W1": inp["L0_ffn_W1"], "b1": inp["L0_ffn_b1"],
             "W2": inp["L0_ffn_W2"], "b2": inp["L0_ffn_b2"]}
            for c in range(N_CORES)]
    res = run_bass_kernel_spmd(nc, maps, core_ids=list(range(N_CORES)))
    x2 = [r["x2"] for r in res.results]

    nc = _prog_l1a()
    wmap = {n: inp[f"L1_attn_{n}"] for n in _L1A_NAMES}
    maps = [dict(x2=x2[c], **wmap) for c in range(N_CORES)]
    res = run_bass_kernel_spmd(nc, maps, core_ids=list(range(N_CORES)))
    x3 = [r["x3"] for r in res.results]

    nc = _prog_l1b()
    maps = [{"x3": x3[c], "rw": inp["L1_ffn_router_w"],
             "w1": inp["L1_ffn_w1"], "b1": inp["L1_ffn_b1"],
             "w2": inp["L1_ffn_w2"], "b2": inp["L1_ffn_b2"]}
            for c in range(N_CORES)]
    res = run_bass_kernel_spmd(nc, maps, core_ids=list(range(N_CORES)))
    out = np.stack([r["out"] for r in res.results])
    return out.reshape(16, 2048, 1024).astype(np.float32)


# version 23
# speedup vs baseline: 1.1223x; 1.0777x over previous
import numpy as np
import concourse.bass as bass
import concourse.mybir as mybir
from contextlib import ExitStack

F32 = mybir.dt.float32
BF16 = mybir.dt.bfloat16
I32 = mybir.dt.int32
AF = mybir.ActivationFunctionType
OP = mybir.AluOpType
AX = mybir.AxisListType

LN_EPS = 1e-5
ISQ128 = 0.08838834764831845
ISQ64 = 0.125
NEG_BIG = -1e30

_uid = [0]


def uname(s):
    _uid[0] += 1
    return f"{s}_u{_uid[0]}"




def ln_stats_comb(q, x_ap, st6, mv, rstd, eps, dep_in, war=None):
    q.op("v", "bn_stats", st6[:, 0:6], x_ap[:, 0:512], deps=[dep_in])
    q.op("v", "bn_stats", st6[:, 6:12], x_ap[:, 512:1024])
    ag = q.op("v", "bn_aggr", mv[:, 0:2], st6[:, 0:12], deps=[war])
    q.op("s", "activation", mv[:, 2:3], mv[:, 1:2], AF.Ln, bias=eps,
         deps=[ag])
    r1 = q.op("s", "activation", rstd[:, 0:1], mv[:, 2:3], AF.Exp, scale=-0.5)
    q.op("v", "tensor_mul", mv[:, 3:4], rstd[:, 0:1], rstd[:, 0:1], deps=[r1])
    q.op("v", "tensor_mul", mv[:, 3:4], mv[:, 3:4], mv[:, 1:2])
    q.op("s", "activation", mv[:, 3:4], mv[:, 3:4], AF.Ln, bias=eps,
         deps=[(q.ch["v"][0], q.ch["v"][1])])
    q.op("s", "activation", mv[:, 3:4], mv[:, 3:4], AF.Exp, scale=-0.5)
    rc = q.op("v", "tensor_mul", rstd[:, 1:2], rstd[:, 0:1], mv[:, 3:4],
              deps=[(q.ch["s"][0], q.ch["s"][1])])
    return r1, rc


class Par:

    def __init__(self, nc):
        self.nc = nc
        self.cm = []
        def sem(name):
            cm = nc.semaphore(name)
            self.cm.append(cm)
            return [cm.__enter__(), 0]
        self.ch = {k: sem("ch_" + k) for k in "tvsg"}
        self.dpool = {"y": [sem(f"dqy{i}") for i in range(10)],
                      "g": [sem(f"dqg{i}") for i in range(6)],
                      "s": [sem(f"dqs{i}") for i in range(6)],
                      "stg": [sem(f"dqstg{i}") for i in range(2)]}
        self.didx = {"y": 0, "g": 0, "s": 0, "stg": 0}
        self.eng = dict(t=nc.tensor, v=nc.vector, s=nc.scalar,
                        g=nc.gpsimd, y=nc.sync)

    def _waits(self, e, deps):
        best = {}
        for d in deps:
            if d is None:
                continue
            if isinstance(d, list):
                for dd in d:
                    if dd is not None:
                        k = id(dd[0])
                        if k not in best or dd[1] > best[k][1]:
                            best[k] = dd
                continue
            k = id(d[0])
            if k not in best or d[1] > best[k][1]:
                best[k] = d
        for d in best.values():
            e.wait_ge(d[0], d[1])

    def op(self, which, name, *a, deps=(), **k):
        e = self.eng[which]
        c = self.ch[which]
        if which != "t":
            e.wait_ge(c[0], c[1])
        self._waits(e, deps)
        ins = getattr(e, name)(*a, **k)
        ins.then_inc(c[0], 1)
        c[1] += 1
        return (c[0], c[1])

    def mm(self, out, lhsT, rhs, start, stop, deps=()):
        e = self.eng["t"]
        self._waits(e, deps)
        ins = e.matmul(out, lhsT, rhs, start=start, stop=stop)
        c = self.ch["t"]
        ins.then_inc(c[0], 1)
        c[1] += 1
        return (c[0], c[1])

    def mm_group(self, mms, deps=()):
        ev = None
        for i, (out, lhsT, rhs) in enumerate(mms):
            ev = self.mm(out, lhsT, rhs, start=(i == 0),
                         stop=(i == len(mms) - 1),
                         deps=deps if i == 0 else ())
        return ev

    def tr(self, out, in_, ident, deps=()):
        e = self.eng["t"]
        self._waits(e, deps)
        ins = e.transpose(out, in_, ident)
        c = self.ch["t"]
        ins.then_inc(c[0], 1)
        c[1] += 1
        return (c[0], c[1])

    def dma(self, which, out, in_, deps=(), pool=None, **k):
        e = self.eng[which]
        qn = pool or ("g" if which == "g" else
                      ("s" if which == "s" else "y"))
        pool = self.dpool[qn]
        d = pool[self.didx[qn] % len(pool)]
        self.didx[qn] += 1
        if d[1] > 0:
            e.wait_ge(d[0], d[1])
        self._waits(e, deps)
        ins = e.dma_start(out=out, in_=in_, **k)
        ins.then_inc(d[0], 16)
        d[1] += 16
        return (d[0], d[1])

    def dma_fence(self):
        e = self.eng["y"]
        for qn, pool in self.dpool.items():
            for d in pool:
                if d[1] > 0:
                    e.wait_ge(d[0], d[1])


def emit_constants(q, ident_bf, ident_f32, ones_bf, ones_f32, cpool):
    e1 = q.op("g", "memset", ones_f32[:, :], 1.0)
    q.op("g", "memset", ones_bf[:, :], 1.0)
    q.op("g", "memset", cpool[:, 0:1], LN_EPS)
    q.op("g", "affine_select", ident_f32[:, :], ones_f32[:, :],
         pattern=[[-1, 128]], compare_op=OP.is_equal, fill=0.0,
         base=0, channel_multiplier=1, deps=[e1])
    q.op("v", "tensor_copy", ident_bf[:, :], ident_f32[:, :],
         deps=[(q.ch["g"][0], q.ch["g"][1])])


def ln_tile(q, x_ap, st6, mv, rstd, z_f32, eps, dep_in, affine=None,
            out_bf=None, war=None):
    q.op("v", "bn_stats", st6[:, 0:6], x_ap[:, 0:512], deps=[dep_in])
    q.op("v", "bn_stats", st6[:, 6:12], x_ap[:, 512:1024])
    ag = q.op("v", "bn_aggr", mv[:, 0:2], st6[:, 0:12], deps=[war])
    q.op("s", "activation", mv[:, 1:2], mv[:, 1:2], AF.Ln, bias=eps,
         deps=[ag])
    r1 = q.op("s", "activation", rstd[:, 0:1], mv[:, 1:2], AF.Exp, scale=-0.5)
    sev = r1
    ev = q.op("v", "tensor_scalar", z_f32, x_ap, mv[:, 0:1], rstd[:, 0:1],
              OP.subtract, OP.mult, deps=[r1])
    if affine is not None:
        g_rep, b_rep = affine
        q.op("v", "bn_stats", st6[:, 0:6], z_f32[:, 0:512])
        q.op("v", "bn_stats", st6[:, 6:12], z_f32[:, 512:1024])
        ag = q.op("v", "bn_aggr", mv[:, 2:4], st6[:, 0:12])
        q.op("s", "activation", mv[:, 3:4], mv[:, 3:4], AF.Ln, bias=eps,
             deps=[ag])
        r2 = q.op("s", "activation", rstd[:, 1:2], mv[:, 3:4], AF.Exp,
                  scale=-0.5)
        sev = r2
        q.op("v", "tensor_scalar", z_f32, z_f32, mv[:, 2:3], rstd[:, 1:2],
             OP.subtract, OP.mult, deps=[r2])
        q.op("v", "tensor_mul", z_f32, z_f32, g_rep)
        ev = q.op("v", "tensor_add", out_bf if out_bf is not None else z_f32,
                  z_f32, b_rep)
    elif out_bf is not None:
        ev = q.op("v", "tensor_copy", out_bf, z_f32)
    return ev, sev


def build_l0a(nc, q, x, Wq, bq, Wk, bk, Wv, bv, gamma, beta, attn_out, consts):
    ident_bf, ident_f32, ones_bf, ones_f32, cpool = consts
    eps = cpool[:, 0:1]
    NB, T, NT, H = 2, 2048, 16, 8

    x3 = x.rearrange("(i p) d -> i p d", p=128)
    wq3 = Wq.rearrange("(c p) d -> c p d", p=128)
    wk3 = Wk.rearrange("(c p) d -> c p d", p=128)
    wv3 = Wv.rearrange("(c p) d -> c p d", p=128)
    attn_flat = attn_out.rearrange("a d -> (a d)")

    with ExitStack() as st0:
        E = st0.enter_context
        wq_bf = E(nc.sbuf_tensor(uname("wq_bf"), [128, 8, 1024], BF16))
        wk_bf = E(nc.sbuf_tensor(uname("wk_bf"), [128, 8, 1024], BF16))
        wvp_bf = E(nc.sbuf_tensor(uname("wvp_bf"), [128, 8, 1024], BF16))
        bq_sb = E(nc.sbuf_tensor(uname("bq_sb"), [128, 8], F32))
        bk_sb = E(nc.sbuf_tensor(uname("bk_sb"), [128, 8], F32))
        g_rep = E(nc.sbuf_tensor(uname("g_rep"), [128, 1024], F32))
        b_rep = E(nc.sbuf_tensor(uname("b_rep"), [128, 1024], F32))
        bvp_bf = E(nc.sbuf_tensor(uname("bvp_bf"), [1, 1024], BF16))

        with ExitStack() as st1:
            E1 = st1.enter_context
            E1(nc.Block())
            wqe = q.dma("g", wq_bf[:, :, :], wq3.rearrange("c p d -> p c d"))
            wke = q.dma("g", wk_bf[:, :, :], wk3.rearrange("c p d -> p c d"))
            bqke = []
            for h in range(H):
                bqke.append(q.dma("y", bq_sb[:, h:h + 1],
                            bq[h * 128:(h + 1) * 128].unsqueeze(1)))
                bqke.append(q.dma("y", bk_sb[:, h:h + 1],
                            bk[h * 128:(h + 1) * 128].unsqueeze(1)))
            grpe = q.dma("y", g_rep[:, :],
                         gamma.unsqueeze(0).broadcast_to((128, 1024)))
            brpe = q.dma("y", b_rep[:, :],
                         beta.unsqueeze(0).broadcast_to((128, 1024)))

        pool_done = [None]

        def emit_wv_pooling(wv_buf, pool_f32, free_ev):
            dmae = {}
            addse = {}
            for cq in range(64):
                c, qq = cq // 8, cq % 8
                dmae[cq] = q.dma("s", wv_buf[cq % 2][:, :],
                                 wv3[c][:, qq * 1024:(qq + 1) * 1024],
                                 deps=[addse.get(cq - 2),
                                       free_ev if cq < 2 else None])
                wvg = wv_buf[cq % 2][:, :].rearrange("p (g e) -> p e g", e=8)
                dst = pool_f32[:, qq * 128:(qq + 1) * 128]
                q.op("v", "tensor_add", dst, wvg[:, 0, :], wvg[:, 1, :],
                     deps=[dmae[cq]])
                for j in range(2, 8):
                    addse[cq] = q.op("v", "tensor_add", dst, dst,
                                     wvg[:, j, :])
                if qq == 7:
                    pool_done[0] = q.op("v", "tensor_copy", wvp_bf[:, c, :],
                                        pool_f32[:, :])
            for p in range(8):
                cq = 64 + p
                dmae[cq] = q.dma("s", wv_buf[cq % 2][0:1, :],
                                 bv[p * 1024:(p + 1) * 1024].unsqueeze(0),
                                 deps=[addse.get(cq - 2)])
                bvg = wv_buf[cq % 2][0:1, :].rearrange("p (g e) -> p e g", e=8)
                dst = pool_f32[0:1, p * 128:(p + 1) * 128]
                q.op("v", "tensor_add", dst, bvg[:, 0, :], bvg[:, 1, :],
                     deps=[dmae[cq]])
                for j in range(2, 8):
                    addse[cq] = q.op("v", "tensor_add", dst, dst,
                                     bvg[:, j, :])
            pool_done[0] = q.op("v", "tensor_copy", bvp_bf[:, :],
                                pool_f32[0:1, 0:1024])

        for b in range(NB):
            with ExitStack() as st2:
                E2 = st2.enter_context
                qT = E2(nc.sbuf_tensor(uname("qT"), [128, 8, T], BF16))
                kT = E2(nc.sbuf_tensor(uname("kT"), [128, 8, T], BF16))
                vp = E2(nc.sbuf_tensor(uname("vp"), [128, 16, 1024], BF16))
                stAB = ExitStack()
                xnT = stAB.enter_context(
                    nc.sbuf_tensor(uname("xnT"), [128, 8, T], BF16))

                with ExitStack() as st3:
                    E3 = st3.enter_context
                    xt = [E3(nc.sbuf_tensor(uname("xt"), [128, 1024], F32))
                          for _ in range(2)]
                    zt = E3(nc.sbuf_tensor(uname("zt"), [128, 1024], F32))
                    xnbf = [E3(nc.sbuf_tensor(uname("xnbf"), [128, 1024], BF16))
                            for _ in range(2)]
                    st6 = [E3(nc.sbuf_tensor(uname("st6"), [128, 12], F32))
                           for _ in range(2)]
                    mv = [E3(nc.sbuf_tensor(uname("mv"), [128, 4], F32))
                          for _ in range(2)]
                    rstd = [E3(nc.sbuf_tensor(uname("rstd"), [128, 2], F32))
                            for _ in range(2)]
                    ptr = [E3(nc.psum_tensor(uname("ptr"), [128, 1024], BF16))
                           for _ in range(2)]
                    E3(nc.Block())
                    lne = {}
                    tre = {}
                    eve = {}
                    rce = {}
                    for i in range(NT + 1):
                        if i < NT:
                            de = q.dma("y", xt[i % 2][:, :], x3[b * NT + i],
                                       deps=[lne.get(i - 2)])
                            _, rce[i] = ln_stats_comb(
                                q, xt[i % 2][:, :], st6[i % 2], mv[i % 2],
                                rstd[i % 2], eps, de)
                        if i > 0:
                            j = i - 1
                            q.op("v", "tensor_scalar", zt[:, :],
                                 xt[j % 2][:, :], mv[j % 2][:, 0:1],
                                 rstd[j % 2][:, 1:2], OP.subtract, OP.mult,
                                 deps=[rce[j]])
                            q.op("v", "tensor_mul", zt[:, :], zt[:, :],
                                 g_rep[:, :], deps=[grpe] if j == 0 else ())
                            lne[j] = q.op("v", "tensor_add", xnbf[j % 2][:, :],
                                          zt[:, :], b_rep[:, :],
                                          deps=[brpe] if j == 0 else ())
                            for c in range(8):
                                tre[(j, c)] = q.tr(
                                    ptr[j % 2][:, c * 128:(c + 1) * 128],
                                    xnbf[j % 2][:, c * 128:(c + 1) * 128],
                                    ident_bf[:, :],
                                    deps=([lne[j], eve.get(j - 2)]
                                          if c == 0 else ()))
                            eve[j] = q.op(
                                "v", "tensor_copy",
                                xnT[:, :, j * 128:(j + 1) * 128],
                                ptr[j % 2][:, 0:1024].rearrange(
                                    "p (c n) -> p c n", n=128),
                                deps=[tre[(j, 7)]])
                    if b == 0:
                        emit_wv_pooling(xt, zt, lne[NT - 1])
                    psq = [E3(nc.psum_tensor(uname("psq"), [128, 512], F32))
                           for _ in range(2)]
                    k = 0
                    evace = {}
                    for h in range(H):
                        for which, wbuf, bias, wev in ((0, wq_bf, bq_sb, wqe),
                                                       (1, wk_bf, bk_sb, wke)):
                            dst = qT if which == 0 else kT
                            for p in range(4):
                                dep = [evace.get(k - 2)]
                                if h == 0 and p == 0:
                                    dep.append(wev)
                                if h == 0 and which == 0:
                                    dep.append(eve[4 * p + 3])
                                ge = q.mm_group([
                                    (psq[k % 2][:, :],
                                     wbuf[:, c, h * 128:(h + 1) * 128],
                                     xnT[:, c, p * 512:(p + 1) * 512])
                                    for c in range(8)],
                                    deps=dep)
                                evace[k] = q.op(
                                    "s", "activation",
                                    dst[:, h, p * 512:(p + 1) * 512],
                                    psq[k % 2][:, :], AF.Identity,
                                    bias=bias[:, h:h + 1],
                                    deps=[ge] + (bqke if h == 0 and p == 0
                                                 else []))
                                k += 1
                    for ti in range(16):
                        for n2 in range(2):
                            mms = [
                                (psq[k % 2][:, :],
                                 xnT[:, c, ti * 128:(ti + 1) * 128],
                                 wvp_bf[:, c, n2 * 512:(n2 + 1) * 512])
                                for c in range(8)]
                            mms.append((psq[k % 2][:, :], ones_bf[:, :],
                                        bvp_bf[:, n2 * 512:(n2 + 1) * 512]))
                            ge = q.mm_group(
                                mms, deps=[evace.get(k - 2)] +
                                ([pool_done[0]] if ti == 0 and n2 == 0
                                 else []))
                            evace[k] = q.op(
                                "s", "activation",
                                vp[:, ti, n2 * 512:(n2 + 1) * 512],
                                psq[k % 2][:, :], AF.Identity, deps=[ge])
                            k += 1
                    q.dma_fence()
                stAB.close()

                with ExitStack() as st3:
                    E3 = st3.enter_context
                    P_sb = E3(nc.sbuf_tensor(uname("P_sb"), [128, 8, 2048], BF16))
                    PT_sb = [E3(nc.sbuf_tensor(uname("PT_sb"), [128, 1024], BF16))
                             for _ in range(2)]
                    aT_sb = [E3(nc.sbuf_tensor(uname("aT_sb"), [128, 512], F32))
                             for _ in range(2)]
                    att_sb = [E3(nc.sbuf_tensor(uname("att_sb"), [128, 512], F32))
                              for _ in range(2)]
                    mx = E3(nc.sbuf_tensor(uname("mx"), [128, 64], F32))
                    rs = E3(nc.sbuf_tensor(uname("rs"), [128, 32], F32))
                    rrs = E3(nc.sbuf_tensor(uname("rrs"), [128, 16], F32))
                    ps_scA = E3(nc.psum_tensor(uname("ps_scA"), [128, 1024], F32))
                    ps_scB = E3(nc.psum_tensor(uname("ps_scB"), [128, 1024], F32))
                    ptb = [E3(nc.psum_tensor(uname("ptb"), [128, 1024], BF16))
                           for _ in range(2)]
                    attP = E3(nc.psum_tensor(uname("attP"), [128, 512], F32))
                    ptf = E3(nc.psum_tensor(uname("ptf"), [128, 128], F32))
                    E3(nc.Block())
                    exp_g = {}
                    mx_war = {}
                    rrs_war = {}
                    rs_war = {}
                    pt_last = {}
                    ptevac = {}
                    pv_ev = {}
                    aT_ev = {}
                    tb_ev = {}
                    tbe_ev = {}
                    dma_ev = {}
                    recip_ev = {}

                    expA_g = {}
                    expB_g = {}
                    deferred = []

                    def flush_deferred():
                        while deferred:
                            qt, eB = deferred.pop(0)
                            q.op("v", "tensor_add", rs[:, 2 * qt:2 * qt + 1],
                                 rs[:, 2 * qt:2 * qt + 1],
                                 rs[:, 2 * qt + 1:2 * qt + 2],
                                 deps=[eB, rrs_war.get(qt)])
                            recip_ev[qt] = q.op("v", "reciprocal",
                                                rrs[:, qt:qt + 1],
                                                rs[:, 2 * qt:2 * qt + 1])
                            rs_war[qt] = recip_ev[qt]

                    def emit_scores(sg, j, h):
                        qt = (sg % 4) * 4 + j
                        qtg = sg * 4 + j
                        hh = sg // 4
                        c0 = 4 * qt
                        sA = None
                        sB = None
                        for p in range(2):
                            sB = q.mm(
                                ps_scB[:, p * 512:(p + 1) * 512],
                                qT[:, hh, qt * 128:(qt + 1) * 128],
                                kT[:, hh, 1024 + p * 512:1024 + (p + 1) * 512],
                                start=True, stop=True,
                                deps=([expB_g.get(qtg - 1)] if p == 0 else ()))
                        for p in range(2):
                            sA = q.mm(
                                ps_scA[:, p * 512:(p + 1) * 512],
                                qT[:, hh, qt * 128:(qt + 1) * 128],
                                kT[:, hh, p * 512:(p + 1) * 512],
                                start=True, stop=True,
                                deps=([expA_g.get(qtg - 1)] if p == 0 else ()))
                        q.op("v", "reduce_max", mx[:, c0 + 1:c0 + 2],
                             ps_scB[:, :], axis=AX.X,
                             deps=[sB, mx_war.get(qt)])
                        q.op("v", "reduce_max", mx[:, c0:c0 + 1],
                             ps_scA[:, :], axis=AX.X, deps=[sA])
                        q.op("v", "tensor_max", mx[:, c0 + 2:c0 + 3],
                             mx[:, c0:c0 + 1], mx[:, c0 + 1:c0 + 2])
                        nm = q.op("v", "tensor_scalar", mx[:, c0 + 3:c0 + 4],
                                  mx[:, c0 + 2:c0 + 3], -ISQ128, None,
                                  OP.mult)
                        eA = q.op("s", "activation",
                                  P_sb[:, 4 * (sg % 2) + j, 0:1024],
                                  ps_scA[:, :], AF.Exp, scale=ISQ128,
                                  bias=mx[:, c0 + 3:c0 + 4],
                                  accum_out=rs[:, 2 * qt:2 * qt + 1],
                                  deps=[nm, pt_last.get(sg - 2),
                                        rs_war.get(qt)] if j == 0 else
                                  [nm, rs_war.get(qt)])
                        eB = q.op("s", "activation",
                                  P_sb[:, 4 * (sg % 2) + j, 1024:2048],
                                  ps_scB[:, :], AF.Exp, scale=ISQ128,
                                  bias=mx[:, c0 + 3:c0 + 4],
                                  accum_out=rs[:, 2 * qt + 1:2 * qt + 2])
                        expA_g[qtg] = eA
                        expB_g[qtg] = eB
                        exp_g[qtg] = eB
                        mx_war[qt] = eA
                        deferred.append((qt, eB))

                    def emit_pv(sg, pairs, h):
                        for kp in pairs:
                            kg = sg * 8 + kp
                            first = []
                            if kp == 0:
                                first.append(exp_g[sg * 4 + 3])
                                first.append(aT_ev.get(sg - 1))
                            te = None
                            for kk2 in range(2):
                                kc = 2 * kp + kk2
                                for j in range(4):
                                    te = q.tr(
                                        ptb[kp % 2][:, kk2 * 512 + j * 128:
                                                    kk2 * 512 + (j + 1) * 128],
                                        P_sb[:, 4 * (sg % 2) + j,
                                             kc * 128:(kc + 1) * 128],
                                        ident_bf[:, :],
                                        deps=(first + [ptevac.get(kg - 2)])
                                        if (kk2 == 0 and j == 0) else ())
                            if kp == 7:
                                pt_last[sg] = te
                            if kp % 2 == 0:
                                pe = q.op("v", "tensor_copy",
                                          PT_sb[kp % 2][:, :],
                                          ptb[kp % 2][:, :],
                                          deps=[te, pv_ev.get((sg, 2 * kp - 3))])
                            else:
                                pe = q.op("s", "activation",
                                          PT_sb[kp % 2][:, :],
                                          ptb[kp % 2][:, :], AF.Identity,
                                          deps=[te, pv_ev.get((sg, 2 * kp - 3))])
                            ptevac[kg] = pe
                            for kk2 in range(2):
                                kc = 2 * kp + kk2
                                pv_ev[(sg, kc)] = q.mm(
                                    attP[:, :],
                                    vp[:, kc, h * 128:(h + 1) * 128],
                                    PT_sb[kp % 2][:, kk2 * 512:
                                                  (kk2 + 1) * 512],
                                    start=(kc == 0), stop=(kc == 15),
                                    deps=[pe] if kk2 == 0 else ())

                    def emit_tail(sg, h):
                        aT_ev[sg] = q.op("v", "tensor_copy",
                                         aT_sb[sg % 2][:, :], attP[:, :],
                                         deps=[pv_ev[(sg, 15)],
                                               tb_ev.get((sg - 2, 3))])
                        for j in range(4):
                            qt = (sg % 4) * 4 + j
                            tb_ev[(sg, j)] = q.tr(
                                ptf[:, :], aT_sb[sg % 2][:, j * 128:(j + 1) * 128],
                                ident_f32[:, :],
                                deps=[aT_ev[sg], tbe_ev.get((sg, j - 1))]
                                if j > 0 else
                                [aT_ev[sg], tbe_ev.get((sg - 1, 3))])
                            tbe_ev[(sg, j)] = q.op(
                                "s", "activation",
                                att_sb[sg % 2][:, j * 128:(j + 1) * 128],
                                ptf[:, :], AF.Identity,
                                scale=rrs[:, qt:qt + 1],
                                deps=[tb_ev[(sg, j)]] +
                                ([dma_ev.get(sg - 2)] if j == 0 else []))
                            rrs_war[qt] = tbe_ev[(sg, j)]
                        hh = sg // 4
                        ss = sg % 4
                        base = b * 2097152 + hh * 262144 + ss * 65536
                        seg = attn_flat[base:base + 65536]
                        dma_ev[sg] = q.dma(
                            "y", seg.rearrange("(j p m) -> p j m", p=128, m=128),
                            att_sb[sg % 2][:, :].rearrange(
                                "p (j m) -> p j m", m=128),
                            deps=[tbe_ev[(sg, 3)]])

                    NSLAB = H * 4
                    for sg in range(NSLAB + 1):
                        for j in range(4):
                            if sg < NSLAB:
                                emit_scores(sg, j, sg // 4)
                                if j > 0 or sg > 0:
                                    flush_deferred()
                            if sg > 0:
                                emit_pv(sg - 1, [2 * j, 2 * j + 1],
                                        (sg - 1) // 4)
                        if sg > 0:
                            flush_deferred()
                            emit_tail(sg - 1, (sg - 1) // 4)
                    flush_deferred()
                    q.dma_fence()
    return nc


def build_l0b(nc, q, x, attn, w_router, W1, b1, W2, b2, x2_out, consts,
              x1h_hbm, hT_hbm, row_hbm):
    ident_bf, ident_f32, ones_bf, ones_f32, cpool = consts
    eps = cpool[:, 0:1]
    NTILE, NITER = 32, 44

    x3 = x.rearrange("(i p) d -> i p d", p=128)
    at3 = attn.rearrange("(i p) d -> i p d", p=128)
    x1h3 = x1h_hbm.rearrange("(i p) d -> i p d", p=128)
    x23 = x2_out.rearrange("(i p) d -> i p d", p=128)
    w13 = W1.rearrange("(c p) f -> c p f", p=128)
    w23 = W2.rearrange("(c p) d -> c p d", p=128)

    with ExitStack() as st0:
        E = st0.enter_context
        w1_bf = E(nc.sbuf_tensor(uname("w1_bf"), [128, 8, 4096], BF16))
        w2_bf = E(nc.sbuf_tensor(uname("w2_bf"), [128, 32, 1024], BF16))
        b1_sb = E(nc.sbuf_tensor(uname("b1_sb"), [128, 32], F32))
        b2_rep = E(nc.sbuf_tensor(uname("b2_rep"), [128, 1024], F32))
        s128 = E(nc.sbuf_tensor(uname("s128"), [128, 32], F32))
        m128 = E(nc.sbuf_tensor(uname("m128"), [128, 32], F32))
        stWR = ExitStack()
        wr_rep = stWR.enter_context(
            nc.sbuf_tensor(uname("wr_rep"), [128, 1024], F32))

        with ExitStack() as st1:
            E1 = st1.enter_context
            E1(nc.Block())
            w1we = q.dma("g", w1_bf[:, :, :], w13.rearrange("c p f -> p c f"))
            w2we = q.dma("g", w2_bf[:, :, :], w23.rearrange("c p d -> p c d"))
            b1we = []
            for mt in range(32):
                b1we.append(q.dma("y", b1_sb[:, mt:mt + 1],
                            b1[mt * 128:(mt + 1) * 128].unsqueeze(1)))
            b2we = q.dma("y", b2_rep[:, :],
                         b2.unsqueeze(0).broadcast_to((128, 1024)))
            wrwe = q.dma("y", wr_rep[:, :],
                         w_router.unsqueeze(0).broadcast_to((128, 1024)))

        with ExitStack() as st1:
            E1 = st1.enter_context
            hT = E1(nc.sbuf_tensor(uname("hT_half"), [128, 8, 2048], BF16))
            xt = [E1(nc.sbuf_tensor(uname("xt"), [128, 1024], F32))
                  for _ in range(2)]
            att = [E1(nc.sbuf_tensor(uname("att"), [128, 1024], F32))
                   for _ in range(2)]
            zt = E1(nc.sbuf_tensor(uname("zt"), [128, 1024], F32))
            hbf = [E1(nc.sbuf_tensor(uname("hbf"), [128, 1024], BF16))
                   for _ in range(2)]
            st6 = [E1(nc.sbuf_tensor(uname("st6"), [128, 12], F32))
                   for _ in range(2)]
            mv = [E1(nc.sbuf_tensor(uname("mv"), [128, 4], F32))
                  for _ in range(2)]
            rstd = [E1(nc.sbuf_tensor(uname("rstd"), [128, 2], F32))
                    for _ in range(2)]
            ptr = [E1(nc.psum_tensor(uname("ptr"), [128, 1024], BF16))
                   for _ in range(2)]
            E1(nc.Block())
            x1de = {}
            adde = {}
            eve = {}
            tre = {}
            r1e = {}
            dume = None
            for i in range(NTILE + 1):
                if i < NTILE:
                    dx = q.dma("y", xt[i % 2][:, :], x3[i],
                               deps=[x1de.get(i - 2)])
                    da = q.dma("y", att[i % 2][:, :], at3[i],
                               deps=[adde.get(i - 2)])
                    q.op("v", "tensor_add", xt[i % 2][:, :], xt[i % 2][:, :],
                         att[i % 2][:, :], deps=[dx, da])
                    r1e[i], _ = ln_stats_comb(q, xt[i % 2][:, :], st6[i % 2],
                                              mv[i % 2], rstd[i % 2], eps,
                                              None)
                if i > 0:
                    j = i - 1
                    half, ih = j // 16, j % 16
                    q.op("v", "tensor_scalar", zt[:, :], xt[j % 2][:, :],
                         mv[j % 2][:, 0:1], rstd[j % 2][:, 0:1],
                         OP.subtract, OP.mult, deps=[r1e[j]])
                    cpv = q.op("v", "tensor_copy", hbf[j % 2][:, :], zt[:, :])
                    adde[j] = q.op("v", "tensor_mul", att[j % 2][:, :],
                                   zt[:, :], wr_rep[:, :],
                                   deps=[wrwe] if j == 0 else ())
                    q.op("v", "reduce_sum", s128[:, j:j + 1], att[j % 2][:, :],
                         axis=AX.X)
                    x1a = q.op("v", "tensor_add", xt[j % 2][:, :],
                               xt[j % 2][:, :], zt[:, :])
                    x1de[j] = q.dma("y", x1h3[j], xt[j % 2][:, :], deps=[x1a])
                    for c in range(8):
                        tre[(j, c)] = q.tr(
                            ptr[j % 2][:, c * 128:(c + 1) * 128],
                            hbf[j % 2][:, c * 128:(c + 1) * 128],
                            ident_bf[:, :],
                            deps=([cpv, eve.get(j - 2)] if c == 0 else ()))
                    eve[j] = q.op("v", "tensor_copy",
                                  hT[:, :, ih * 128:(ih + 1) * 128],
                                  ptr[j % 2][:, 0:1024].rearrange(
                                      "p (c n) -> p c n", n=128),
                                  deps=[tre[(j, 7)]] +
                                  ([dume] if ih == 0 and dume else []))
                    if ih == 15:
                        dume = q.dma(
                            "y", hT_hbm[:, :, half * 2048:(half + 1) * 2048],
                            hT[:, :, :], deps=[eve[j], dume])
            q.dma_fence()
        stWR.close()

        with ExitStack() as st1:
            E1 = st1.enter_context
            hT = E1(nc.sbuf_tensor(uname("hT_c"), [128, 8, 2048], BF16))
            h2 = E1(nc.sbuf_tensor(uname("h2"), [128, 32, 512], BF16))
            x1h_t = [E1(nc.sbuf_tensor(uname("x1h_t"), [128, 1024], F32))
                     for _ in range(1)]
            out_t = [E1(nc.sbuf_tensor(uname("out_t"), [128, 1024], F32))
                     for _ in range(1)]
            stt = E1(nc.sbuf_tensor(uname("stt"), [1, 16], F32))
            sti = E1(nc.sbuf_tensor(uname("sti"), [1, 4], I32))
            tpp = E1(nc.sbuf_tensor(uname("tpp"), [128, 2], F32))
            ge128 = E1(nc.sbuf_tensor(uname("ge128"), [128, 32], F32))
            red2 = E1(nc.sbuf_tensor(uname("red2"), [128, 2], F32))
            th128 = E1(nc.sbuf_tensor(uname("th128"), [128, 32], F32))
            ones_col = E1(nc.sbuf_tensor(uname("ones_col"), [128, 1], F32))
            ps1 = [E1(nc.psum_tensor(uname("ps1"), [128, 512], F32))
                   for _ in range(2)]
            ps2 = [E1(nc.psum_tensor(uname("ps2"), [128, 512], F32))
                   for _ in range(4)]
            psb = E1(nc.psum_tensor(uname("psb"), [128, 2], F32))
            E1(nc.Block())
            lo, hi, tt, cnt = (stt[:, 0:2], stt[:, 2:4], stt[:, 4:6],
                               stt[:, 6:8])
            mce = q.op("g", "memset", ones_col[:, :], 1.0)
            q.op("v", "memset", lo[:, :], -1000.0)
            q.op("v", "memset", hi[:, :], 1000.0)
            q.op("v", "tensor_add", tt, lo, hi)
            te = q.op("v", "tensor_scalar_mul", tt, tt, 0.5)
            bis_last_s = None
            for it in range(NITER):
                bc = q.mm(psb[:, :], ones_f32[0:1, :], tt, start=True,
                          stop=True, deps=[te, mce])
                bce = q.op("s", "activation", tpp[:, :], psb[:, :],
                           AF.Identity, deps=[bc])
                g1 = q.op("v", "tensor_scalar", ge128[:, 0:16],
                          s128[:, 0:16], tpp[:, 0:1], None, OP.is_ge,
                          deps=[bce])
                q.op("v", "tensor_scalar", ge128[:, 16:32], s128[:, 16:32],
                     tpp[:, 1:2], None, OP.is_ge)
                q.op("v", "reduce_sum", red2[:, 0:1], ge128[:, 0:16],
                     axis=AX.X)
                re = q.op("v", "reduce_sum", red2[:, 1:2], ge128[:, 16:32],
                          axis=AX.X)
                cm = q.mm(psb[0:1, :], ones_col[:, :], red2[:, :], start=True,
                          stop=True, deps=[re])
                ce = q.op("s", "activation", cnt, psb[0:1, :], AF.Identity,
                          deps=[cm])
                q.op("v", "tensor_scalar", sti[:, 0:2], cnt, 1024.0, None,
                     OP.is_ge, deps=[ce])
                q.op("v", "tensor_scalar", sti[:, 2:4], cnt, 1024.0, None,
                     OP.is_lt)
                q.op("v", "select", lo, sti[:, 0:2], tt, lo, add_drain=True)
                q.op("v", "select", hi, sti[:, 2:4], tt, hi, add_drain=True)
                q.op("v", "tensor_add", tt, lo, hi)
                te = q.op("v", "tensor_scalar_mul", tt, tt, 0.5)
            bc = q.mm(psb[:, :], ones_f32[0:1, :], lo, start=True, stop=True,
                      deps=[te])
            bce = q.op("s", "activation", tpp[:, :], psb[:, :], AF.Identity,
                       deps=[bc])
            q.op("v", "tensor_scalar", ge128[:, 0:16], s128[:, 0:16],
                 tpp[:, 0:1], None, OP.is_ge, deps=[bce])
            q.op("v", "tensor_scalar", ge128[:, 16:32], s128[:, 16:32],
                 tpp[:, 1:2], None, OP.is_ge)
            the = q.op("s", "activation", th128[:, :], s128[:, :], AF.Tanh,
                       scale=0.5, deps=[bce])
            q.op("v", "tensor_scalar", th128[:, :], th128[:, :], 0.5, 0.5,
                 OP.mult, OP.add, deps=[the])
            m_ev = q.op("v", "tensor_mul", m128[:, :], th128[:, :],
                        ge128[:, :])

            k1 = 0
            k2 = 0
            gele = {}
            w2le = {}
            combe = {}
            dmae = {}
            x1de = {}
            hle = None
            for half in range(2):
                hle = q.dma("y", hT[:, :, :],
                            hT_hbm[:, :, half * 2048:(half + 1) * 2048],
                            deps=[w2le.get(k2 - 1), hle])
                for sl in range(4):
                    tok0 = half * 2048 + sl * 512
                    slab_gel = []
                    for mt in range(32):
                        dep = [gele.get(k1 - 2)]
                        if mt == 0:
                            dep.append(hle)
                            dep.append(w2le.get(k2 - 4))
                            if k1 == 0:
                                dep.append(w1we)
                        ge_ = q.mm_group([
                            (ps1[k1 % 2][:, :],
                             w1_bf[:, c, mt * 128:(mt + 1) * 128],
                             hT[:, c, sl * 512:(sl + 1) * 512])
                            for c in range(8)], deps=dep)
                        gele[k1] = q.op("s", "activation", h2[:, mt, :],
                                        ps1[k1 % 2][:, :],
                                        AF.Gelu_apprx_tanh,
                                        bias=b1_sb[:, mt:mt + 1],
                                        deps=[ge_] + (b1we if k1 == 0 else []))
                        slab_gel.append(gele[k1])
                        k1 += 1
                    for ti in range(4):
                        gi = (tok0 // 128) + ti
                        xde = q.dma("y", x1h_t[0][:, :], x1h3[gi],
                                    deps=[combe.get((gi - 1, 1))])
                        for ds in range(2):
                            mms = [
                                (ps2[k2 % 4][:, :],
                                 h2[:, fc, ti * 128:(ti + 1) * 128],
                                 w2_bf[:, fc, ds * 512:(ds + 1) * 512])
                                for fc in range(32)]
                            dep = [combe.get((gi - 2, ds))]
                            if ti == 0 and ds == 0:
                                dep.append(slab_gel[-1])
                                if k2 == 0:
                                    dep.append(w2we)
                            w2le[k2] = q.mm_group(mms, deps=dep)
                            c1 = q.op("v", "scalar_tensor_tensor",
                                      out_t[0][:, ds * 512:(ds + 1) * 512],
                                      ps2[k2 % 4][:, :], m128[:, gi:gi + 1],
                                      x1h_t[0][:, ds * 512:(ds + 1) * 512],
                                      OP.mult, OP.add,
                                      deps=[w2le[k2], xde, m_ev,
                                            dmae.get(gi - 1)] +
                                      ([b2we] if k2 == 0 else []))
                            combe[(gi, ds)] = q.op(
                                "v", "scalar_tensor_tensor",
                                out_t[0][:, ds * 512:(ds + 1) * 512],
                                b2_rep[:, ds * 512:(ds + 1) * 512],
                                m128[:, gi:gi + 1],
                                out_t[0][:, ds * 512:(ds + 1) * 512],
                                OP.mult, OP.add)
                            k2 += 1
                        dmae[gi] = q.dma("y", x23[gi], out_t[0][:, :],
                                         deps=[combe[(gi, 1)]])
            q.dma_fence()
    return nc


def build_l1a(nc, q, x2, wdict, x3_out, consts, z_hbm, g_hbm, ao_hbm,
              x1i_hbm, hT_hbm):
    ident_bf, ident_f32, ones_bf, ones_f32, cpool = consts
    eps = cpool[:, 0:1]
    NB, T, NT, H, DH = 2, 2048, 16, 16, 64

    x3v = x2.rearrange("(i p) d -> i p d", p=128)
    zv = z_hbm.rearrange("(i p) d -> i p d", p=128)
    gv = g_hbm.rearrange("(i p) d -> i p d", p=128)
    x1iv = x1i_hbm.rearrange("(i p) d -> i p d", p=128)
    xov = x3_out.rearrange("(i p) d -> i p d", p=128)
    aov = ao_hbm.rearrange("(i p) d -> i p d", p=128)

    W = {k: v.rearrange("(c p) d -> c p d", p=128) if k in
         ("WQ", "WK", "WV", "WO", "Wg", "W1", "Bw") else v
         for k, v in wdict.items()}
    w23 = wdict["W2"].rearrange("(c p) d -> c p d", p=128)

    with ExitStack() as st0:
        E = st0.enter_context
        stAttn = ExitStack()
        EA = stAttn.enter_context
        ln1g = EA(nc.sbuf_tensor(uname("ln1g"), [128, 1024], F32))
        ln1b = EA(nc.sbuf_tensor(uname("ln1b"), [128, 1024], F32))
        bw_bf = EA(nc.sbuf_tensor(uname("bw_bf"), [128, 8, 64], BF16))
        cw_bf = EA(nc.sbuf_tensor(uname("cw_bf"), [64, 1024], BF16))
        apow = EA(nc.sbuf_tensor(uname("apow"), [64, 12], F32))
        wmask = EA(nc.sbuf_tensor(uname("wmask"), [128, 256], F32))

        with ExitStack() as st1:
            E1 = st1.enter_context
            zero128 = E1(nc.sbuf_tensor(uname("zero128"), [128, 128], F32))
            E1(nc.Block())
            l1ge = q.dma("y", ln1g[:, :],
                         wdict["ln1_g"].unsqueeze(0).broadcast_to((128, 1024)))
            l1be = q.dma("y", ln1b[:, :],
                         wdict["ln1_b"].unsqueeze(0).broadcast_to((128, 1024)))
            bwe = q.dma("g", bw_bf[:, :, :], W["Bw"].rearrange("c p d -> p c d"))
            cwe = q.dma("g", cw_bf[:, :], wdict["Cw"])
            ade = q.dma("y", apow[:, 0:1], wdict["A"].unsqueeze(1))
            for j in range(10):
                q.op("v", "tensor_mul", apow[0:64, j + 1:j + 2],
                     apow[0:64, j:j + 1], apow[0:64, j:j + 1],
                     deps=[ade] if j == 0 else ())
            ze = q.op("g", "memset", zero128[:, :], 0.0)
            q.op("g", "affine_select", wmask[:, 0:128], zero128[:, :],
                 pattern=[[1, 128]], compare_op=OP.is_ge, fill=NEG_BIG,
                 base=-1, channel_multiplier=-1, deps=[ze])
            q.op("g", "affine_select", wmask[:, 128:256], zero128[:, :],
                 pattern=[[-1, 128]], compare_op=OP.is_ge, fill=NEG_BIG,
                 base=0, channel_multiplier=1)

        for b in range(NB):
            with ExitStack() as stb:
                Eb = stb.enter_context
                qT = Eb(nc.sbuf_tensor(uname("qT"), [128, 8, T], BF16))
                kT = Eb(nc.sbuf_tensor(uname("kT"), [128, 8, T], BF16))
                v_sb = Eb(nc.sbuf_tensor(uname("v_sb"), [128, 16, 1024], BF16))
                u_sb = Eb(nc.sbuf_tensor(uname("u_sb"), [64, 2048], F32))
                stT_bf = Eb(nc.sbuf_tensor(uname("stT_bf"), [64, 2048], BF16))
                stAB = ExitStack()
                hT = stAB.enter_context(
                    nc.sbuf_tensor(uname("hT"), [128, 8, T], BF16))

                with ExitStack() as st1:
                    E1 = st1.enter_context
                    xt = [E1(nc.sbuf_tensor(uname("xt"), [128, 1024], F32))
                          for _ in range(2)]
                    zt = [E1(nc.sbuf_tensor(uname("zt"), [128, 1024], F32))
                          for _ in range(1)]
                    hbf = [E1(nc.sbuf_tensor(uname("hbf"), [128, 1024], BF16))
                           for _ in range(1)]
                    st6 = [E1(nc.sbuf_tensor(uname("st6"), [128, 12], F32))
                           for _ in range(2)]
                    mv = [E1(nc.sbuf_tensor(uname("mv"), [128, 4], F32))
                          for _ in range(2)]
                    rstd = [E1(nc.sbuf_tensor(uname("rstd"), [128, 2], F32))
                            for _ in range(2)]
                    ptr = [E1(nc.psum_tensor(uname("ptr"), [128, 1024], BF16))
                           for _ in range(2)]
                    E1(nc.Block())
                    zde = {}
                    eve = {}
                    tre = {}
                    rce = {}
                    for i in range(NT + 1):
                        if i < NT:
                            de = q.dma("y", xt[i % 2][:, :], x3v[b * NT + i],
                                       deps=[eve.get(i - 2)])
                            rce[i] = ln_stats_comb(q, xt[i % 2][:, :],
                                                   st6[i % 2], mv[i % 2],
                                                   rstd[i % 2], eps, de)
                        if i > 0:
                            j = i - 1
                            gi = b * NT + j
                            r1e, rcomb_e = rce[j]
                            zz = q.op("v", "tensor_scalar", zt[0][:, :],
                                      xt[j % 2][:, :], mv[j % 2][:, 0:1],
                                      rstd[j % 2][:, 0:1],
                                      OP.subtract, OP.mult,
                                      deps=[r1e, zde.get(j - 1)])
                            zde[j] = q.dma("y", zv[gi], zt[0][:, :],
                                           deps=[zz])
                            q.op("v", "tensor_scalar", xt[j % 2][:, :],
                                 xt[j % 2][:, :], mv[j % 2][:, 0:1],
                                 rstd[j % 2][:, 1:2], OP.subtract, OP.mult,
                                 deps=[rcomb_e])
                            q.op("v", "tensor_mul", xt[j % 2][:, :],
                                 xt[j % 2][:, :],
                                 ln1g[:, :], deps=[l1ge] if j == 0 else ())
                            he = q.op("v", "tensor_add", hbf[0][:, :],
                                      xt[j % 2][:, :], ln1b[:, :],
                                      deps=([l1be] if j == 0 else []) +
                                      [tre.get((j - 1, 7))])
                            for c in range(8):
                                tre[(j, c)] = q.tr(
                                    ptr[j % 2][:, c * 128:(c + 1) * 128],
                                    hbf[0][:, c * 128:(c + 1) * 128],
                                    ident_bf[:, :],
                                    deps=([he, eve.get(j - 2)]
                                          if c == 0 else ()))
                            eve[j] = q.op(
                                "v", "tensor_copy",
                                hT[:, :, j * 128:(j + 1) * 128],
                                ptr[j % 2][:, 0:1024].rearrange(
                                    "p (c n) -> p c n", n=128),
                                deps=[tre[(j, 7)]])
                    bb_sb = E1(nc.sbuf_tensor(uname("bb_sb"), [128, 16], F32))
                    psq = [E1(nc.psum_tensor(uname("psq"), [128, 512], F32))
                           for _ in range(2)]
                    bbe = {}
                    for mt in range(8):
                        bbe[mt] = q.dma(
                            "y", bb_sb[:, mt:mt + 1],
                            wdict["bQ"][mt * 128:(mt + 1) * 128].unsqueeze(1))
                        bbe[8 + mt] = q.dma(
                            "y", bb_sb[:, 8 + mt:9 + mt],
                            wdict["bK"][mt * 128:(mt + 1) * 128].unsqueeze(1))
                    k = 0
                    evace = {}
                    with ExitStack() as st2:
                        E2 = st2.enter_context
                        wq_bf = E2(nc.sbuf_tensor(uname("wq_bf"),
                                                  [128, 8, 1024], BF16))
                        wk_bf = E2(nc.sbuf_tensor(uname("wk_bf"),
                                                  [128, 8, 1024], BF16))
                        we1 = q.dma("g", wq_bf[:, :, :],
                                    W["WQ"].rearrange("c p d -> p c d"))
                        we2 = q.dma("g", wk_bf[:, :, :],
                                    W["WK"].rearrange("c p d -> p c d"))
                        for mt in range(8):
                            for which, wbuf, we in ((0, wq_bf, we1),
                                                    (1, wk_bf, we2)):
                                dst = qT if which == 0 else kT
                                boff = 0 if which == 0 else 8
                                for p in range(4):
                                    dep = [evace.get(k - 2)]
                                    if mt == 0 and p == 0:
                                        dep.append(we)
                                    if mt == 0 and which == 0:
                                        dep.append(eve[4 * p + 3])
                                    ge = q.mm_group([
                                        (psq[k % 2][:, :],
                                         wbuf[:, c, mt * 128:(mt + 1) * 128],
                                         hT[:, c, p * 512:(p + 1) * 512])
                                        for c in range(8)], deps=dep)
                                    evace[k] = q.op(
                                        "s", "activation",
                                        dst[:, mt, p * 512:(p + 1) * 512],
                                        psq[k % 2][:, :], AF.Identity,
                                        bias=bb_sb[:, boff + mt:boff + mt + 1],
                                        deps=[ge] +
                                        ([bbe[boff + mt]] if p == 0 else []))
                                    k += 1
                    with ExitStack() as st2:
                        E2 = st2.enter_context
                        wv_bf = E2(nc.sbuf_tensor(uname("wv_bf"),
                                                  [128, 8, 1024], BF16))
                        wg_bf = E2(nc.sbuf_tensor(uname("wg_bf"),
                                                  [128, 8, 1024], BF16))
                        bv_bf = E2(nc.sbuf_tensor(uname("bv_bf"), [1, 1024], BF16))
                        bg_bf = E2(nc.sbuf_tensor(uname("bg_bf"), [1, 1024], BF16))
                        gbf = [E2(nc.sbuf_tensor(uname("gbf"), [128, 1024], BF16))
                               for _ in range(1)]
                        we1 = q.dma("g", wv_bf[:, :, :],
                                    W["WV"].rearrange("c p d -> p c d"),
                                    deps=[evace[k - 1]])
                        we2 = q.dma("g", wg_bf[:, :, :],
                                    W["Wg"].rearrange("c p d -> p c d"))
                        be1 = q.dma("g", bv_bf[:, :], wdict["bV"].unsqueeze(0))
                        be2 = q.dma("g", bg_bf[:, :], wdict["bg"].unsqueeze(0))
                        gde = {}
                        for ti in range(16):
                            for n2 in range(2):
                                dep = [evace.get(k - 2)]
                                if ti == 0 and n2 == 0:
                                    dep += [we1, be1]
                                mms = [
                                    (psq[k % 2][:, :],
                                     hT[:, c, ti * 128:(ti + 1) * 128],
                                     wv_bf[:, c, n2 * 512:(n2 + 1) * 512])
                                    for c in range(8)]
                                mms.append((psq[k % 2][:, :], ones_bf[:, :],
                                            bv_bf[:, n2 * 512:(n2 + 1) * 512]))
                                ge = q.mm_group(mms, deps=dep)
                                evace[k] = q.op(
                                    "s", "activation",
                                    v_sb[:, ti, n2 * 512:(n2 + 1) * 512],
                                    psq[k % 2][:, :], AF.Identity, deps=[ge])
                                k += 1
                            for n2 in range(2):
                                dep = [evace.get(k - 2)]
                                if ti == 0 and n2 == 0:
                                    dep += [we2, be2]
                                if n2 == 0:
                                    dep += [gde.get(ti - 1)]
                                mms = [
                                    (psq[k % 2][:, :],
                                     hT[:, c, ti * 128:(ti + 1) * 128],
                                     wg_bf[:, c, n2 * 512:(n2 + 1) * 512])
                                    for c in range(8)]
                                mms.append((psq[k % 2][:, :], ones_bf[:, :],
                                            bg_bf[:, n2 * 512:(n2 + 1) * 512]))
                                ge = q.mm_group(mms, deps=dep)
                                evace[k] = q.op(
                                    "s", "activation",
                                    gbf[0][:, n2 * 512:(n2 + 1) * 512],
                                    psq[k % 2][:, :], AF.Sigmoid, deps=[ge])
                                k += 1
                            gde[ti] = q.dma("y", gv[b * NT + ti],
                                            gbf[0][:, :],
                                            deps=[evace[k - 1]])
                        for p in range(4):
                            ge = q.mm_group([
                                (psq[k % 2][0:64, :],
                                 bw_bf[:, c, 0:64],
                                 hT[:, c, p * 512:(p + 1) * 512])
                                for c in range(8)],
                                deps=[evace.get(k - 2)] +
                                ([bwe] if p == 0 else []))
                            evace[k] = q.op(
                                "v", "tensor_copy",
                                u_sb[:, p * 512:(p + 1) * 512],
                                psq[k % 2][0:64, :], deps=[ge])
                            k += 1
                        q.dma_fence()
                stAB.close()

                with ExitStack() as st1:
                    E1 = st1.enter_context
                    u_sb2 = E1(nc.sbuf_tensor(uname("u_sb2"), [64, 2048], F32))
                    ao_sb = E1(nc.sbuf_tensor(uname("ao_sb"), [128, 16, 1024], BF16))
                    msk = [E1(nc.sbuf_tensor(uname("msk"), [128, 256], F32))
                           for _ in range(2)]
                    P_bf = [E1(nc.sbuf_tensor(uname("P_bf"), [128, 256], BF16))
                            for _ in range(2)]
                    PT_sb = [E1(nc.sbuf_tensor(uname("PT_sb"), [128, 256], BF16))
                             for _ in range(2)]
                    mx = E1(nc.sbuf_tensor(uname("mx"), [128, 8], F32))
                    rs = E1(nc.sbuf_tensor(uname("rs"), [128, 16], F32))
                    ps_sc = [E1(nc.psum_tensor(uname("ps_sc"), [128, 256], F32))
                             for _ in range(2)]
                    ptb = [E1(nc.psum_tensor(uname("ptb"), [128, 256], BF16))
                           for _ in range(2)]
                    pao = [E1(nc.psum_tensor(uname("pao"), [128, 64], F32))
                           for _ in range(2)]
                    jnk = E1(nc.psum_tensor(uname("jnk"), [128, 256], F32))
                    E1(nc.Block())
                    src, dst = u_sb, u_sb2
                    for j in range(11):
                        o = 1 << j
                        q.op("v", "tensor_copy", dst[:, 0:o], src[:, 0:o])
                        q.op("v", "scalar_tensor_tensor", dst[:, o:2048],
                             src[:, 0:2048 - o], apow[0:64, j:j + 1],
                             src[:, o:2048], OP.mult, OP.add)
                        src, dst = dst, src
                    sce_scan = q.op("v", "tensor_copy", stT_bf[:, :], src[:, :])
                    sc_ev = {}
                    mske = {}
                    expe = {}
                    rece = {}
                    pte = {}
                    ptev = {}
                    aoe = {}
                    aoev = {}
                    rs = E1(nc.sbuf_tensor(uname("rs4"), [128, 8], F32)) \
                        if False else rs

                    def c_part1(kk):
                        h, qb = kk // 16, kk % 16
                        mt, off = h // 2, (h % 2) * 64
                        first = (qb == 0)
                        kw = 128 if first else 256
                        kb0 = qb if first else qb - 1
                        sc_ev[kk] = q.mm(
                            ps_sc[kk % 2][:, 0:kw],
                            qT[off:off + 64, mt, qb * 128:(qb + 1) * 128],
                            kT[off:off + 64, mt, kb0 * 128:kb0 * 128 + kw],
                            start=True, stop=True,
                            deps=[mske.get(kk - 2)])
                        mske[kk] = q.op("v", "tensor_add",
                                        msk[kk % 2][:, 0:kw],
                                        ps_sc[kk % 2][:, 0:kw],
                                        wmask[:, 256 - kw:256],
                                        deps=[sc_ev[kk], expe.get(kk - 2)])
                        me = q.op("v", "reduce_max",
                                  mx[:, 2 * (kk % 2):2 * (kk % 2) + 1],
                                  msk[kk % 2][:, 0:kw], axis=AX.X,
                                  deps=[expe.get(kk - 2)])
                        nm = q.op("v", "tensor_scalar",
                                  mx[:, 2 * (kk % 2) + 1:2 * (kk % 2) + 2],
                                  mx[:, 2 * (kk % 2):2 * (kk % 2) + 1],
                                  -ISQ64, None, OP.mult)
                        expe[kk] = q.op(
                            "s", "activation", P_bf[kk % 2][:, 0:kw],
                            msk[kk % 2][:, 0:kw], AF.Exp, scale=ISQ64,
                            bias=mx[:, 2 * (kk % 2) + 1:2 * (kk % 2) + 2],
                            accum_out=rs[:, 4 * (kk % 4):4 * (kk % 4) + 1],
                            deps=[nm, pte.get(kk - 2), rece.get(kk - 4)])
                        rece[kk] = q.op(
                            "v", "reciprocal",
                            rs[:, 4 * (kk % 4) + 1:4 * (kk % 4) + 2],
                            rs[:, 4 * (kk % 4):4 * (kk % 4) + 1],
                            deps=[expe[kk], aoev.get(kk - 4)])
                        q.mm(jnk[:, :], ident_bf[:, :],
                             v_sb[:, 0, 0:256], start=True, stop=True,
                             deps=[mske[kk]])

                    def c_part2(kk):
                        h, qb = kk // 16, kk % 16
                        first = (qb == 0)
                        kw = 128 if first else 256
                        kb0 = qb if first else qb - 1
                        te = None
                        for kk2 in range(kw // 128):
                            te = q.tr(ptb[kk % 2][:, kk2 * 128:(kk2 + 1) * 128],
                                      P_bf[kk % 2][:, kk2 * 128:(kk2 + 1) * 128],
                                      ident_bf[:, :],
                                      deps=[expe[kk], ptev.get(kk - 2)]
                                      if kk2 == 0 else ())
                        pte[kk] = te
                        ptev[kk] = q.op("v", "tensor_copy",
                                        PT_sb[kk % 2][:, 0:kw],
                                        ptb[kk % 2][:, 0:kw],
                                        deps=[te, aoe.get(kk - 2)])
                        ae = None
                        for kk2 in range(kw // 128):
                            ae = q.mm(pao[kk % 2][:, :],
                                      PT_sb[kk % 2][:, kk2 * 128:(kk2 + 1) * 128],
                                      v_sb[:, kb0 + kk2, h * 64:(h + 1) * 64],
                                      start=(kk2 == 0),
                                      stop=(kk2 == kw // 128 - 1),
                                      deps=[ptev[kk], aoev.get(kk - 2)]
                                      if kk2 == 0 else ())
                        aoe[kk] = ae
                        aoev[kk] = q.op(
                            "s", "activation",
                            ao_sb[:, qb, h * 64:(h + 1) * 64],
                            pao[kk % 2][:, :], AF.Identity,
                            scale=rs[:, 4 * (kk % 4) + 1:4 * (kk % 4) + 2],
                            deps=[ae, rece[kk]])

                    for kk in range(H * 16 + 1):
                        if kk < H * 16:
                            c_part1(kk)
                        if kk > 0:
                            c_part2(kk - 1)
                    kk = H * 16
                    dume = None
                    for ti in range(NT):
                        dume = q.dma("y", aov[b * NT + ti], ao_sb[:, ti, :],
                                     deps=[aoev[kk - 1], dume])
                    q.dma_fence()

                with ExitStack() as st1:
                    E1 = st1.enter_context
                    wo_bf = E1(nc.sbuf_tensor(uname("wo_bf"), [128, 8, 1024], BF16))
                    bo_bf = E1(nc.sbuf_tensor(uname("bo_bf"), [1, 1024], BF16))
                    aot = [E1(nc.sbuf_tensor(uname("aot"), [128, 1024], BF16))
                           for _ in range(2)]
                    aoT = [E1(nc.sbuf_tensor(uname("aoT"), [128, 8, 128], BF16))
                           for _ in range(2)]
                    zt = [E1(nc.sbuf_tensor(uname("zt2"), [128, 1024], F32))
                          for _ in range(2)]
                    gt = [E1(nc.sbuf_tensor(uname("gt"), [128, 1024], F32))
                          for _ in range(2)]
                    ssm_t = E1(nc.sbuf_tensor(uname("ssm_t"), [128, 512], F32))
                    t1 = E1(nc.sbuf_tensor(uname("t1"), [128, 512], F32))
                    x1i_t = [E1(nc.sbuf_tensor(uname("x1i_t"), [128, 1024], F32))
                             for _ in range(2)]
                    ptr = [E1(nc.psum_tensor(uname("ptr2"), [128, 1024], BF16))
                           for _ in range(2)]
                    pat = [E1(nc.psum_tensor(uname("pat"), [128, 512], F32))
                           for _ in range(2)]
                    pss = [E1(nc.psum_tensor(uname("pss"), [128, 512], F32))
                           for _ in range(2)]
                    E1(nc.Block())
                    woe = q.dma("g", wo_bf[:, :, :],
                                W["WO"].rearrange("c p d -> p c d"))
                    boe = q.dma("g", bo_bf[:, :], wdict["bO"].unsqueeze(0))
                    aode = {}
                    tre = {}
                    aoTe = {}
                    x1de = {}
                    cmbe = {}
                    ate = {}
                    sse2 = {}
                    for ti in range(NT):
                        gi = b * NT + ti
                        d1 = q.dma("y", aot[ti % 2][:, :], aov[gi],
                                   deps=[tre.get((ti - 2, 7))])
                        d2 = q.dma("y", zt[ti % 2][:, :], zv[gi],
                                   deps=[cmbe.get((ti - 2, 1))])
                        d3 = q.dma("g", gt[ti % 2][:, :], gv[gi],
                                   deps=[cmbe.get((ti - 2, 1))])
                        for c in range(8):
                            tre[(ti, c)] = q.tr(
                                ptr[ti % 2][:, c * 128:(c + 1) * 128],
                                aot[ti % 2][:, c * 128:(c + 1) * 128],
                                ident_bf[:, :],
                                deps=([d1, aoTe.get(ti - 2)] if c == 0
                                      else ()))
                        aoTe[ti] = q.op(
                            "v", "tensor_copy", aoT[ti % 2][:, :, :],
                            ptr[ti % 2][:, 0:1024].rearrange(
                                "p (c n) -> p c n", n=128),
                            deps=[tre[(ti, 7)]])
                        for ds in range(2):
                            mms = [(pat[ds][:, :],
                                    aoT[ti % 2][:, c, 0:128],
                                    wo_bf[:, c, ds * 512:(ds + 1) * 512])
                                   for c in range(8)]
                            mms.append((pat[ds][:, :], ones_bf[:, :],
                                        bo_bf[:, ds * 512:(ds + 1) * 512]))
                            dep = [aoTe[ti], cmbe.get((ti - 1, ds))]
                            if ti == 0:
                                dep.append(woe)
                                dep.append(boe)
                            ate[(ti, ds)] = q.mm_group(mms, deps=dep)
                            sse2[(ti, ds)] = q.mm(
                                pss[ds][:, :],
                                stT_bf[0:64, ti * 128:(ti + 1) * 128],
                                cw_bf[0:64, ds * 512:(ds + 1) * 512],
                                start=True, stop=True,
                                deps=[sce_scan if ti == 0 and ds == 0 else None,
                                      cwe if ti == 0 and ds == 0 else None,
                                      cmbe.get((ti - 1, ds))])
                            e1 = q.op("v", "tensor_copy", ssm_t[:, :],
                                      pss[ds][:, :], deps=[sse2[(ti, ds)]])
                            e2 = q.op("v", "tensor_sub", t1[:, :],
                                      pat[ds][:, :], ssm_t[:, :],
                                      deps=[ate[(ti, ds)]])
                            q.op("v", "tensor_mul", t1[:, :], t1[:, :],
                                 gt[ti % 2][:, ds * 512:(ds + 1) * 512],
                                 deps=[d3])
                            q.op("v", "tensor_add", ssm_t[:, :], ssm_t[:, :],
                                 zt[ti % 2][:, ds * 512:(ds + 1) * 512],
                                 deps=[d2])
                            cmbe[(ti, ds)] = q.op(
                                "v", "tensor_add",
                                x1i_t[ti % 2][:, ds * 512:(ds + 1) * 512],
                                t1[:, :], ssm_t[:, :],
                                deps=[x1de.get(ti - 2)])
                        x1de[ti] = q.dma("y", x1iv[gi], x1i_t[ti % 2][:, :],
                                         deps=[cmbe[(ti, 1)]])
                    q.dma_fence()
        stAttn.close()

        with ExitStack() as stm:
            Em = stm.enter_context
            w1_bf = Em(nc.sbuf_tensor(uname("w1_bf"), [128, 8, 4096], BF16))
            w2_bf = Em(nc.sbuf_tensor(uname("w2_bf"), [128, 32, 1024], BF16))
            b1_sb = Em(nc.sbuf_tensor(uname("b1_sb"), [128, 32], F32))
            b2_rep = Em(nc.sbuf_tensor(uname("b2_rep"), [128, 1024], F32))
            stLN = ExitStack()
            ln2g = stLN.enter_context(nc.sbuf_tensor(uname("ln2g"), [128, 1024], F32))
            ln2b = stLN.enter_context(nc.sbuf_tensor(uname("ln2b"), [128, 1024], F32))
            with ExitStack() as st1:
                E1 = st1.enter_context
                E1(nc.Block())
                l2ge = q.dma("y", ln2g[:, :],
                             wdict["ln2_g"].unsqueeze(0).broadcast_to(
                                 (128, 1024)))
                l2be = q.dma("y", ln2b[:, :],
                             wdict["ln2_b"].unsqueeze(0).broadcast_to(
                                 (128, 1024)))
                w1we = q.dma("g", w1_bf[:, :, :],
                             W["W1"].rearrange("c p d -> p c d"))
                w2we = q.dma("g", w2_bf[:, :, :],
                             w23.rearrange("c p d -> p c d"))
                b2we = q.dma("y", b2_rep[:, :],
                             wdict["b2"].unsqueeze(0).broadcast_to(
                                 (128, 1024)))
                b1we = []
                for mt in range(32):
                    b1we.append(q.dma(
                        "y", b1_sb[:, mt:mt + 1],
                        wdict["b1"][mt * 128:(mt + 1) * 128].unsqueeze(1)))

            with ExitStack() as st1:
                E1 = st1.enter_context
                hT = E1(nc.sbuf_tensor(uname("h2T"), [128, 8, 2048], BF16))
                xt = [E1(nc.sbuf_tensor(uname("xt2"), [128, 1024], F32))
                      for _ in range(2)]
                zt = E1(nc.sbuf_tensor(uname("zt3"), [128, 1024], F32))
                hbf = [E1(nc.sbuf_tensor(uname("hbf2"), [128, 1024], BF16))
                       for _ in range(2)]
                st6 = [E1(nc.sbuf_tensor(uname("st62"), [128, 12], F32))
                       for _ in range(2)]
                mv = [E1(nc.sbuf_tensor(uname("mv2"), [128, 4], F32))
                      for _ in range(2)]
                rstd = [E1(nc.sbuf_tensor(uname("rstd2"), [128, 2], F32))
                        for _ in range(2)]
                ptr = [E1(nc.psum_tensor(uname("ptr3"), [128, 1024], BF16))
                       for _ in range(2)]
                E1(nc.Block())
                eve = {}
                tre = {}
                rce = {}
                dume = None
                for i in range(33):
                    if i < 32:
                        de = q.dma("y", xt[i % 2][:, :], x1iv[i],
                                   deps=[eve.get(i - 2)])
                        rce[i] = ln_stats_comb(q, xt[i % 2][:, :], st6[i % 2],
                                               mv[i % 2], rstd[i % 2], eps, de)
                    if i > 0:
                        j = i - 1
                        half, ih = j // 16, j % 16
                        q.op("v", "tensor_scalar", zt[:, :], xt[j % 2][:, :],
                             mv[j % 2][:, 0:1], rstd[j % 2][:, 1:2],
                             OP.subtract, OP.mult, deps=[rce[j][1]])
                        q.op("v", "tensor_mul", zt[:, :], zt[:, :],
                             ln2g[:, :], deps=[l2ge] if j == 0 else ())
                        he = q.op("v", "tensor_add", hbf[j % 2][:, :],
                                  zt[:, :], ln2b[:, :],
                                  deps=[l2be] if j == 0 else ())
                        for c in range(8):
                            tre[(j, c)] = q.tr(
                                ptr[j % 2][:, c * 128:(c + 1) * 128],
                                hbf[j % 2][:, c * 128:(c + 1) * 128],
                                ident_bf[:, :],
                                deps=([he, eve.get(j - 2)] if c == 0 else ()))
                        eve[j] = q.op(
                            "v", "tensor_copy",
                            hT[:, :, ih * 128:(ih + 1) * 128],
                            ptr[j % 2][:, 0:1024].rearrange(
                                "p (c n) -> p c n", n=128),
                            deps=[tre[(j, 7)]] +
                            ([dume] if ih == 0 and dume else []))
                        if ih == 15:
                            dume = q.dma(
                                "y",
                                hT_hbm[:, :, half * 2048:(half + 1) * 2048],
                                hT[:, :, :], deps=[eve[j], dume])
                q.dma_fence()
            stLN.close()

            with ExitStack() as st1:
                E1 = st1.enter_context
                hT = E1(nc.sbuf_tensor(uname("h2T2"), [128, 8, 1024], BF16))
                h3 = E1(nc.sbuf_tensor(uname("h3"), [128, 32, 512], BF16))
                x2_t = [E1(nc.sbuf_tensor(uname("x2_t"), [128, 1024], F32))
                        for _ in range(1)]
                x1i_t2 = [E1(nc.sbuf_tensor(uname("x1i_t2"), [128, 1024], F32))
                          for _ in range(1)]
                out_t = [E1(nc.sbuf_tensor(uname("out_t2"), [128, 1024], F32))
                         for _ in range(1)]
                ps1 = [E1(nc.psum_tensor(uname("ps1b"), [128, 512], F32))
                       for _ in range(2)]
                ps2 = [E1(nc.psum_tensor(uname("ps2b"), [128, 512], F32))
                       for _ in range(4)]
                E1(nc.Block())
                k1 = 0
                k2 = 0
                gele = {}
                w2le = {}
                combe = {}
                dmae = {}
                hle = None
                for half in range(4):
                    hle = q.dma("y", hT[:, :, :],
                                hT_hbm[:, :, half * 1024:(half + 1) * 1024],
                                deps=[w2le.get(k2 - 1), hle])
                    for sl in range(2):
                        tok0 = half * 1024 + sl * 512
                        slab_gel = None
                        for mt in range(32):
                            dep = [gele.get(k1 - 2)]
                            if mt == 0:
                                dep.append(hle)
                                dep.append(w2le.get(k2 - 4))
                                if k1 == 0:
                                    dep.append(w1we)
                            ge_ = q.mm_group([
                                (ps1[k1 % 2][:, :],
                                 w1_bf[:, c, mt * 128:(mt + 1) * 128],
                                 hT[:, c, sl * 512:(sl + 1) * 512])
                                for c in range(8)], deps=dep)
                            gele[k1] = q.op("s", "activation", h3[:, mt, :],
                                            ps1[k1 % 2][:, :], AF.Gelu,
                                            bias=b1_sb[:, mt:mt + 1],
                                            deps=[ge_] + (b1we if k1 == 0
                                                          else []))
                            slab_gel = gele[k1]
                            k1 += 1
                        for ti in range(4):
                            gi = (tok0 // 128) + ti
                            xde = q.dma("y", x2_t[0][:, :], x3v[gi],
                                        deps=[combe.get((gi - 1, 1))])
                            xie = q.dma("y", x1i_t2[0][:, :], x1iv[gi],
                                        deps=[combe.get((gi - 1, 1))])
                            for ds in range(2):
                                mms = [
                                    (ps2[k2 % 4][:, :],
                                     h3[:, fc, ti * 128:(ti + 1) * 128],
                                     w2_bf[:, fc, ds * 512:(ds + 1) * 512])
                                    for fc in range(32)]
                                dep = [combe.get((gi - 2, ds))]
                                if ti == 0 and ds == 0:
                                    dep.append(slab_gel)
                                    if k2 == 0:
                                        dep.append(w2we)
                                w2le[k2] = q.mm_group(mms, deps=dep)
                                c1 = q.op("v", "scalar_tensor_tensor",
                                          out_t[0][:, ds * 512:(ds + 1) * 512],
                                          ps2[k2 % 4][:, :], 1.0,
                                          x2_t[0][:, ds * 512:(ds + 1) * 512],
                                          OP.mult, OP.add,
                                          deps=[w2le[k2], xde, xie,
                                                dmae.get(gi - 1)] +
                                          ([b2we] if k2 == 0 else []))
                                c2 = q.op("v", "scalar_tensor_tensor",
                                          out_t[0][:, ds * 512:(ds + 1) * 512],
                                          b2_rep[:, ds * 512:(ds + 1) * 512],
                                          1.0,
                                          out_t[0][:, ds * 512:(ds + 1) * 512],
                                          OP.mult, OP.add)
                                combe[(gi, ds)] = q.op(
                                    "v", "tensor_add",
                                    out_t[0][:, ds * 512:(ds + 1) * 512],
                                    out_t[0][:, ds * 512:(ds + 1) * 512],
                                    x1i_t2[0][:, ds * 512:(ds + 1) * 512])
                                k2 += 1
                            dmae[gi] = q.dma("y", xov[gi],
                                             out_t[0][:, :],
                                             deps=[combe[(gi, 1)]])
                q.dma_fence()
    return nc


def build_l1b(nc, q, x3, router_w, w1, b1, w2, b2, out, consts,
              hT_hbm, w1bf_hbm, w2bf_hbm, crow_hbm, stage_ev=None):
    ident_bf, ident_f32, ones_bf, ones_f32, cpool = consts
    eps = cpool[:, 0:1]
    NT = 32
    E_ = 8

    x3v = x3.rearrange("(i p) d -> i p d", p=128)
    outv = out.rearrange("(i p) d -> i p d", p=128)

    with ExitStack() as st0:
        E = st0.enter_context
        rwT_bf = E(nc.sbuf_tensor(uname("rwT_bf"), [128, 8, 8], BF16))
        c_sb = E(nc.sbuf_tensor(uname("c_sb"), [128, 32, 8], F32))
        c8 = E(nc.sbuf_tensor(uname("c8"), [128, 8], F32))
        b2all_bf = E(nc.sbuf_tensor(uname("b2all_bf"), [8, 1024], BF16))
        w1T = E(nc.sbuf_tensor(uname("w1T"), [128, 8, 4096], BF16))
        b1c = E(nc.sbuf_tensor(uname("b1c"), [128, 32], F32))

        if stage_ev is None:
            stage_ev = {}
        with ExitStack() as st1:
            E1 = st1.enter_context
            rw_sb = E1(nc.sbuf_tensor(uname("rw_sb"), [8, 1024], F32))
            prt = E1(nc.psum_tensor(uname("prt"), [128, 8], F32))
            E1(nc.Block())
            if not stage_ev:
                for e in range(E_):
                    e1 = q.dma("g", w1bf_hbm[e], w1[e])
                    e2 = q.dma("g", w2bf_hbm[e], w2[e])
                    stage_ev[e] = (e1, e2)
            rde = q.dma("y", rw_sb[:, :], router_w)
            pv = None
            for c in range(8):
                t_ = q.tr(prt[:, :], rw_sb[:, c * 128:(c + 1) * 128],
                          ident_f32[0:8, 0:8], deps=[rde, pv])
                pv = q.op("s", "activation", rwT_bf[:, c, :], prt[:, :],
                          AF.Identity, deps=[t_])

        with ExitStack() as st1:
            E1 = st1.enter_context
            cT_sb = E1(nc.sbuf_tensor(uname("cT_sb"), [8, 4096], BF16))
            hT = E1(nc.sbuf_tensor(uname("hTq"), [128, 8, 2048], BF16))
            oinit = [E1(nc.sbuf_tensor(uname("oinit"), [128, 1024], F32))
                     for _ in range(2)]
            xt = [E1(nc.sbuf_tensor(uname("xt3"), [128, 1024], F32))
                  for _ in range(2)]
            zt = E1(nc.sbuf_tensor(uname("zt4"), [128, 1024], F32))
            hbf = E1(nc.sbuf_tensor(uname("hbf3"), [128, 1024], BF16))
            st6 = E1(nc.sbuf_tensor(uname("st63"), [128, 12], F32))
            mv = E1(nc.sbuf_tensor(uname("mv3"), [128, 4], F32))
            rstd = E1(nc.sbuf_tensor(uname("rstd3"), [128, 2], F32))
            lgt = E1(nc.sbuf_tensor(uname("lgt"), [128, 8], F32))
            mx8 = E1(nc.sbuf_tensor(uname("mx8"), [128, 8], F32))
            sc8 = E1(nc.sbuf_tensor(uname("sc8"), [128, 16], F32))
            junk = E1(nc.sbuf_tensor(uname("junk"), [128, 8], F32))
            ptr = [E1(nc.psum_tensor(uname("ptr4"), [128, 1024], BF16))
                   for _ in range(2)]
            plg = E1(nc.psum_tensor(uname("plg"), [128, 8], F32))
            pcr = [E1(nc.psum_tensor(uname("pcr"), [8, 128], F32))
                   for _ in range(2)]
            pini = E1(nc.psum_tensor(uname("pini"), [128, 1024], F32))
            E1(nc.Block())
            b2de = q.dma("g", b2all_bf[:, :], b2)
            wde_h0 = []
            b1de_ev = [[]]

            def _th_b1(m0):
                for mt in range(m0, m0 + 8):
                    b1de_ev[0].append(q.dma(
                        "y", b1c[:, mt:mt + 1],
                        b1[0][mt * 128:(mt + 1) * 128].unsqueeze(1)))
            pref = [lambda m0=m0: _th_b1(m0) for m0 in (0, 8, 16, 24)]
            for cc in range(8):
                pref.append(lambda cc=cc: wde_h0.append(q.dma(
                    "y", w1T[:, cc, 0:2048],
                    w1bf_hbm[0][0:2048, cc * 128:(cc + 1) * 128],
                    transpose=True, deps=[stage_ev[0][0]])))
            lne = {}
            oevi = {}
            sse = {}
            eve = {}
            tre = {}
            oie = {}
            lge = {}
            tope = {}
            cre = {}
            dume = None
            for i in range(NT):
                half, ih = i // 16, i % 16
                de = q.dma("y", xt[i % 2][:, :], x3v[i],
                           deps=[lne.get(i - 2), oevi.get(i - 2)])
                r1e, _ = ln_stats_comb(q, xt[i % 2][:, :], st6, mv, rstd,
                                       zt[:, :][0:128, 0:1] if False else eps,
                                       de) if False else ln_stats_comb(
                                           q, xt[i % 2][:, :], st6, mv, rstd,
                                           eps, de)
                cpe = q.op("v", "tensor_scalar", hbf[:, :], xt[i % 2][:, :],
                           mv[:, 0:1], rstd[:, 0:1], OP.subtract, OP.mult,
                           deps=[r1e])
                lne[i] = cpe
                for c in range(8):
                    tre[(i, c)] = q.tr(
                        ptr[i % 2][:, c * 128:(c + 1) * 128],
                        hbf[:, c * 128:(c + 1) * 128], ident_bf[:, :],
                        deps=([cpe, eve.get(i - 2)] if c == 0 else ()))
                eve[i] = q.op(
                    "v", "tensor_copy", hT[:, :, ih * 128:(ih + 1) * 128],
                    ptr[i % 2][:, 0:1024].rearrange("p (c n) -> p c n", n=128),
                    deps=[tre[(i, 7)]] + ([dume] if ih == 0 and dume else []))
                lge[i] = q.mm_group([
                    (plg[:, :], hT[:, c, ih * 128:(ih + 1) * 128],
                     rwT_bf[:, c, :]) for c in range(8)],
                    deps=[eve[i], tope.get(i - 1)])
                le = q.op("s", "activation", lgt[:, :], plg[:, :], AF.Identity,
                          deps=[lge[i], tope.get(i - 1)])
                q.op("v", "max", mx8[:, :], lgt[:, :], deps=[le])
                q.op("v", "tensor_scalar", sc8[:, 0:1], mx8[:, 0:1], -1.0,
                     None, OP.mult)
                ze = q.op("s", "activation", junk[:, :], lgt[:, :], AF.Exp,
                          bias=sc8[:, 0:1], accum_out=sc8[:, 1:2],
                          deps=[(q.ch["v"][0], q.ch["v"][1])])
                de2 = q.op("v", "tensor_sub", sc8[:, 2:3], mx8[:, 1:2],
                           mx8[:, 0:1])
                e2e = q.op("s", "activation", sc8[:, 3:4], sc8[:, 2:3],
                           AF.Exp, deps=[de2])
                q.op("v", "reciprocal", sc8[:, 4:5], sc8[:, 1:2], deps=[ze])
                q.op("v", "tensor_mul", sc8[:, 5:6], sc8[:, 3:4], sc8[:, 4:5],
                     deps=[e2e])
                q.op("v", "tensor_add", sc8[:, 6:7], sc8[:, 4:5], sc8[:, 5:6])
                q.op("v", "tensor_scalar", sc8[:, 6:7], sc8[:, 6:7], 1e-9,
                     None, OP.add)
                q.op("v", "reciprocal", sc8[:, 7:8], sc8[:, 6:7])
                q.op("v", "tensor_mul", sc8[:, 8:9], sc8[:, 4:5], sc8[:, 7:8])
                q.op("v", "tensor_mul", sc8[:, 9:10], sc8[:, 5:6], sc8[:, 7:8])
                q.op("v", "tensor_scalar", c8[:, :], lgt[:, :], mx8[:, 0:1],
                     None, OP.is_equal)
                q.op("v", "tensor_scalar", c8[:, :], c8[:, :], sc8[:, 8:9],
                     None, OP.mult)
                q.op("v", "tensor_scalar", junk[:, :], lgt[:, :], mx8[:, 1:2],
                     None, OP.is_equal)
                q.op("v", "tensor_scalar", junk[:, :], junk[:, :],
                     sc8[:, 9:10], None, OP.mult)
                tope[i] = q.op("v", "tensor_add", c_sb[:, i, :], c8[:, :],
                               junk[:, :])
                ct = q.tr(pcr[i % 2][:, :], c_sb[:, i, :], ident_f32[:, :],
                          deps=[tope[i], cre.get(i - 2)])
                cre[i] = q.op("s", "activation",
                              cT_sb[:, i * 128:(i + 1) * 128],
                              pcr[i % 2][0:8, :], AF.Identity, deps=[ct])
                ibe = q.mm_group([(pini[:, 0:512],
                                   cT_sb[:, i * 128:(i + 1) * 128],
                                   b2all_bf[:, 0:512])],
                                 deps=[cre[i], oevi.get(i - 2),
                                       b2de if i == 0 else None])
                ibe2 = q.mm_group([(pini[:, 512:1024],
                                    cT_sb[:, i * 128:(i + 1) * 128],
                                    b2all_bf[:, 512:1024])])
                ov = q.op("v", "tensor_add", oinit[i % 2][:, :],
                          pini[:, :].rearrange("p n -> p n"), xt[i % 2][:, :],
                          deps=[ibe2, oie.get(i - 2)])
                oevi[i] = ov
                oie[i] = q.dma("y", outv[i], oinit[i % 2][:, :], deps=[ov])
                if ih == 15:
                    dume = q.dma("y",
                                 hT_hbm[:, :, half * 2048:(half + 1) * 2048],
                                 hT[:, :, :], deps=[eve[i], dume])
                if pref:
                    pref.pop(0)()
            while pref:
                pref.pop(0)()
            q.dma_fence()

        with ExitStack() as st1:
            E1 = st1.enter_context
            w2T = E1(nc.sbuf_tensor(uname("w2T"), [128, 32, 1024], BF16))
            hT = [E1(nc.sbuf_tensor(uname("hTq2"), [128, 8, 1024], BF16))
                  for _ in range(2)]
            h2cT = E1(nc.sbuf_tensor(uname("h2cT"), [128, 32, 512], BF16))
            out_sb = [E1(nc.sbuf_tensor(uname("out_sb"), [128, 1024], F32))
                      for _ in range(2)]
            ps1 = [E1(nc.psum_tensor(uname("ps1c"), [128, 512], F32))
                   for _ in range(3)]
            ps2 = [E1(nc.psum_tensor(uname("ps2c"), [128, 512], F32))
                   for _ in range(4)]
            E1(nc.Block())
            k1 = 0
            k2 = 0
            rele = {}
            w2le = {}
            oevac = {}
            acc_ev = {}
            prev_acc = None
            hT_free = {}
            hle = None
            w1_free_h0 = None
            w1_free_h1 = None
            w2_free = None
            misc_free = None
            for e in range(E_):
                if e > 0:
                    wde_h0 = []
                    for cc in range(8):
                        wde_h0.append(q.dma(
                            "y", w1T[:, cc, 0:2048],
                            w1bf_hbm[e][0:2048, cc * 128:(cc + 1) * 128],
                            transpose=True,
                            deps=[stage_ev[e][0], w1_free_h0]
                            if cc == 0 else [stage_ev[e][0]]))
                wde_h1 = []
                for qtr in range(4):
                    gq = e * 4 + qtr
                    hle_prev = hle
                    hle = q.dma("y", hT[gq % 2][:, :, :],
                                hT_hbm[:, :, qtr * 1024:(qtr + 1) * 1024],
                                deps=[hT_free.get(gq - 2), hle_prev])
                    if qtr == 0:
                        if e > 0:
                            b1de_ev[0] = []
                            for mt in range(32):
                                b1de_ev[0].append(q.dma(
                                    "y", b1c[:, mt:mt + 1],
                                    b1[e][mt * 128:(mt + 1) * 128]
                                    .unsqueeze(1),
                                    deps=[misc_free] if mt == 0 else ()))
                        for cc in range(8):
                            wde_h1.append(q.dma(
                                "y", w1T[:, cc, 2048:4096],
                                w1bf_hbm[e][2048:4096,
                                            cc * 128:(cc + 1) * 128],
                                transpose=True,
                                deps=[stage_ev[e][0], w1_free_h1]
                                if cc == 0 else [stage_ev[e][0]]))
                        w2de = []
                        for fc in range(32):
                            w2de.append(q.dma(
                                "y", w2T[:, fc, :],
                                w2bf_hbm[e][:, fc * 128:(fc + 1) * 128],
                                transpose=True,
                                deps=[stage_ev[e][1], w2_free]
                                if fc == 0 else [stage_ev[e][1]]))
                    for sl in range(2):
                        tok0 = qtr * 1024 + sl * 512
                        sloc = sl * 512
                        for mt in range(32):
                            dep = [rele.get(k1 - 3)]
                            if mt == 0:
                                dep.append(hle)
                                dep.append(w2le.get(k2 - 4))
                                if qtr == 0 and sl == 0:
                                    dep.append(wde_h0)
                                    dep.append(b1de_ev[0])
                            if mt == 16 and qtr == 0 and sl == 0:
                                dep.append(wde_h1)
                            mms = [
                                (ps1[k1 % 3][:, :],
                                 w1T[:, c, mt * 128:(mt + 1) * 128],
                                 hT[gq % 2][:, c, sloc:sloc + 512])
                                for c in range(8)]
                            ge_ = q.mm_group(mms, deps=dep)
                            rele[k1] = q.op("s", "activation", h2cT[:, mt, :],
                                            ps1[k1 % 3][:, :], AF.Relu,
                                            bias=b1c[:, mt:mt + 1],
                                            deps=[ge_])
                            last_rel = rele[k1]
                            if qtr == 3 and sl == 1:
                                if mt == 15:
                                    w1_free_h0 = ge_
                                elif mt == 31:
                                    w1_free_h1 = ge_
                            k1 += 1
                        if sl == 1:
                            hT_free[gq] = ge_
                        for ti in range(4):
                            gi = (tok0 // 128) + ti
                            for ds in range(2):
                                mms = [
                                    (ps2[k2 % 4][:, :],
                                     h2cT[:, fc, ti * 128:(ti + 1) * 128],
                                     w2T[:, fc, ds * 512:(ds + 1) * 512])
                                    for fc in range(32)]
                                dep = [oevac.get(k2 - 4)]
                                if ti == 0 and ds == 0:
                                    dep.append(last_rel)
                                    dep.append(w2de)
                                w2le[k2] = q.mm_group(mms, deps=dep)
                                oevac[k2] = q.op(
                                    "s", "activation",
                                    out_sb[gi % 2][:, ds * 512:(ds + 1) * 512],
                                    ps2[k2 % 4][:, :], AF.Identity,
                                    scale=c_sb[:, gi, e:e + 1],
                                    deps=[w2le[k2], prev_acc])
                                k2 += 1
                            prev_acc = q.dma(
                                "g", outv[gi], out_sb[gi % 2][:, :],
                                accum_op=OP.add,
                                deps=[oevac[k2 - 1], acc_ev.get(gi)])
                            acc_ev[gi] = prev_acc
                        w2_free = w2le[k2 - 1]
                        misc_free = w2le[k2 - 1]
            q.dma_fence()
    return nc


from concourse.bass_utils import run_bass_kernel_spmd

N_CORES = 8
TOK = 4096


def _consts_alloc(nc, q):
    st = []
    ident_bf = nc.sbuf_tensor("ident_bf", [128, 128], BF16); st.append(ident_bf)
    ident_f32 = nc.sbuf_tensor("ident_f32", [128, 128], F32); st.append(ident_f32)
    ones_bf = nc.sbuf_tensor("ones_bf", [1, 128], BF16); st.append(ones_bf)
    ones_f32 = nc.sbuf_tensor("ones_f32", [128, 128], F32); st.append(ones_f32)
    cpool = nc.sbuf_tensor("cpool", [128, 4], F32); st.append(cpool)
    objs = [t.__enter__() for t in st]
    with nc.Block() as _b:
        emit_constants(q, *objs)
    return tuple(objs)


def _final_fence(nc, q):
    with nc.Block() as _b:
        q.dma_fence()


def _prog_l0a():
    nc = bass.Bass("TRN2", target_bir_lowering=False)
    X = nc.dram_tensor("x", [TOK, 1024], F32, kind="ExternalInput").ap()
    WQ = nc.dram_tensor("Wq", [1024, 1024], F32, kind="ExternalInput").ap()
    BQ = nc.dram_tensor("bq", [1024], F32, kind="ExternalInput").ap()
    WK = nc.dram_tensor("Wk", [1024, 1024], F32, kind="ExternalInput").ap()
    BK = nc.dram_tensor("bk", [1024], F32, kind="ExternalInput").ap()
    WV = nc.dram_tensor("Wv", [1024, 8192], F32, kind="ExternalInput").ap()
    BV = nc.dram_tensor("bv", [8192], F32, kind="ExternalInput").ap()
    GM = nc.dram_tensor("gamma", [1024], F32, kind="ExternalInput").ap()
    BT = nc.dram_tensor("beta", [1024], F32, kind="ExternalInput").ap()
    ATT = nc.dram_tensor("attn", [TOK, 1024], F32, kind="ExternalOutput").ap()
    q = Par(nc)
    consts = _consts_alloc(nc, q)
    build_l0a(nc, q, X, WQ, BQ, WK, BK, WV, BV, GM, BT, ATT, consts)
    _final_fence(nc, q)
    return nc


def _prog_l0b():
    nc = bass.Bass("TRN2", target_bir_lowering=False)
    X = nc.dram_tensor("x", [TOK, 1024], F32, kind="ExternalInput").ap()
    ATT = nc.dram_tensor("attn", [TOK, 1024], F32, kind="ExternalInput").ap()
    WR = nc.dram_tensor("wr", [1024], F32, kind="ExternalInput").ap()
    W1 = nc.dram_tensor("W1", [1024, 4096], F32, kind="ExternalInput").ap()
    B1 = nc.dram_tensor("b1", [4096], F32, kind="ExternalInput").ap()
    W2 = nc.dram_tensor("W2", [4096, 1024], F32, kind="ExternalInput").ap()
    B2 = nc.dram_tensor("b2", [1024], F32, kind="ExternalInput").ap()
    X2 = nc.dram_tensor("x2", [TOK, 1024], F32, kind="ExternalOutput").ap()
    X1H = nc.dram_tensor("x1h", [TOK, 1024], F32, kind="Internal").ap()
    HTH = nc.dram_tensor("hth", [128, 8, TOK], BF16, kind="Internal").ap()
    ROW = nc.dram_tensor("rowh", [TOK], F32, kind="Internal").ap()
    q = Par(nc)
    consts = _consts_alloc(nc, q)
    build_l0b(nc, q, X, ATT, WR, W1, B1, W2, B2, X2, consts, X1H, HTH, ROW)
    _final_fence(nc, q)
    return nc


_L1A_NAMES = ["ln1_g", "ln1_b", "WQ", "bQ", "WK", "bK", "WV", "bV", "WO",
              "bO", "Wg", "bg", "A", "Bw", "Cw", "ln2_g", "ln2_b", "W1",
              "b1", "W2", "b2"]
_L1A_SHAPES = {"ln1_g": [1024], "ln1_b": [1024], "WQ": [1024, 1024],
               "bQ": [1024], "WK": [1024, 1024], "bK": [1024],
               "WV": [1024, 1024], "bV": [1024], "WO": [1024, 1024],
               "bO": [1024], "Wg": [1024, 1024], "bg": [1024], "A": [64],
               "Bw": [1024, 64], "Cw": [64, 1024], "ln2_g": [1024],
               "ln2_b": [1024], "W1": [1024, 4096], "b1": [4096],
               "W2": [4096, 1024], "b2": [1024]}


def _prog_l1a():
    nc = bass.Bass("TRN2", target_bir_lowering=False)
    X2 = nc.dram_tensor("x2", [TOK, 1024], F32, kind="ExternalInput").ap()
    dram = {n: nc.dram_tensor(n, _L1A_SHAPES[n], F32,
                              kind="ExternalInput").ap() for n in _L1A_NAMES}
    X3 = nc.dram_tensor("x3", [TOK, 1024], F32, kind="ExternalOutput").ap()
    ZH = nc.dram_tensor("zh", [TOK, 1024], F32, kind="Internal").ap()
    GH = nc.dram_tensor("gh", [TOK, 1024], BF16, kind="Internal").ap()
    AOH = nc.dram_tensor("aoh", [TOK, 1024], BF16, kind="Internal").ap()
    X1IH = nc.dram_tensor("x1ih", [TOK, 1024], F32, kind="Internal").ap()
    HTH = nc.dram_tensor("hth", [128, 8, TOK], BF16, kind="Internal").ap()
    q = Par(nc)
    consts = _consts_alloc(nc, q)
    build_l1a(nc, q, X2, dram, X3, consts, ZH, GH, AOH, X1IH, HTH)
    _final_fence(nc, q)
    return nc


def _prog_l1b():
    nc = bass.Bass("TRN2", target_bir_lowering=False)
    X3 = nc.dram_tensor("x3", [TOK, 1024], F32, kind="ExternalInput").ap()
    RW = nc.dram_tensor("rw", [8, 1024], F32, kind="ExternalInput").ap()
    W1 = nc.dram_tensor("w1", [8, 4096, 1024], F32, kind="ExternalInput").ap()
    B1 = nc.dram_tensor("b1", [8, 4096], F32, kind="ExternalInput").ap()
    W2 = nc.dram_tensor("w2", [8, 1024, 4096], F32, kind="ExternalInput").ap()
    B2 = nc.dram_tensor("b2", [8, 1024], F32, kind="ExternalInput").ap()
    OUT = nc.dram_tensor("out", [TOK, 1024], F32, kind="ExternalOutput").ap()
    HTH = nc.dram_tensor("hth", [128, 8, TOK], BF16, kind="Internal").ap()
    W1BF = nc.dram_tensor("w1bf", [8, 4096, 1024], BF16, kind="Internal").ap()
    W2BF = nc.dram_tensor("w2bf", [8, 1024, 4096], BF16, kind="Internal").ap()
    CROW = nc.dram_tensor("crow", [8, TOK], BF16, kind="Internal").ap()
    q = Par(nc)
    consts = _consts_alloc(nc, q)
    build_l1b(nc, q, X3, RW, W1, B1, W2, B2, OUT, consts,
              HTH, W1BF, W2BF, CROW)
    _final_fence(nc, q)
    return nc


def _prog_fused():
    nc = bass.Bass("TRN2", target_bir_lowering=False)
    I = lambda name, shape: nc.dram_tensor(name, shape, F32,
                                           kind="ExternalInput").ap()
    X = I("x", [TOK, 1024])
    WQ0, BQ0 = I("l0Wq", [1024, 1024]), I("l0bq", [1024])
    WK0, BK0 = I("l0Wk", [1024, 1024]), I("l0bk", [1024])
    WV0, BV0 = I("l0Wv", [1024, 8192]), I("l0bv", [8192])
    GM0, BT0 = I("l0gamma", [1024]), I("l0beta", [1024])
    WRm, W1m, B1m = I("mwr", [1024]), I("mW1", [1024, 4096]), I("mb1", [4096])
    W2m, B2m = I("mW2", [4096, 1024]), I("mb2", [1024])
    dram1 = {n: I("a_" + n, _L1A_SHAPES[n]) for n in _L1A_NAMES}
    RW = I("erw", [8, 1024])
    W1e, B1e = I("ew1", [8, 4096, 1024]), I("eb1", [8, 4096])
    W2e, B2e = I("ew2", [8, 1024, 4096]), I("eb2", [8, 1024])
    OUT = nc.dram_tensor("out", [TOK, 1024], F32, kind="ExternalOutput").ap()

    S = lambda name, shape, dt=F32: nc.dram_tensor(name, shape, dt,
                                                   kind="Internal").ap()
    ATT = S("s_att", [TOK, 1024])
    X2 = S("s_x2", [TOK, 1024])
    X3 = S("s_x3", [TOK, 1024])
    X1H = S("s_x1h", [TOK, 1024])
    HTH = S("s_hth", [128, 8, TOK], BF16)
    ROW = S("s_row", [TOK])
    ZH = S("s_zh", [TOK, 1024])
    GH = S("s_gh", [TOK, 1024], BF16)
    AOH = S("s_aoh", [TOK, 1024], BF16)
    X1IH = S("s_x1ih", [TOK, 1024])
    W1BF = S("s_w1bf", [8, 4096, 1024], BF16)
    W2BF = S("s_w2bf", [8, 1024, 4096], BF16)
    CROW = S("s_crow", [8, TOK], BF16)

    q = Par(nc)
    consts = _consts_alloc(nc, q)
    stage_ev = {}
    with nc.Block() as _bstg:
        for e in range(8):
            e1 = q.dma("g", W1BF[e], W1e[e], pool="stg")
            e2 = q.dma("g", W2BF[e], W2e[e], pool="stg")
            stage_ev[e] = (e1, e2)
    build_l0a(nc, q, X, WQ0, BQ0, WK0, BK0, WV0, BV0, GM0, BT0, ATT, consts)
    build_l0b(nc, q, X, ATT, WRm, W1m, B1m, W2m, B2m, X2, consts,
              X1H, HTH, ROW)
    build_l1a(nc, q, X2, dram1, X3, consts, ZH, GH, AOH, X1IH, HTH)
    build_l1b(nc, q, X3, RW, W1e, B1e, W2e, B2e, OUT, consts,
              HTH, W1BF, W2BF, CROW, stage_ev=stage_ev)
    _final_fence(nc, q)
    return nc


FUSED = True


def _kernel_fused(inp):
    x = inp["L0_attn_x"].reshape(N_CORES, TOK, 1024)
    nc = _prog_fused()
    base = {"l0Wq": inp["L0_attn_Wq"], "l0bq": inp["L0_attn_bq"],
            "l0Wk": inp["L0_attn_Wk"], "l0bk": inp["L0_attn_bk"],
            "l0Wv": inp["L0_attn_Wv"], "l0bv": inp["L0_attn_bv"],
            "l0gamma": inp["L0_attn_gamma"], "l0beta": inp["L0_attn_beta"],
            "mwr": inp["L0_ffn_w_router"], "mW1": inp["L0_ffn_W1"],
            "mb1": inp["L0_ffn_b1"], "mW2": inp["L0_ffn_W2"],
            "mb2": inp["L0_ffn_b2"],
            "erw": inp["L1_ffn_router_w"], "ew1": inp["L1_ffn_w1"],
            "eb1": inp["L1_ffn_b1"], "ew2": inp["L1_ffn_w2"],
            "eb2": inp["L1_ffn_b2"]}
    base.update({"a_" + n: inp[f"L1_attn_{n}"] for n in _L1A_NAMES})
    maps = [dict(x=x[c], **base) for c in range(N_CORES)]
    res = run_bass_kernel_spmd(nc, maps, core_ids=list(range(N_CORES)))
    out = np.stack([r["out"] for r in res.results])
    return out.reshape(16, 2048, 1024).astype(np.float32)


def kernel(**inputs):
    inp = {k: np.ascontiguousarray(v) for k, v in inputs.items()}
    if FUSED:
        return _kernel_fused(inp)
    x = inp["L0_attn_x"].reshape(N_CORES, TOK, 1024)

    nc = _prog_l0a()
    maps = [{"x": x[c], "Wq": inp["L0_attn_Wq"], "bq": inp["L0_attn_bq"],
             "Wk": inp["L0_attn_Wk"], "bk": inp["L0_attn_bk"],
             "Wv": inp["L0_attn_Wv"], "bv": inp["L0_attn_bv"],
             "gamma": inp["L0_attn_gamma"], "beta": inp["L0_attn_beta"]}
            for c in range(N_CORES)]
    res = run_bass_kernel_spmd(nc, maps, core_ids=list(range(N_CORES)))
    attn = [r["attn"] for r in res.results]

    nc = _prog_l0b()
    maps = [{"x": x[c], "attn": attn[c], "wr": inp["L0_ffn_w_router"],
             "W1": inp["L0_ffn_W1"], "b1": inp["L0_ffn_b1"],
             "W2": inp["L0_ffn_W2"], "b2": inp["L0_ffn_b2"]}
            for c in range(N_CORES)]
    res = run_bass_kernel_spmd(nc, maps, core_ids=list(range(N_CORES)))
    x2 = [r["x2"] for r in res.results]

    nc = _prog_l1a()
    wmap = {n: inp[f"L1_attn_{n}"] for n in _L1A_NAMES}
    maps = [dict(x2=x2[c], **wmap) for c in range(N_CORES)]
    res = run_bass_kernel_spmd(nc, maps, core_ids=list(range(N_CORES)))
    x3 = [r["x3"] for r in res.results]

    nc = _prog_l1b()
    maps = [{"x3": x3[c], "rw": inp["L1_ffn_router_w"],
             "w1": inp["L1_ffn_w1"], "b1": inp["L1_ffn_b1"],
             "w2": inp["L1_ffn_w2"], "b2": inp["L1_ffn_b2"]}
            for c in range(N_CORES)]
    res = run_bass_kernel_spmd(nc, maps, core_ids=list(range(N_CORES)))
    out = np.stack([r["out"] for r in res.results])
    return out.reshape(16, 2048, 1024).astype(np.float32)
